# version 04
# baseline (speedup 1.0000x reference)
import numpy as np
import ml_dtypes
import concourse.bass as bass
import concourse.mybir as mybir
from concourse.bass_utils import run_bass_kernel_spmd

F32 = mybir.dt.float32
BF16 = mybir.dt.bfloat16
I32 = mybir.dt.int32
AF = mybir.ActivationFunctionType
ALU = mybir.AluOpType
AX = mybir.AxisListType
BF = ml_dtypes.bfloat16

NCORES = 8
B, S, D = 16, 2048, 1024
BPC = B // NCORES
TPC = BPC * S
NTB = S // 128
ND = D // 128
NHEADS, DK = 16, 64
ROPE_THETA = 10000.0
LN_EPS = 1e-5


class KB:

    def __init__(self, name):
        self.nc = bass.Bass("TRN2", target_bir_lowering=False, name=name,
                            detect_race_conditions=False)
        self.sem = {}
        self.cnt = {}
        for nm in ("sdma", "gdma", "pe", "dve", "act", "pool"):
            self.sem[nm] = self.nc.alloc_semaphore("s_" + nm)
            self.cnt[nm] = 0

    def sdma(self, out, in_):
        if self.cnt["sdma"]:
            self.nc.sync.wait_ge(self.sem["sdma"], self.cnt["sdma"])
        self.nc.sync.dma_start(out=out, in_=in_).then_inc(self.sem["sdma"], 16)
        self.cnt["sdma"] += 16

    def gdma(self, out, in_):
        if self.cnt["gdma"]:
            self.nc.gpsimd.wait_ge(self.sem["gdma"], self.cnt["gdma"])
        self.nc.gpsimd.dma_start(out=out, in_=in_).then_inc(self.sem["gdma"], 16)
        self.cnt["gdma"] += 16

    def tick(self, nm, instr, k=1):
        instr.then_inc(self.sem[nm], k)
        self.cnt[nm] += k

    def vop(self, f, *args, **kw):
        self.wait(self.nc.vector, "dve")
        ins = f(*args, **kw)
        self.tick("dve", ins)
        return ins

    def sop(self, f, *args, **kw):
        self.wait(self.nc.scalar, "act")
        ins = f(*args, **kw)
        self.tick("act", ins)
        return ins

    def pop(self, f, *args, **kw):
        self.wait(self.nc.gpsimd, "pool")
        ins = f(*args, **kw)
        self.tick("pool", ins)
        return ins

    def mark(self, nm):
        return (nm, self.cnt[nm])

    def wait_mark(self, eng, mk):
        nm, v = mk
        if v > 0:
            eng.wait_ge(self.sem[nm], v)

    def wait(self, eng, *names):
        for nm in names:
            if self.cnt[nm] > 0:
                eng.wait_ge(self.sem[nm], self.cnt[nm])

    def barrier(self):
        nc = self.nc
        for eng in (nc.sync, nc.gpsimd, nc.tensor, nc.vector, nc.scalar):
            self.wait(eng, "sdma", "gdma", "pe", "dve", "act", "pool")

    def finish(self):
        self.wait(self.nc.sync, "sdma", "gdma")


def _rope_tables():
    pos = np.arange(S, dtype=np.float64)
    expo = np.arange(0, DK, 2, dtype=np.float64) / DK
    inv_freq = ROPE_THETA ** expo
    ang = pos[None, :] / inv_freq[:, None]
    cos = np.repeat(np.cos(ang), 2, axis=0)
    sin = np.repeat(np.sin(ang), 2, axis=0)
    sgn = np.where(np.arange(DK) % 2 == 0, -1.0, 1.0)[:, None]
    cc = np.concatenate([cos, cos], axis=0).astype(BF)
    ss = np.concatenate([sin * sgn, sin * sgn], axis=0).astype(BF)
    r = np.zeros((DK, DK), dtype=BF)
    for j in range(0, DK, 2):
        r[j, j + 1] = 1.0
        r[j + 1, j] = 1.0
    r2 = np.zeros((128, 128), dtype=BF)
    r2[0:64, 0:64] = r
    r2[64:128, 64:128] = r
    return cc, ss, r2


class LNStats:

    def __init__(self, nc, tag):
        sb = nc.alloc_sbuf_tensor
        self.m = sb(f"ln_m_{tag}", [128, 1], F32)
        self.q = sb(f"ln_q_{tag}", [128, 1], F32)
        self.m2 = sb(f"ln_m2_{tag}", [128, 1], F32)
        self.v = sb(f"ln_v_{tag}", [128, 1], F32)
        self.r = sb(f"ln_r_{tag}", [128, 1], F32)
        self.b = sb(f"ln_b_{tag}", [128, 1], F32)


def ln_tile(kb, xt, out_bf, sq, st, epscol):
    nc = kb.nc
    v, s = nc.vector, nc.scalar
    kb.vop(v.reduce_sum, st.m[:, :], xt, AX.X)
    kb.vop(v.tensor_scalar_mul, st.m[:, :], st.m[:, :], 1.0 / D)
    kb.vop(v.tensor_tensor, st.m2[:, :], st.m[:, :], st.m[:, :], ALU.mult)
    kb.sop(s.activation, sq, xt, AF.Square, accum_out=st.q[:, :])
    kb.wait(v, "act")
    kb.vop(v.tensor_scalar, st.v[:, :], st.q[:, :], 1.0 / D, st.m2[:, 0:1],
           ALU.mult, ALU.subtract)
    kb.wait(s, "dve")
    kb.sop(s.activation, st.v[:, :], st.v[:, :], AF.Ln, bias=epscol[:, 0:1])
    kb.sop(s.activation, st.r[:, :], st.v[:, :], AF.Exp, scale=-0.5)
    kb.wait(v, "act")
    kb.vop(v.tensor_scalar, st.b[:, :], st.m[:, :], st.r[:, 0:1], -1.0,
           ALU.mult, ALU.mult)
    kb.wait(s, "dve")
    kb.sop(s.activation, out_bf, xt, AF.Identity, bias=st.b[:, 0:1], scale=st.r[:, 0:1])


def build_l0_attn(debug_taps=False):
    kb = KB("l0attn")
    nc = kb.nc
    x = nc.dram_tensor("x", [TPC, D], F32, kind="ExternalInput").ap()
    wqkvT = nc.dram_tensor("wqkvT", [D, 3 * D], F32, kind="ExternalInput").ap()
    woT = nc.dram_tensor("woT", [D, D], F32, kind="ExternalInput").ap()
    x1 = nc.dram_tensor("x1", [TPC, D], F32, kind="ExternalOutput").ap()

    cc_np, ss_np, r2_np = _rope_tables()
    consts = {
        "cc": cc_np, "ss": ss_np, "r2": r2_np,
        "ident": np.eye(128, dtype=BF),
        "upmask": (np.arange(128)[:, None] <= np.arange(128)[None, :]).astype(BF),
        "ones": np.ones((128, 128), dtype=BF),
    }
    cap = {k: nc.inline_tensor(v, "c_" + k).ap() for k, v in consts.items()}

    sb = nc.alloc_sbuf_tensor
    wq = [sb(f"wq{i}", [128, 3 * D], BF16) for i in range(ND)]
    wo = [sb(f"wo{i}", [128, D], BF16) for i in range(ND)]
    ccs = sb("ccs", [128, S], BF16)
    sss = sb("sss", [128, S], BF16)
    r2s = sb("r2s", [128, 128], BF16)
    ids = sb("ids", [128, 128], BF16)
    upm = sb("upm", [128, 128], BF16)
    ones = sb("ones1", [128, 128], BF16)
    xt = [sb(f"xt{i}", [128, D], F32) for i in range(2)]
    sq = sb("sq", [128, D], F32)
    xlnb = sb("xlnb", [128, D], BF16)
    xlnT = [sb(f"xlnT{i}", [128, S], BF16) for i in range(ND)]
    vaug = [sb(f"vaug{i}", [128, NHEADS * 65], BF16) for i in range(NTB)]
    qk_raw = sb("qk_raw", [128, S], BF16)
    q2 = sb("q2", [128, S], BF16)
    k2 = sb("k2", [128, S], BF16)
    rt = sb("rt", [128, 512], BF16)
    rt2 = sb("rt2", [128, 512], BF16)
    pt = sb("pt", [128, 512], BF16)
    rb = sb("rb", [128, 512], BF16)
    aoutT = [sb(f"aoutT{i}", [128, S], BF16) for i in range(ND)]
    wsum = sb("wsum", [128, D], F32)
    epscol = sb("epscol", [128, 1], F32)
    st = LNStats(nc, "a")

    ps = nc.alloc_psum_tensor
    p_tr = ps("p_tr", [128, 512], BF16)
    p_qk = ps("p_qk", [128, 512], F32)
    p_rot = ps("p_rot", [128, 512], F32)
    p_s = ps("p_s", [128, 512], F32)
    p_o = ps("p_o", [128, 512], F32)
    p_b = ps("p_b", [128, 512], F32)
    p_w0 = ps("p_w0", [128, 512], F32)
    p_w1 = ps("p_w1", [128, 512], F32)

    kb.sdma(ccs[:, :], cap["cc"][:, :])
    kb.sdma(sss[:, :], cap["ss"][:, :])
    kb.sdma(r2s[:, :], cap["r2"][:, :])
    kb.sdma(ids[:, :], cap["ident"][:, :])
    kb.sdma(upm[:, :], cap["upmask"][:, :])
    kb.sdma(ones[:, :], cap["ones"][:, :])
    for i in range(ND):
        kb.gdma(wq[i][:, :], wqkvT[128 * i:128 * (i + 1), :])
        kb.gdma(wo[i][:, :], woT[128 * i:128 * (i + 1), :])
    kb.vop(nc.vector.memset, epscol[:, :], LN_EPS)
    kb.barrier()

    t, v, s = nc.tensor, nc.vector, nc.scalar
    for b in range(BPC):
        tok0 = b * S
        ln_done = [kb.mark("act"), kb.mark("act")]
        tr_evac = kb.mark("dve")
        v_evac = kb.mark("act")
        for tt in range(NTB):
            xb_ = xt[tt % 2]
            kb.wait_mark(nc.sync, ln_done[tt % 2])
            kb.sdma(xb_[:, :], x[tok0 + 128 * tt: tok0 + 128 * (tt + 1), :])
            kb.wait(v, "sdma")
            kb.wait(s, "sdma")
            kb.wait(s, "pe")
            ln_tile(kb, xb_[:, :], xlnb[:, :], sq[:, :], st, epscol)
            ln_done[tt % 2] = kb.mark("act")
            kb.wait(t, "act")
            for g in range(2):
                kb.wait_mark(t, tr_evac)
                for k_ in range(4):
                    dt = 4 * g + k_
                    ins = t.transpose(p_tr[:, 128 * k_:128 * (k_ + 1)],
                                      xlnb[:, 128 * dt:128 * (dt + 1)], ids[:, :])
                    if k_ == 3:
                        kb.tick("pe", ins)
                kb.wait(v, "pe")
                for k_ in range(4):
                    dt = 4 * g + k_
                    kb.vop(v.tensor_copy, xlnT[dt][:, 128 * tt:128 * (tt + 1)],
                           p_tr[:, 128 * k_:128 * (k_ + 1)])
                tr_evac = kb.mark("dve")
            kb.wait(t, "dve")
            for c in range(2):
                kb.wait_mark(t, v_evac)
                for dt in range(ND):
                    ins = t.matmul(p_qk[:, :], xlnT[dt][:, 128 * tt:128 * (tt + 1)],
                                   wq[dt][:, 2 * D + 512 * c: 2 * D + 512 * (c + 1)],
                                   start=(dt == 0), stop=(dt == ND - 1))
                kb.tick("pe", ins)
                kb.wait(s, "pe")
                dst = vaug[tt][:, :].rearrange("p (h w) -> p h w", w=65)[:, 8 * c:8 * (c + 1), 0:64]
                kb.sop(s.activation, dst, p_qk[:, :].rearrange("p (h w) -> p h w", w=64),
                       AF.Copy)
                v_evac = kb.mark("act")
            kb.wait(v, "act")
            kb.vop(v.memset, vaug[tt][:, :].rearrange("p (h w) -> p h w", w=65)[:, :, 64:65], 1.0)

        kb.barrier()

        po_used = kb.mark("dve")
        pb_used = kb.mark("act")
        qk_evac = kb.mark("act")
        rot_used = kb.mark("dve")
        for hp in range(NHEADS // 2):
            for dst, col0 in ((q2, 128 * hp), (k2, D + 128 * hp)):
                for jq in range(4):
                    kb.wait_mark(t, qk_evac)
                    for dt in range(ND):
                        ins = t.matmul(p_qk[:, :], wq[dt][:, col0:col0 + 128],
                                       xlnT[dt][:, 512 * jq:512 * (jq + 1)],
                                       start=(dt == 0), stop=(dt == ND - 1))
                    kb.tick("pe", ins)
                    kb.wait(s, "pe")
                    kb.wait(s, "dve")
                    kb.sop(s.activation, qk_raw[:, 512 * jq:512 * (jq + 1)],
                           p_qk[:, :], AF.Copy)
                    qk_evac = kb.mark("act")
                    kb.wait(t, "act")
                    kb.wait_mark(t, rot_used)
                    kb.tick("pe", t.matmul(p_rot[:, :], r2s[:, :],
                                           qk_raw[:, 512 * jq:512 * (jq + 1)],
                                           start=True, stop=True))
                    kb.wait(v, "pe")
                    kb.wait(v, "act")
                    sl = slice(512 * jq, 512 * (jq + 1))
                    kb.vop(v.tensor_tensor, rt[:, :], p_rot[:, :], sss[:, sl], ALU.mult)
                    kb.vop(v.tensor_tensor, rt2[:, :], qk_raw[:, sl], ccs[:, sl], ALU.mult)
                    kb.vop(v.tensor_add, dst[:, sl], rt[:, :], rt2[:, :])
                    rot_used = kb.mark("dve")

            for h01 in range(2):
                h = 2 * hp + h01
                base = 64 * h01
                for jq in range(4):
                    nk = 4 * jq + 4
                    for ki in range(nk):
                        kb.wait(t, "dve")
                        kb.wait(t, "act")
                        kb.tick("pe", t.matmul(p_s[:, :],
                                               k2[base:base + 64, 128 * ki:128 * (ki + 1)],
                                               q2[base:base + 64, 512 * jq:512 * (jq + 1)],
                                               start=True, stop=True))
                        kb.wait(s, "pe")
                        kb.sop(s.activation, pt[:, :], p_s[:, :], AF.Exp, scale=0.125)
                        ld = ki - 4 * jq
                        if ld >= 0:
                            kb.wait(v, "act")
                            if ld > 0:
                                kb.vop(v.memset, pt[:, 0:128 * ld], 0.0)
                            kb.vop(v.tensor_tensor, pt[:, 128 * ld:128 * (ld + 1)],
                                   pt[:, 128 * ld:128 * (ld + 1)],
                                   upm[:, :], ALU.mult)
                            kb.wait(t, "dve")
                        else:
                            kb.wait(t, "act")
                        if ki == 0:
                            kb.wait_mark(t, po_used)
                        ins = t.matmul(p_o[0:65, :], vaug[ki][:, 65 * h:65 * h + 65],
                                       pt[:, :], start=(ki == 0), stop=(ki == nk - 1))
                        if ki == nk - 1:
                            kb.tick("pe", ins)
                    kb.wait(v, "pe")
                    with nc.allow_low_precision(reason="bf16 softmax recip within tolerance"):
                        kb.vop(v.reciprocal, rb[64:65, 0:512], p_o[64:65, :])
                    kb.wait(t, "dve")
                    kb.wait_mark(t, pb_used)
                    kb.tick("pe", t.matmul(p_b[0:64, :], ones[64:65, 0:64],
                                           rb[64:65, 0:512], start=True, stop=True))
                    kb.wait(s, "pe")
                    kb.wait_mark(s, po_used)
                    kb.sop(s.activation, rb[0:64, 0:512], p_b[0:64, :], AF.Copy)
                    pb_used = kb.mark("act")
                    kb.wait(v, "act")
                    kb.vop(v.tensor_tensor,
                           aoutT[h // 2][64 * (h % 2):64 * (h % 2) + 64, 512 * jq:512 * (jq + 1)],
                           p_o[0:64, :], rb[0:64, 0:512], ALU.mult)
                    po_used = kb.mark("dve")

        kb.barrier()

        res_done = [kb.mark("dve"), kb.mark("dve")]
        for tt in range(NTB):
            xb_ = xt[tt % 2]
            kb.wait_mark(nc.sync, res_done[tt % 2])
            kb.sdma(xb_[:, :], x[tok0 + 128 * tt: tok0 + 128 * (tt + 1), :])
            kb.wait(t, "dve")
            for c, pw in ((0, p_w0), (1, p_w1)):
                for ft in range(ND):
                    ins = t.matmul(pw[:, :], aoutT[ft][:, 128 * tt:128 * (tt + 1)],
                                   wo[ft][:, 512 * c:512 * (c + 1)],
                                   start=(ft == 0), stop=(ft == ND - 1))
                kb.tick("pe", ins)
            kb.wait(v, "pe")
            kb.wait(v, "sdma")
            kb.vop(v.tensor_add, wsum[:, 0:512], xb_[:, 0:512], p_w0[:, :])
            kb.vop(v.tensor_add, wsum[:, 512:1024], xb_[:, 512:1024], p_w1[:, :])
            res_done[tt % 2] = kb.mark("dve")
            kb.wait(nc.sync, "dve")
            kb.sdma(x1[tok0 + 128 * tt: tok0 + 128 * (tt + 1), :], wsum[:, :])
        kb.barrier()

    if debug_taps:
        taps = {
            "dbg_xlnT0": xlnT[0], "dbg_q2": q2, "dbg_k2": k2,
            "dbg_vaug0": vaug[0], "dbg_aoutT0": aoutT[0],
        }
        for nm, tile in taps.items():
            shp = list(tile.shape) if hasattr(tile, "shape") else [128, S]
            o = nc.dram_tensor(nm, [128, shp[1]], BF16, kind="ExternalOutput").ap()
            kb.sdma(o[:, :], tile[:, :])
        kb.barrier()

    kb.finish()
    return kb.nc



def emit_ln_transpose(kb, src_tiles, xlnb, sq, st, epscol, xlnT, ids, p_tr, tr_state):
    nc = kb.nc
    t, v, s = nc.tensor, nc.vector, nc.scalar
    for tt, xb_ in enumerate(src_tiles):
        kb.wait(s, "pe")
        ln_tile(kb, xb_[:, :], xlnb[:, :], sq[:, :], st, epscol)
        kb.wait(t, "act")
        for g in range(2):
            kb.wait_mark(t, tr_state["tr_evac"])
            for k_ in range(4):
                dt = 4 * g + k_
                ins = t.transpose(p_tr[:, 128 * k_:128 * (k_ + 1)],
                                  xlnb[:, 128 * dt:128 * (dt + 1)], ids[:, :])
                if k_ == 3:
                    kb.tick("pe", ins)
            kb.wait(v, "pe")
            for k_ in range(4):
                dt = 4 * g + k_
                kb.vop(v.tensor_copy, xlnT[dt][:, 128 * tt:128 * (tt + 1)],
                       p_tr[:, 128 * k_:128 * (k_ + 1)])
            tr_state["tr_evac"] = kb.mark("dve")


def build_l0_ffn(debug_taps=False):
    kb = KB("l0ffn")
    nc = kb.nc
    E, FF = 8, 4096
    x = nc.dram_tensor("x", [TPC, D], F32, kind="ExternalInput").ap()
    wr = nc.dram_tensor("wr", [D, E], F32, kind="ExternalInput").ap()
    br = nc.dram_tensor("br", [1, E], F32, kind="ExternalInput").ap()
    w1 = nc.dram_tensor("w1", [E, D, FF], F32, kind="ExternalInput").ap()
    b1r = nc.dram_tensor("b1r", [E, 128, FF // 128], F32, kind="ExternalInput").ap()
    w2 = nc.dram_tensor("w2", [E, FF, D], F32, kind="ExternalInput").ap()
    b2 = nc.dram_tensor("b2", [E, D], F32, kind="ExternalInput").ap()
    x2 = nc.dram_tensor("x2", [TPC, D], F32, kind="ExternalOutput").ap()

    tie_np = (np.arange(8)[None, :] <= np.arange(8)[:, None]).astype(np.float32)
    consts = {
        "ident": np.eye(128, dtype=BF),
        "onesrow": np.ones((1, 128), dtype=BF),
        "tie": np.broadcast_to(tie_np.reshape(1, 64), (128, 64)).astype(np.float32),
    }
    cap = {k: nc.inline_tensor(np.ascontiguousarray(v), "c_" + k).ap() for k, v in consts.items()}

    sb = nc.alloc_sbuf_tensor
    ids = sb("ids", [128, 128], BF16)
    onesr = sb("onesr", [1, 128], BF16)
    ties = sb("ties", [128, 64], F32)
    wrs = [sb(f"wrs{i}", [128, E], BF16) for i in range(ND)]
    brs = sb("brs", [1, E], BF16)
    b2s = sb("b2s", [E, D], BF16)
    b1s = sb("b1s", [128, FF // 128], F32)
    acc = [sb(f"acc{i}", [128, D], F32) for i in range(NTB)]
    sq = sb("sq", [128, D], F32)
    xlnb = sb("xlnb", [128, D], BF16)
    xlnT = [sb(f"xlnT{i}", [128, S], BF16) for i in range(ND)]
    hT = [sb(f"hT{i}", [128, S], BF16) for i in range(8)]
    w1q = [sb(f"w1q{i}", [128, 1024], BF16) for i in range(ND)]
    w2q = [sb(f"w2q{i}", [128, D], BF16) for i in range(8)]
    lg = sb("lg", [128, E], F32)
    ga = [sb(f"ga{i}", [128, E], F32) for i in range(NTB)]
    gab = sb("gab", [128, E], BF16)
    gT = sb("gT", [E, S], BF16)
    mg = sb("mg", [128, 64], F32)
    eqg = sb("eqg", [128, 64], F32)
    cum = sb("cum", [128, 64], F32)
    kp = sb("kp", [128, E], F32)
    kp2 = sb("kp2", [128, E], F32)
    gmax = sb("gmax", [128, 1], F32)
    gsum = sb("gsum", [128, 1], F32)
    epscol = sb("epscol", [128, 1], F32)
    st = LNStats(nc, "f")

    ps = nc.alloc_psum_tensor
    p_tr = ps("p_tr", [128, 512], BF16)
    p_g = ps("p_g", [128, E], F32)
    p_h0 = ps("p_h0", [128, 512], F32)
    p_h1 = ps("p_h1", [128, 512], F32)
    p_y0 = ps("p_y0", [128, 512], F32)
    p_y1 = ps("p_y1", [128, 512], F32)

    t, v, s = nc.tensor, nc.vector, nc.scalar
    kb.sdma(ids[:, :], cap["ident"][:, :])
    kb.sdma(onesr[:, :], cap["onesrow"][:, :])
    kb.sdma(ties[:, :], cap["tie"][:, :])
    kb.gdma(brs[:, :], br[:, :])
    kb.gdma(b2s[:, :], b2[:, :])
    for i in range(ND):
        kb.gdma(wrs[i][:, :], wr[128 * i:128 * (i + 1), :])
    kb.vop(v.memset, epscol[:, :], LN_EPS)
    kb.barrier()

    for b in range(BPC):
        tok0 = b * S
        for tt in range(NTB):
            kb.sdma(acc[tt][:, :], x[tok0 + 128 * tt: tok0 + 128 * (tt + 1), :])
        kb.barrier()
        tr_state = {"tr_evac": kb.mark("dve")}
        emit_ln_transpose(kb, [a for a in acc], xlnb, sq, st, epscol, xlnT, ids, p_tr, tr_state)
        kb.barrier()

        gt_evac = kb.mark("dve")
        for tt in range(NTB):
            kb.wait(t, "dve")
            for dt in range(ND):
                ins = t.matmul(p_g[:, :], xlnT[dt][:, 128 * tt:128 * (tt + 1)],
                               wrs[dt][:, :], start=(dt == 0), stop=False)
            kb.tick("pe", t.matmul(p_g[:, :], onesr[0:1, :], brs[0:1, :],
                                   start=False, stop=True))
            kb.wait(v, "pe")
            kb.vop(v.tensor_copy, lg[:, :], p_g[:, :])
            kb.vop(v.reduce_max, gmax[:, :], lg[:, :], AX.X, negate=True)
            kb.wait(s, "dve")
            kb.sop(s.activation, lg[:, :], lg[:, :], AF.Exp, bias=gmax[:, 0:1])
            kb.wait(v, "act")
            kb.vop(v.reduce_sum, gsum[:, :], lg[:, :], AX.X)
            kb.vop(v.reciprocal, gsum[:, :], gsum[:, :])
            kb.vop(v.tensor_scalar_mul, lg[:, :], lg[:, :], gsum[:, 0:1])
            A = lg[:, :].unsqueeze(1).broadcast_to((128, 8, 8))
            Bm = lg[:, :].unsqueeze(2).broadcast_to((128, 8, 8))
            mg3 = mg[:, :].rearrange("p (e f) -> p e f", f=8)
            eq3 = eqg[:, :].rearrange("p (e f) -> p e f", f=8)
            kb.vop(v.tensor_tensor, mg3, A, Bm, ALU.is_gt)
            kb.vop(v.tensor_tensor, eq3, A, Bm, ALU.is_equal)
            kb.vop(v.tensor_tensor, eqg[:, :], eqg[:, :], ties[:, :], ALU.mult)
            kb.vop(v.tensor_add, mg[:, :], mg[:, :], eqg[:, :])
            kb.vop(v.tensor_tensor, mg3, mg3, A, ALU.mult)
            kb.vop(v.tensor_reduce, cum[:, 0:8], mg3, AX.X, ALU.add)
            kb.vop(v.tensor_single_scalar, kp[:, :], cum[:, 0:8], 0.8, ALU.is_lt)
            kb.vop(v.tensor_tensor, kp2[:, :], cum[:, 0:8], lg[:, :], ALU.is_equal)
            kb.vop(v.tensor_max, kp[:, :], kp[:, :], kp2[:, :])
            kb.vop(v.tensor_tensor, ga[tt][:, :], lg[:, :], kp[:, :], ALU.mult)
            with nc.allow_low_precision(reason="bf16 gate copy for transpose"):
                kb.vop(v.tensor_copy, gab[:, :], ga[tt][:, :])
            kb.wait(t, "dve")
            kb.wait_mark(t, gt_evac)
            kb.tick("pe", t.transpose(p_tr[0:8, 0:128], gab[:, :], ids[:, :]))
            kb.wait(v, "pe")
            kb.vop(v.tensor_copy, gT[0:8, 128 * tt:128 * (tt + 1)], p_tr[0:8, 0:128])
            gt_evac = kb.mark("dve")
        kb.barrier()

        for tt in range(NTB):
            kb.wait(t, "dve")
            for c, py in ((0, p_y0), (1, p_y1)):
                kb.tick("pe", t.matmul(py[:, :], gT[0:8, 128 * tt:128 * (tt + 1)],
                                       b2s[:, 512 * c:512 * (c + 1)], start=True, stop=True))
            kb.wait(v, "pe")
            kb.vop(v.tensor_add, acc[tt][:, 0:512], acc[tt][:, 0:512], p_y0[:, :])
            kb.vop(v.tensor_add, acc[tt][:, 512:1024], acc[tt][:, 512:1024], p_y1[:, :])
        kb.barrier()

        for e in range(E):
            kb.sdma(b1s[:, :], b1r[e, :, :])
            for qq in range(4):
                f0 = 1024 * qq
                for i in range(ND):
                    kb.gdma(w1q[i][:, :], w1[e, 128 * i:128 * (i + 1), f0:f0 + 1024])
                for i in range(8):
                    kb.gdma(w2q[i][:, :], w2[e, f0 + 128 * i:f0 + 128 * (i + 1), :])
                kb.barrier()
                hb = [p_h0, p_h1]
                h_evac = [kb.mark("act"), kb.mark("act")]
                for ft in range(8):
                    for tc in range(4):
                        ph = hb[(ft * 4 + tc) % 2]
                        kb.wait_mark(t, h_evac[(ft * 4 + tc) % 2])
                        for dt in range(ND):
                            ins = t.matmul(ph[:, :], w1q[dt][:, 128 * ft:128 * (ft + 1)],
                                           xlnT[dt][:, 512 * tc:512 * (tc + 1)],
                                           start=(dt == 0), stop=(dt == ND - 1))
                        kb.tick("pe", ins)
                        kb.wait(s, "pe")
                        fcol = 8 * qq + ft
                        kb.sop(s.activation, hT[ft][:, 512 * tc:512 * (tc + 1)],
                               ph[:, :], AF.Relu, bias=b1s[:, fcol:fcol + 1])
                        h_evac[(ft * 4 + tc) % 2] = kb.mark("act")
                kb.barrier()
                y_done = kb.mark("dve")
                for tt in range(NTB):
                    kb.wait_mark(t, y_done)
                    for c, py in ((0, p_y0), (1, p_y1)):
                        for fl in range(8):
                            ins = t.matmul(py[:, :], hT[fl][:, 128 * tt:128 * (tt + 1)],
                                           w2q[fl][:, 512 * c:512 * (c + 1)],
                                           start=(fl == 0), stop=(fl == 7))
                        kb.tick("pe", ins)
                    kb.wait(v, "pe")
                    kb.vop(v.scalar_tensor_tensor, acc[tt][:, 0:512], p_y0[:, :],
                           ga[tt][:, e:e + 1], acc[tt][:, 0:512], ALU.mult, ALU.add)
                    kb.vop(v.scalar_tensor_tensor, acc[tt][:, 512:1024], p_y1[:, :],
                           ga[tt][:, e:e + 1], acc[tt][:, 512:1024], ALU.mult, ALU.add)
                    y_done = kb.mark("dve")
                kb.barrier()
        for tt in range(NTB):
            kb.wait(nc.sync, "dve")
            kb.sdma(x2[tok0 + 128 * tt: tok0 + 128 * (tt + 1), :], acc[tt][:, :])
        kb.barrier()

    if debug_taps:
        taps = {"dbg_lg": lg, "dbg_cum": cum, "dbg_kp": kp, "dbg_ga15": ga[15],
                "dbg_gT": gT, "dbg_hT0": hT[0]}
        for nm, tile in taps.items():
            shp = tile.shape
            o = nc.dram_tensor(nm, list(shp), F32 if tile.dtype == F32 else BF16,
                               kind="ExternalOutput").ap()
            kb.sdma(o[:, :], tile[:, :])
        kb.barrier()
    kb.finish()
    return kb.nc


def emit_ln_transpose_stream(kb, x_ap, tok0, xt2, xlnb, sq, st, epscol, xlnT, ids,
                             p_tr, tr_state):
    nc = kb.nc
    t, v, s = nc.tensor, nc.vector, nc.scalar
    ln_done = [kb.mark("act"), kb.mark("act")]
    for tt in range(NTB):
        xb_ = xt2[tt % 2]
        kb.wait_mark(nc.sync, ln_done[tt % 2])
        kb.sdma(xb_[:, :], x_ap[tok0 + 128 * tt: tok0 + 128 * (tt + 1), :])
        kb.wait(v, "sdma")
        kb.wait(s, "sdma")
        kb.wait(s, "pe")
        ln_tile(kb, xb_[:, :], xlnb[:, :], sq[:, :], st, epscol)
        ln_done[tt % 2] = kb.mark("act")
        kb.wait(t, "act")
        for g in range(2):
            kb.wait_mark(t, tr_state["tr_evac"])
            for k_ in range(4):
                dt = 4 * g + k_
                ins = t.transpose(p_tr[:, 128 * k_:128 * (k_ + 1)],
                                  xlnb[:, 128 * dt:128 * (dt + 1)], ids[:, :])
                if k_ == 3:
                    kb.tick("pe", ins)
            kb.wait(v, "pe")
            for k_ in range(4):
                dt = 4 * g + k_
                kb.vop(v.tensor_copy, xlnT[dt][:, 128 * tt:128 * (tt + 1)],
                       p_tr[:, 128 * k_:128 * (k_ + 1)])
            tr_state["tr_evac"] = kb.mark("dve")


def build_l1_attn(debug_taps=False, stop=None):
    kb = KB("mamba")
    nc = kb.nc
    DI = 2 * D
    NH, HP, NST = 32, 64, 128
    CONVD = DI + 2 * NST
    CH = 64
    NCH = S // CH
    PAD = 3
    x = nc.dram_tensor("x", [TPC, D], F32, kind="ExternalInput").ap()
    winT = nc.dram_tensor("winT", [D, 4384], F32, kind="ExternalInput").ap()
    convw = nc.dram_tensor("convw", [CONVD, 4], F32, kind="ExternalInput").ap()
    convb = nc.dram_tensor("convb", [CONVD, 1], F32, kind="ExternalInput").ap()
    dtb = nc.dram_tensor("dtb", [NH, 1], F32, kind="ExternalInput").ap()
    alog = nc.dram_tensor("alog", [NH, 1], F32, kind="ExternalInput").ap()
    dpar = nc.dram_tensor("dpar", [NH, 1], F32, kind="ExternalInput").ap()
    nwr = nc.dram_tensor("nwr", [128, DI // 128], F32, kind="ExternalInput").ap()
    woutT = nc.dram_tensor("woutT", [DI, D], F32, kind="ExternalInput").ap()
    x3 = nc.dram_tensor("x3", [TPC, D], F32, kind="ExternalOutput").ap()

    seg = np.ones((NH, S), dtype=BF); seg[:, ::CH] = 0.0
    consts = {
        "ident": np.eye(128, dtype=BF),
        "identf": np.eye(128, dtype=np.float32),
        "onesrow": np.ones((1, 128), dtype=BF),
        "onesrowf": np.ones((1, 128), dtype=np.float32),
        "segmask": seg,
        "bigmask": (1e9 * (np.arange(64)[:, None] > np.arange(64)[None, :])).astype(np.float32),
        "upm64": (np.arange(64)[:, None] <= np.arange(64)[None, :]).astype(np.float32),
    }
    cap = {k: nc.inline_tensor(np.ascontiguousarray(v), "c_" + k).ap() for k, v in consts.items()}

    sb = nc.alloc_sbuf_tensor
    ids = sb("ids", [128, 128], BF16)
    idf = sb("idf", [128, 128], F32)
    onesr = sb("onesr", [1, 128], BF16)
    onesrf = sb("onesrf", [1, 128], F32)
    segs = sb("segs", [NH, S], BF16)
    bigm = sb("bigm", [64, 64], F32)
    upm64 = sb("upm64", [64, 64], F32)
    xt2 = [sb(f"xt{i}", [128, D], F32) for i in range(2)]
    sq = sb("sq", [128, D], F32)
    xlnb = sb("xlnb", [128, D], BF16)
    xlnT = [sb(f"xlnT{i}", [128, S], BF16) for i in range(ND)]
    winc = [sb(f"winc{i}", [128, 128], BF16) for i in range(ND)]
    xc = [sb(f"xc{i}", [128, PAD + S], BF16) for i in range(18)]
    cw = [sb(f"cw{i}", [128, 4], F32) for i in range(18)]
    cb = [sb(f"cb{i}", [128, 1], F32) for i in range(18)]
    cs1 = sb("cs1", [128, S], F32)
    dtv = sb("dtv", [NH, S], F32)
    dacs = sb("dacs", [NH, S], F32)
    dtbc = sb("dtbc", [NH, 1], F32)
    acol = sb("acol", [NH, 1], F32)
    onecol = sb("onecol", [NH, 1], F32)
    dpb = sb("dpb", [64, NH], F32)
    dacs_tm = sb("dacs_tm", [128, 512], F32)
    dtv_tm = sb("dtv_tm", [128, 512], F32)
    dtq = sb("dtq", [64, NH], F32)
    dc = sb("dc", [64, NH], F32)
    ea = sb("ea", [64, NH], F32)
    Gm = sb("Gm", [64, 64], BF16)
    dpx = sb("dpx", [64, DI], BF16)
    btm = sb("btm", [64, 128], BF16)
    cdrow = sb("cdrow", [1, NH], F32)
    yscr = sb("yscr", [64, 512], F32)
    zs = sb("zs", [128, S], BF16)
    rrowb = sb("rrowb", [1, S], BF16)
    rb128 = sb("rb128", [128, S], BF16)
    nws = sb("nws", [128, DI // 128], F32)
    wout = [sb(f"wout{i}", [128, D], BF16) for i in range(16)]
    epscol = sb("epscol", [128, 1], F32)
    onescol = sb("onescol", [128, 1], BF16)
    st = LNStats(nc, "m")
    a0 = nc.lookup_mloc(xlnT[0]).addr
    at = nc.alloc_sbuf_tensor_at
    A2f = at("A2f", [64, DI], F32, offset=a0)
    statef = at("statef", [128, DI], F32, offset=a0 + 8192)
    stateb = at("stateb", [128, DI], BF16, offset=a0 + 16384)
    x_tm = at("x_tm", [64, DI], BF16, offset=a0 + 20480)
    dtx = at("dtx", [64, DI], BF16, offset=a0 + 24576)
    MT = at("MT", [64, DI], BF16, offset=a0 + 28672)
    dtxd = x_tm
    zaddr = nc.lookup_mloc(zs).addr
    yh = at("yh", [64, DI], BF16, offset=zaddr)
    cdb64 = cs1
    dtm0 = nc.lookup_mloc(dacs_tm).addr
    assert nc.lookup_mloc(dtv_tm).addr == dtm0 + 2048
    wsum = at("wsum", [128, D], F32, offset=dtm0)
    zdram = nc.dram_tensor("zdram", [DI, S], BF16, kind="Internal").ap()

    ps = nc.alloc_psum_tensor
    p_trb = ps("p_trb", [128, 512], BF16)
    p_trf = ps("p_trf", [128, 512], F32)
    p_ya = [ps(f"p_ya{i}", [128, 512], F32) for i in range(2)]
    p_yo = [ps(f"p_yo{i}", [128, 512], F32) for i in range(2)]
    p_st = [ps(f"p_st{i}", [128, 512], F32) for i in range(2)]

    t, v, s = nc.tensor, nc.vector, nc.scalar
    kb.sdma(ids[:, :], cap["ident"][:, :])
    kb.sdma(idf[:, :], cap["identf"][:, :])
    kb.sdma(onesr[:, :], cap["onesrow"][:, :])
    kb.sdma(onesrf[:, :], cap["onesrowf"][:, :])
    kb.sdma(segs[:, :], cap["segmask"][:, :])
    kb.sdma(bigm[:, :], cap["bigmask"][:, :])
    kb.sdma(upm64[:, :], cap["upm64"][:, :])
    kb.sdma(nws[:, :], nwr[:, :])
    for j in range(18):
        kb.sdma(cw[j][:, :], convw[128 * j:128 * (j + 1), :])
        kb.sdma(cb[j][:, :], convb[128 * j:128 * (j + 1), :])
    kb.sdma(dtbc[:, :], dtb[:, :])
    kb.sdma(acol[:, :], alog[:, :])
    for j in range(16):
        kb.gdma(wout[j][:, :], woutT[128 * j:128 * (j + 1), :])
    kb.vop(v.memset, epscol[:, :], LN_EPS)
    kb.vop(v.memset, onecol[:, :], 1.0)
    with nc.allow_low_precision(reason="ones"):
        kb.vop(v.memset, onescol[:, :], 1.0)
    kb.barrier()
    kb.sop(s.activation, acol[:, :], acol[:, :], AF.Exp)
    kb.wait(v, "act")
    kb.vop(v.tensor_scalar_mul, acol[:, :], acol[:, :], -1.0)
    kb.sdma(dtq[0:NH, 0:1], dpar[:, :])
    kb.barrier()
    kb.tick("pe", t.transpose(p_trf[0:1, 0:NH], dtq[0:NH, 0:1], idf[0:NH, 0:NH]))
    kb.wait(v, "pe")
    kb.vop(v.tensor_copy, cdrow[0:1, 0:NH], p_trf[0:1, 0:NH])
    kb.wait(t, "dve")
    kb.tick("pe", t.matmul(p_trf[0:64, 0:NH], onesrf[0:1, 0:64], cdrow[0:1, 0:NH],
                           start=True, stop=True))
    kb.wait(v, "pe")
    kb.vop(v.tensor_copy, dpb[:, :], p_trf[0:64, 0:NH])
    kb.barrier()

    def load_win_cols(c0, n):
        kb.wait(nc.gpsimd, "pe")
        for i in range(ND):
            kb.gdma(winc[i][:, 0:n], winT[128 * i:128 * (i + 1), c0:c0 + n])

    for b in range(BPC):
        tok0 = b * S
        tr_state = {"tr_evac": kb.mark("dve")}
        emit_ln_transpose_stream(kb, x, tok0, xt2, xlnb, sq, st, epscol, xlnT, ids,
                                 p_trb, tr_state)
        kb.barrier()
        if stop == "ln":
            break

        for j in range(18):
            kb.vop(v.memset, xc[j][:, 0:PAD], 0.0)
            load_win_cols(2 * D + 128 * j, 128)
            kb.barrier()
            ev = kb.mark("act")
            for tc in range(4):
                pq = p_st[tc % 2]
                kb.wait_mark(t, ev)
                for dt_ in range(ND):
                    ins = t.matmul(pq[:, :], winc[dt_][:, 0:128],
                                   xlnT[dt_][:, 512 * tc:512 * (tc + 1)],
                                   start=(dt_ == 0), stop=(dt_ == ND - 1))
                kb.tick("pe", ins)
                kb.wait(s, "pe")
                kb.sop(s.activation, xc[j][:, PAD + 512 * tc:PAD + 512 * (tc + 1)],
                       pq[:, :], AF.Copy)
                ev = kb.mark("act")
            kb.barrier()
        for j in range(16):
            load_win_cols(128 * j, 128)
            kb.barrier()
            ev = kb.mark("act")
            for tc in range(4):
                pq = p_ya[tc % 2]
                kb.wait_mark(t, ev)
                for dt_ in range(ND):
                    ins = t.matmul(pq[:, :], winc[dt_][:, 0:128],
                                   xlnT[dt_][:, 512 * tc:512 * (tc + 1)],
                                   start=(dt_ == 0), stop=(dt_ == ND - 1))
                kb.tick("pe", ins)
                kb.wait(s, "pe")
                kb.wait(s, "sdma")
                kb.sop(s.activation, zs[:, 512 * tc:512 * (tc + 1)], pq[:, :], AF.Silu)
                ev = kb.mark("act")
            kb.wait(nc.sync, "act")
            kb.sdma(zdram[128 * j:128 * (j + 1), :], zs[:, :])
        kb.barrier()
        load_win_cols(2 * D + 2304, 32)
        kb.barrier()
        ev = kb.mark("dve")
        for tc in range(4):
            pq = p_st[tc % 2]
            kb.wait_mark(t, ev)
            for dt_ in range(ND):
                ins = t.matmul(pq[0:NH, :], winc[dt_][:, 0:32],
                               xlnT[dt_][:, 512 * tc:512 * (tc + 1)],
                               start=(dt_ == 0), stop=(dt_ == ND - 1))
            kb.tick("pe", ins)
            kb.wait(v, "pe")
            kb.vop(v.tensor_copy, dtv[:, 512 * tc:512 * (tc + 1)], pq[0:NH, :])
            ev = kb.mark("dve")
        kb.barrier()

        if stop == "m1":
            break
        for j in range(18):
            kb.wait(v, "act")
            kb.vop(v.tensor_scalar_mul, cs1[:, :], xc[j][:, 0:S], cw[j][:, 0:1])
            for k_ in range(1, 4):
                kb.vop(v.scalar_tensor_tensor, cs1[:, :], xc[j][:, k_:k_ + S],
                       cw[j][:, k_:k_ + 1], cs1[:, :], ALU.mult, ALU.add)
            kb.wait(s, "dve")
            kb.sop(s.activation, xc[j][:, PAD:PAD + S], cs1[:, :], AF.Silu,
                   bias=cb[j][:, 0:1])
        kb.barrier()

        if stop == "m2":
            break
        kb.vop(v.tensor_scalar_add, dtv[:, :], dtv[:, :], dtbc[:, 0:1])
        kb.vop(v.tensor_scalar_max, dacs[:, :], dtv[:, :], 0.0)
        kb.wait(s, "dve")
        kb.sop(s.activation, cs1[0:NH, :], dtv[:, :], AF.Abs)
        kb.sop(s.activation, cs1[0:NH, :], cs1[0:NH, :], AF.Exp, scale=-1.0)
        kb.sop(s.activation, cs1[0:NH, :], cs1[0:NH, :], AF.Ln, bias=onecol[0:NH, 0:1])
        kb.wait(v, "act")
        kb.vop(v.tensor_add, dtv[:, :], dacs[:, :], cs1[0:NH, :])
        kb.vop(v.tensor_scalar_mul, cs1[0:NH, :], dtv[:, :], acol[:, 0:1])
        kb.vop(v.tensor_tensor_scan, dacs[:, :], segs[:, :], cs1[0:NH, :], 0.0,
               ALU.mult, ALU.add)
        ev = kb.mark("dve")
        for tt in range(NTB):
            kb.wait(t, "dve")
            kb.wait(t, "act")
            kb.wait_mark(t, ev)
            t.transpose(p_trf[:, 0:NH], dacs[0:NH, 128 * tt:128 * (tt + 1)],
                        idf[0:NH, 0:NH])
            kb.tick("pe", t.transpose(p_trf[:, 64:64 + NH],
                                      dtv[0:NH, 128 * tt:128 * (tt + 1)],
                                      idf[0:NH, 0:NH]))
            kb.wait(v, "pe")
            kb.vop(v.tensor_copy, dacs_tm[:, 32 * tt:32 * (tt + 1)], p_trf[:, 0:NH])
            kb.vop(v.tensor_copy, dtv_tm[:, 32 * tt:32 * (tt + 1)], p_trf[:, 64:64 + NH])
            ev = kb.mark("dve")
        kb.barrier()

        if stop == "m3":
            break
        kb.vop(v.memset, statef[:, :], 0.0)
        kb.vop(v.memset, stateb[:, :], 0.0)
        kb.barrier()

        for c in range(NCH):
            ttc, r0 = c // 2, 64 * (c % 2)
            cs_ = slice(PAD + CH * c, PAD + CH * (c + 1))
            ev = kb.mark("dve")
            for jg in range(4):
                kb.wait(t, "dve")
                kb.wait_mark(t, ev)
                for k_ in range(4):
                    j = 4 * jg + k_
                    ins = t.transpose(p_trb[0:64, 128 * k_:128 * (k_ + 1)],
                                      xc[j][:, cs_], ids[:, :])
                kb.tick("pe", ins)
                kb.wait(v, "pe")
                for k_ in range(4):
                    j = 4 * jg + k_
                    kb.vop(v.tensor_copy, x_tm[:, 128 * j:128 * (j + 1)],
                           p_trb[0:64, 128 * k_:128 * (k_ + 1)])
                ev = kb.mark("dve")
            kb.wait(t, "dve")
            t.transpose(p_trb[0:64, 0:128], xc[16][:, cs_], ids[:, :])
            t.matmul(p_trf[0:64, 0:64], xc[16][:, cs_], xc[17][:, cs_],
                     start=True, stop=True)
            kb.tick("pe", t.transpose(p_trf[0:1, 64:64 + NH],
                                      dacs[0:NH, CH * c + 63:CH * c + 64],
                                      idf[0:NH, 0:NH]))
            kb.wait(v, "pe")
            kb.vop(v.tensor_copy, btm[:, :], p_trb[0:64, 0:128])
            with nc.allow_low_precision(reason="bf16 G"):
                kb.vop(v.tensor_tensor, Gm[:, :], p_trf[0:64, 0:64], upm64[:, :], ALU.mult)
            kb.vop(v.tensor_copy, cdrow[0:1, 0:NH], p_trf[0:1, 64:64 + NH])
            kb.wait(s, "dve")
            kb.sop(s.activation, cdrow[0:1, 0:NH], cdrow[0:1, 0:NH], AF.Exp)
            kb.wait(v, "act")
            kb.vop(v.tensor_copy,
                   cdb64[0:1, 0:DI].rearrange("o (h q) -> o h q", q=HP),
                   cdrow[0:1, 0:NH].unsqueeze(2).broadcast_to((1, NH, HP)))
            kb.vop(v.tensor_copy, dtq[:, :], dtv_tm[r0:r0 + 64, 32 * ttc:32 * (ttc + 1)])
            x3d = x_tm[:, :].rearrange("p (h q) -> p h q", q=HP)
            kb.wait(v, "dve")
            with nc.allow_low_precision(reason="bf16 dtx"):
                kb.vop(v.tensor_tensor, dtx[:, :].rearrange("p (h q) -> p h q", q=HP),
                       x3d, dtq[:, :].unsqueeze(2).broadcast_to((64, NH, HP)), ALU.mult)
            kb.vop(v.tensor_copy, A2f[:, :].rearrange("p (h q) -> p h q", q=HP),
                   dacs_tm[r0:r0 + 64, 32 * ttc:32 * (ttc + 1)].unsqueeze(2)
                   .broadcast_to((64, NH, HP)))
            kb.wait(s, "dve")
            kb.sop(s.activation, ea[:, :],
                   dacs_tm[r0:r0 + 64, 32 * ttc:32 * (ttc + 1)], AF.Exp)
            evg = kb.mark("dve")
            for hg in range(4):
                kb.wait(t, "dve")
                kb.wait_mark(t, evg)
                for k_ in range(8):
                    h = 8 * hg + k_
                    ins = t.transpose(p_trf[0:64, 64 * k_:64 * (k_ + 1)],
                                      A2f[:, 64 * h:64 * (h + 1)], idf[0:64, 0:64])
                kb.tick("pe", ins)
                kb.wait(v, "pe")
                kb.vop(v.tensor_sub, A2f[:, 512 * hg:512 * (hg + 1)],
                       p_trf[0:64, :], A2f[:, 512 * hg:512 * (hg + 1)])
                kb.vop(v.tensor_sub,
                       A2f[:, 512 * hg:512 * (hg + 1)].rearrange("p (h q) -> p h q", q=HP),
                       A2f[:, 512 * hg:512 * (hg + 1)].rearrange("p (h q) -> p h q", q=HP),
                       bigm[:, :].unsqueeze(1).broadcast_to((64, 8, HP)))
                evg = kb.mark("dve")
            kb.wait(s, "dve")
            kb.sop(s.activation, A2f[:, :], A2f[:, :], AF.Exp)
            kb.wait(v, "act")
            with nc.allow_low_precision(reason="bf16 SSD L"):
                kb.vop(v.tensor_tensor, MT[:, :].rearrange("p (h q) -> p h q", q=HP),
                       A2f[:, :].rearrange("p (h q) -> p h q", q=HP),
                       Gm[:, :].unsqueeze(1).broadcast_to((64, NH, HP)), ALU.mult)
            kb.vop(v.tensor_copy, dc[:, :].unsqueeze(2),
                   A2f[:, :].rearrange("p (h q) -> p h q", q=HP)[:, :, 63:64])
            with nc.allow_low_precision(reason="bf16 dpx"):
                kb.vop(v.tensor_tensor, dpx[:, :].rearrange("p (h q) -> p h q", q=HP),
                       x3d, dpb[:, :].unsqueeze(2).broadcast_to((64, NH, HP)), ALU.mult)
            with nc.allow_low_precision(reason="bf16 dtxd"):
                kb.vop(v.tensor_tensor, dtxd[:, :].rearrange("p (h q) -> p h q", q=HP),
                       dtx[:, :].rearrange("p (h q) -> p h q", q=HP),
                       dc[:, :].unsqueeze(2).broadcast_to((64, NH, HP)), ALU.mult)
            for Hh in range(2):
                h0 = 16 * Hh
                kb.wait(t, "dve")
                for k_ in range(16):
                    h = h0 + k_
                    t.matmul(p_ya[k_ // 8][0:64, 64 * (k_ % 8):64 * (k_ % 8 + 1)],
                             MT[:, 64 * h:64 * (h + 1)], dtx[:, 64 * h:64 * (h + 1)],
                             start=True, stop=True)
                for k_ in range(16):
                    h = h0 + k_
                    ins = t.matmul(p_yo[k_ // 8][0:64, 64 * (k_ % 8):64 * (k_ % 8 + 1)],
                                   xc[17][:, cs_], stateb[:, 64 * h:64 * (h + 1)],
                                   start=True, stop=True)
                kb.tick("pe", ins)
                for sg_ in range(2):
                    ins = t.matmul(p_st[sg_][:, :], btm[:, :],
                                   dtxd[:, 1024 * Hh + 512 * sg_:1024 * Hh + 512 * (sg_ + 1)],
                                   start=True, stop=True)
                kb.tick("pe", ins)
                kb.wait(v, "pe")
                for sg_ in range(2):
                    so = 1024 * Hh + 512 * sg_
                    kb.vop(v.tensor_tensor, yscr[:, :].rearrange("p (h q) -> p h q", q=HP),
                           p_yo[sg_][0:64, :].rearrange("p (h q) -> p h q", q=HP),
                           ea[:, h0 + 8 * sg_:h0 + 8 * (sg_ + 1)].unsqueeze(2)
                           .broadcast_to((64, 8, HP)), ALU.mult)
                    kb.vop(v.tensor_add, yscr[:, :], yscr[:, :], p_ya[sg_][0:64, :])
                    with nc.allow_low_precision(reason="bf16 yh"):
                        kb.vop(v.tensor_add, yh[:, so:so + 512], yscr[:, :],
                               dpx[:, so:so + 512])
                kb.wait(t, "dve")
                for sg_ in range(2):
                    so = 1024 * Hh + 512 * sg_
                    ins = t.matmul(p_ya[sg_][:, :], onesrf[0:1, :],
                                   cdb64[0:1, so:so + 512], start=True, stop=True)
                kb.tick("pe", ins)
                kb.wait(v, "pe")
                for sg_ in range(2):
                    so = 1024 * Hh + 512 * sg_
                    sl4 = slice(so, so + 512)
                    kb.vop(v.tensor_tensor, statef[:, sl4], statef[:, sl4],
                           p_ya[sg_][:, :], ALU.mult)
                    kb.vop(v.tensor_add, statef[:, sl4], statef[:, sl4],
                           p_st[sg_][:, :])
                    with nc.allow_low_precision(reason="bf16 state copy"):
                        kb.vop(v.tensor_copy, stateb[:, sl4], statef[:, sl4])
            ev = kb.mark("dve")
            for jg in range(2):
                kb.wait(t, "dve")
                kb.wait_mark(t, ev)
                for k_ in range(8):
                    j = 8 * jg + k_
                    ins = t.transpose(p_trb[:, 64 * k_:64 * (k_ + 1)],
                                      yh[:, 128 * j:128 * (j + 1)], ids[0:64, 0:64])
                kb.tick("pe", ins)
                kb.wait(v, "pe")
                for k_ in range(8):
                    j = 8 * jg + k_
                    kb.vop(v.tensor_copy, xc[j][:, cs_], p_trb[:, 64 * k_:64 * (k_ + 1)])
                ev = kb.mark("dve")
        kb.barrier()

        if stop == "m4":
            break
        kb.vop(v.memset, cs1[0:1, 0:S], 0.0)
        for j in range(16):
            kb.wait(nc.sync, "pe")
            kb.wait(nc.sync, "dve")
            kb.sdma(zs[:, :], zdram[128 * j:128 * (j + 1), :])
            kb.wait(v, "sdma")
            with nc.allow_low_precision(reason="bf16 gated stream"):
                kb.vop(v.tensor_tensor, xc[j][:, PAD:PAD + S], xc[j][:, PAD:PAD + S],
                       zs[:, :], ALU.mult)
            kb.wait(s, "dve")
            kb.sop(s.activation, zs[:, :], xc[j][:, PAD:PAD + S], AF.Square)
            kb.wait(t, "act")
            for tc in range(4):
                kb.tick("pe", t.matmul(p_trf[0:1, 0:512], onescol[:, 0:1],
                                       zs[:, 512 * tc:512 * (tc + 1)],
                                       start=True, stop=True))
                kb.wait(v, "pe")
                kb.vop(v.tensor_add, cs1[0:1, 512 * tc:512 * (tc + 1)],
                       cs1[0:1, 512 * tc:512 * (tc + 1)], p_trf[0:1, 0:512])
                kb.wait(t, "dve")
        kb.barrier()
        if stop == "m5g":
            break
        kb.sop(s.activation, cs1[0:1, 0:S], cs1[0:1, 0:S], AF.Ln,
               scale=1.0 / DI, bias=epscol[0:1, 0:1])
        kb.sop(s.activation, cs1[0:1, 0:S], cs1[0:1, 0:S], AF.Exp, scale=-0.5)
        kb.wait(v, "act")
        with nc.allow_low_precision(reason="bf16 rstd"):
            kb.vop(v.tensor_copy, rrowb[0:1, :], cs1[0:1, 0:S])
        kb.wait(t, "dve")
        ev = kb.mark("dve")
        for tc in range(4):
            kb.wait_mark(t, ev)
            kb.tick("pe", t.matmul(p_trf[:, 0:512], onesr[0:1, :],
                                   rrowb[0:1, 512 * tc:512 * (tc + 1)],
                                   start=True, stop=True))
            kb.wait(v, "pe")
            with nc.allow_low_precision(reason="bf16 rstd bcast"):
                kb.vop(v.tensor_copy, rb128[:, 512 * tc:512 * (tc + 1)], p_trf[:, 0:512])
            ev = kb.mark("dve")
        for j in range(16):
            with nc.allow_low_precision(reason="bf16 gn"):
                kb.vop(v.scalar_tensor_tensor, xc[j][:, PAD:PAD + S],
                       xc[j][:, PAD:PAD + S], nws[:, j:j + 1], rb128[:, :],
                       ALU.mult, ALU.mult)
        kb.barrier()
        res_done = [kb.mark("dve"), kb.mark("dve")]
        for tt in range(NTB):
            xb_ = xt2[tt % 2]
            kb.wait_mark(nc.sync, res_done[tt % 2])
            kb.sdma(xb_[:, :], x[tok0 + 128 * tt: tok0 + 128 * (tt + 1), :])
            kb.wait(t, "dve")
            for cc_, pw in ((0, p_st[0]), (1, p_st[1])):
                for j in range(16):
                    ins = t.matmul(pw[:, :], xc[j][:, PAD + 128 * tt:PAD + 128 * (tt + 1)],
                                   wout[j][:, 512 * cc_:512 * (cc_ + 1)],
                                   start=(j == 0), stop=(j == 15))
                kb.tick("pe", ins)
            kb.wait(v, "pe")
            kb.wait(v, "sdma")
            kb.vop(v.tensor_add, wsum[:, 0:512], xb_[:, 0:512], p_st[0][:, :])
            kb.vop(v.tensor_add, wsum[:, 512:1024], xb_[:, 512:1024], p_st[1][:, :])
            res_done[tt % 2] = kb.mark("dve")
            kb.wait(nc.sync, "dve")
            kb.sdma(x3[tok0 + 128 * tt: tok0 + 128 * (tt + 1), :], wsum[:, :])
        kb.barrier()
        if stop == "m5w":
            break
    if debug_taps:
        groups = {
            "ln": {"d_xlnT0": xlnT[0]},
            "m1": {"d_xlnT0": xlnT[0], "d_xc5": xc[5], "d_dtv": dtv},
            "m2": {"d_xc5": xc[5], "d_xc16": xc[16], "d_xc17": xc[17]},
            "m3": {"d_dtv": dtv, "d_dacs": dacs, "d_dacs_tm": dacs_tm},
            "m4": {"d_xc5": xc[5], "d_yh": yh, "d_statef": statef, "d_xtm": x_tm,
                   "d_dtx": dtx, "d_MT": MT, "d_A2f": A2f, "d_Gm": Gm, "d_ea": ea,
                   "d_dc": dc, "d_dpb": dpb, "d_dacs_tm": dacs_tm},
            "m5g": {"d_xc5": xc[5], "d_cs1": cs1},
            "m5w": {"d_rb128": rb128, "d_xc5": xc[5], "d_cs1": cs1},
        }
        taps = groups.get(stop, {"d_xlnT0": xlnT[0]})
        for nm, tile in taps.items():
            shp = list(tile.shape)
            o = nc.dram_tensor(nm, shp, tile.dtype, kind="ExternalOutput").ap()
            kb.sdma(o[:, :], tile[:, :])
        kb.barrier()
    kb.finish()
    return kb.nc


def build_l1_gate():
    kb = KB("l1gate")
    nc = kb.nc
    E = 16
    x = nc.dram_tensor("x", [TPC, D], F32, kind="ExternalInput").ap()
    gwT = nc.dram_tensor("gwT", [D, E], F32, kind="ExternalInput").ap()
    o_xln = nc.dram_tensor("o_xln", [TPC, D], BF16, kind="ExternalOutput").ap()
    o_oh = nc.dram_tensor("o_oh", [TPC, E], F32, kind="ExternalOutput").ap()
    o_loc = nc.dram_tensor("o_loc", [TPC, 1], F32, kind="ExternalOutput").ap()
    o_pmax = nc.dram_tensor("o_pmax", [TPC, 1], F32, kind="ExternalOutput").ap()
    o_eidx = nc.dram_tensor("o_eidx", [TPC, 1], F32, kind="ExternalOutput").ap()
    o_cnt = nc.dram_tensor("o_cnt", [1, E], F32, kind="ExternalOutput").ap()

    consts = {
        "ident": np.eye(128, dtype=BF),
        "ustrict": (np.arange(128)[:, None] < np.arange(128)[None, :]).astype(BF),
        "iota16": np.broadcast_to(np.arange(16, dtype=np.float32), (128, 16)).copy(),
        "ones16": np.ones((128, 16), dtype=np.float32),
    }
    cap = {k: nc.inline_tensor(np.ascontiguousarray(v), "c_" + k).ap() for k, v in consts.items()}

    sb = nc.alloc_sbuf_tensor
    ids = sb("ids", [128, 128], BF16)
    ustr = sb("ustr", [128, 128], BF16)
    iot = sb("iot", [128, E], F32)
    on16 = sb("on16", [128, E], F32)
    onescol = sb("onescol", [128, 1], BF16)
    onesrf = sb("onesrf2", [1, 128], F32)
    gws = [sb(f"gws{i}", [128, E], BF16) for i in range(ND)]
    xt2 = [sb(f"xt{i}", [128, D], F32) for i in range(2)]
    sq = sb("sq", [128, D], F32)
    xlnb = sb("xlnb", [128, D], BF16)
    xlt = [sb(f"xlt{i}", [128, 128], BF16) for i in range(ND)]
    lg = sb("lg", [128, E], F32)
    ohb = sb("ohb", [128, E], BF16)
    ohf = sb("ohf", [128, E], F32)
    cum = sb("cum", [128, E], F32)
    loc = sb("loc", [128, E], F32)
    gmax = sb("gmax", [128, 1], F32)
    gsum = sb("gsum", [128, 1], F32)
    loccol = sb("loccol", [128, 1], F32)
    eidxc = sb("eidxc", [128, 1], F32)
    roff = sb("roff", [1, E], F32)
    epscol = sb("epscol", [128, 1], F32)
    st = LNStats(nc, "g")

    ps = nc.alloc_psum_tensor
    p_tr = ps("p_tr", [128, 512], BF16)
    p_g = ps("p_g", [128, E], F32)
    p_w = ps("p_w", [128, E], F32)
    p_b = ps("p_b", [128, E], F32)
    p_r = ps("p_r", [1, E], F32)

    t, v, s = nc.tensor, nc.vector, nc.scalar
    kb.sdma(ids[:, :], cap["ident"][:, :])
    kb.sdma(ustr[:, :], cap["ustrict"][:, :])
    kb.sdma(iot[:, :], cap["iota16"][:, :])
    kb.sdma(on16[:, :], cap["ones16"][:, :])
    for i in range(ND):
        kb.gdma(gws[i][:, :], gwT[128 * i:128 * (i + 1), :])
    kb.vop(v.memset, epscol[:, :], LN_EPS)
    with nc.allow_low_precision(reason="ones"):
        kb.vop(v.memset, onescol[:, :], 1.0)
    kb.vop(v.memset, onesrf[:, :], 1.0)
    kb.vop(v.memset, roff[:, :], 0.0)
    kb.barrier()

    ln_done = [kb.mark("act"), kb.mark("act")]
    tr_evac = kb.mark("pe")
    for gt in range(TPC // 128):
        xb_ = xt2[gt % 2]
        kb.wait_mark(nc.sync, ln_done[gt % 2])
        kb.sdma(xb_[:, :], x[128 * gt:128 * (gt + 1), :])
        kb.wait(v, "sdma")
        kb.wait(s, "sdma")
        kb.wait(s, "pe")
        ln_tile(kb, xb_[:, :], xlnb[:, :], sq[:, :], st, epscol)
        ln_done[gt % 2] = kb.mark("act")
        kb.wait(nc.sync, "act")
        kb.sdma(o_xln[128 * gt:128 * (gt + 1), :], xlnb[:, :])
        kb.wait(t, "act")
        for g in range(2):
            kb.wait(t, "dve")
            for k_ in range(4):
                dt = 4 * g + k_
                ins = t.transpose(p_tr[:, 128 * k_:128 * (k_ + 1)],
                                  xlnb[:, 128 * dt:128 * (dt + 1)], ids[:, :])
            kb.tick("pe", ins)
            kb.wait(v, "pe")
            for k_ in range(4):
                dt = 4 * g + k_
                kb.vop(v.tensor_copy, xlt[dt][:, :], p_tr[:, 128 * k_:128 * (k_ + 1)])
        kb.wait(t, "dve")
        for dt in range(ND):
            ins = t.matmul(p_g[:, :], xlt[dt][:, :], gws[dt][:, :],
                           start=(dt == 0), stop=(dt == ND - 1))
        kb.tick("pe", ins)
        kb.wait(v, "pe")
        kb.vop(v.tensor_copy, lg[:, :], p_g[:, :])
        kb.vop(v.reduce_max, gmax[:, :], lg[:, :], AX.X, negate=True)
        kb.wait(s, "dve")
        kb.sop(s.activation, lg[:, :], lg[:, :], AF.Exp, bias=gmax[:, 0:1])
        kb.wait(v, "act")
        kb.vop(v.reduce_sum, gsum[:, :], lg[:, :], AX.X)
        kb.vop(v.reciprocal, gsum[:, :], gsum[:, :])
        kb.vop(v.tensor_scalar_mul, lg[:, :], lg[:, :], gsum[:, 0:1])
        kb.vop(v.reduce_max, gmax[:, :], lg[:, :], AX.X)
        kb.vop(v.tensor_scalar, ohf[:, :], lg[:, :], gmax[:, 0:1], None, ALU.is_equal)
        kb.vop(v.tensor_tensor_scan, cum[:, :], on16[:, :], ohf[:, :], 0.0,
               ALU.mult, ALU.add)
        kb.vop(v.tensor_single_scalar, cum[:, :], cum[:, :], 1.0, ALU.is_equal)
        kb.vop(v.tensor_tensor, ohf[:, :], ohf[:, :], cum[:, :], ALU.mult)
        with nc.allow_low_precision(reason="bf16 onehot"):
            kb.vop(v.tensor_copy, ohb[:, :], ohf[:, :])
        kb.vop(v.tensor_tensor, cum[:, :], ohf[:, :], iot[:, :], ALU.mult)
        kb.vop(v.reduce_sum, eidxc[:, :], cum[:, :], AX.X)
        kb.wait(t, "dve")
        t.matmul(p_w[:, :], ustr[:, :], ohb[:, :], start=True, stop=False)
        kb.tick("pe", t.matmul(p_w[:, :], onesrf[0:1, :], roff[0:1, :],
                               start=False, stop=True))
        kb.wait(v, "pe")
        kb.vop(v.tensor_tensor, loc[:, :], p_w[:, :], ohf[:, :], ALU.mult)
        kb.vop(v.reduce_sum, loccol[:, :], loc[:, :], AX.X)
        kb.wait(t, "dve")
        kb.tick("pe", t.matmul(p_r[0:1, :], onescol[:, 0:1], ohb[:, :],
                               start=True, stop=True))
        kb.wait(v, "pe")
        kb.vop(v.tensor_add, roff[0:1, :], roff[0:1, :], p_r[0:1, :])
        kb.wait(nc.sync, "dve")
        kb.sdma(o_oh[128 * gt:128 * (gt + 1), :], ohf[:, :])
        kb.sdma(o_loc[128 * gt:128 * (gt + 1), :], loccol[:, :])
        kb.sdma(o_pmax[128 * gt:128 * (gt + 1), :], gmax[:, :])
        kb.sdma(o_eidx[128 * gt:128 * (gt + 1), :], eidxc[:, :])
    kb.wait(nc.sync, "dve")
    kb.sdma(o_cnt[:, :], roff[:, :])
    kb.finish()
    return kb.nc


def build_l1_slots():
    kb = KB("l1slots")
    nc = kb.nc
    E, C = 16, 2560
    allcnt = nc.dram_tensor("allcnt", [NCORES, E], F32, kind="ExternalInput").ap()
    coremask = nc.dram_tensor("coremask", [NCORES, 1], F32, kind="ExternalInput").ap()
    tokidx = nc.dram_tensor("tokidx", [TPC, 1], F32, kind="ExternalInput").ap()
    oh = nc.dram_tensor("oh", [TPC, E], F32, kind="ExternalInput").ap()
    loc = nc.dram_tensor("loc", [TPC, 1], F32, kind="ExternalInput").ap()
    pmax = nc.dram_tensor("pmax", [TPC, 1], F32, kind="ExternalInput").ap()
    eidx = nc.dram_tensor("eidx", [TPC, 1], F32, kind="ExternalInput").ap()
    o_idx = nc.dram_tensor("o_idx", [E * C, 1], F32, kind="ExternalOutput").ap()
    o_w = nc.dram_tensor("o_w", [TPC, 1], F32, kind="ExternalOutput").ap()
    o_slot = nc.dram_tensor("o_slot", [TPC, 1], F32, kind="ExternalOutput").ap()

    sb = nc.alloc_sbuf_tensor
    acnt = sb("acnt", [NCORES, E], F32)
    cmsk = sb("cmsk", [NCORES, 1], F32)
    cobc = sb("cobc", [128, E], F32)
    ohs = sb("ohs", [128, E], F32)
    locs = sb("locs", [128, 1], F32)
    pms = sb("pms", [128, 1], F32)
    eis = sb("eis", [128, 1], F32)
    tks = sb("tks", [128, 1], F32)
    sel = sb("sel", [128, E], F32)
    gl = sb("gl", [128, 1], F32)
    kp = sb("kp", [128, 1], F32)
    wc = sb("wc", [128, 1], F32)
    sl2 = sb("sl2", [128, 1], F32)
    vals = sb("vals", [128, 1], F32)
    sidx = sb("sidx", [128, 1], I32)
    zt = sb("zt", [128, 320], F32)
    p_c = nc.alloc_psum_tensor("p_c", [128, E], F32)

    t, v, s = nc.tensor, nc.vector, nc.scalar
    kb.sdma(acnt[:, :], allcnt[:, :])
    kb.sdma(cmsk[:, :], coremask[:, :])
    kb.vop(v.memset, zt[:, :], 0.0)
    kb.barrier()
    kb.sdma(o_idx[:, :].rearrange("(p q) o -> p (q o)", p=128), zt[:, :])
    onesr = sb("onesr", [1, 128], F32)
    kb.vop(v.memset, onesr[:, :], 1.0)
    kb.tick("pe", t.matmul(p_c[0:1, :], cmsk[:, :], acnt[:, :], start=True, stop=True))
    kb.wait(v, "pe")
    kb.vop(v.tensor_copy, cobc[0:1, :], p_c[0:1, :])
    kb.wait(t, "dve")
    kb.tick("pe", t.matmul(p_c[:, :], onesr[0:1, :], cobc[0:1, :], start=True, stop=True))
    kb.wait(v, "pe")
    kb.vop(v.tensor_copy, cobc[:, :], p_c[:, :])
    kb.barrier()

    for gt in range(TPC // 128):
        r = slice(128 * gt, 128 * (gt + 1))
        kb.wait(nc.sync, "dve")
        kb.wait(nc.sync, "gdma")
        kb.sdma(ohs[:, :], oh[r, :])
        kb.sdma(locs[:, :], loc[r, :])
        kb.sdma(pms[:, :], pmax[r, :])
        kb.sdma(eis[:, :], eidx[r, :])
        kb.sdma(tks[:, :], tokidx[r, :])
        kb.wait(v, "sdma")
        kb.vop(v.tensor_tensor, sel[:, :], ohs[:, :], cobc[:, :], ALU.mult)
        kb.vop(v.reduce_sum, gl[:, :], sel[:, :], AX.X)
        kb.vop(v.tensor_add, gl[:, :], gl[:, :], locs[:, :])
        kb.vop(v.tensor_single_scalar, kp[:, :], gl[:, :], float(C), ALU.is_lt)
        kb.vop(v.tensor_tensor, wc[:, :], pms[:, :], kp[:, :], ALU.mult)
        kb.vop(v.tensor_scalar, sl2[:, :], eis[:, :], float(C), gl[:, 0:1],
               ALU.mult, ALU.add)
        kb.vop(v.tensor_scalar, kp[:, :], kp[:, :], -1e9, 1e9, ALU.mult, ALU.add)
        kb.vop(v.tensor_add, sl2[:, :], sl2[:, :], kp[:, :])
        kb.vop(v.tensor_scalar_add, vals[:, :], tks[:, :], 1.0)
        with nc.allow_low_precision(reason="int index cast"):
            kb.vop(v.tensor_copy, sidx[:, :], sl2[:, :])
        kb.wait(nc.sync, "dve")
        kb.sdma(o_w[r, :], wc[:, :])
        kb.sdma(o_slot[r, :], sl2[:, :])
        kb.wait(nc.gpsimd, "dve")
        kb.wait(nc.gpsimd, "sdma")
        ins = nc.gpsimd.indirect_dma_start(
            o_idx[:, :], bass.IndirectOffsetOnAxis(sidx[:, 0:1], 0),
            vals[:, 0:1], None, bounds_check=E * C - 1, oob_is_err=False)
        ins.then_inc(kb.sem["gdma"], 16)
        kb.cnt["gdma"] += 16
    kb.finish()
    return kb.nc


def build_l1_experts():
    kb = KB("l1exp")
    nc = kb.nc
    E, C, FF = 16, 2560, 4096
    NEC = 2
    idx2T = nc.dram_tensor("idx2T", [NEC * C, NCORES], F32, kind="ExternalInput").ap()
    xga = nc.dram_tensor("xga", [B * S, D], BF16, kind="ExternalInput").ap()
    w1 = nc.dram_tensor("w1", [NEC, D, FF], F32, kind="ExternalInput").ap()
    w2 = nc.dram_tensor("w2", [NEC, FF, D], F32, kind="ExternalInput").ap()
    o_y = nc.dram_tensor("o_y", [NEC * C, D], BF16, kind="ExternalOutput").ap()

    sb = nc.alloc_sbuf_tensor
    idc = nc.inline_tensor(np.eye(128, dtype=BF), "c_ident").ap()
    idss = sb("idss", [128, 128], BF16)
    code = sb("code", [128, NCORES], F32)
    gidx = sb("gidx", [128, 1], F32)
    gemp = sb("gemp", [128, 1], F32)
    gi32 = sb("gi32", [128, 1], I32)
    xg = sb("xg", [128, D], BF16)
    xgT = [sb(f"xgT{i}", [128, C], BF16) for i in range(ND)]
    hT = [sb(f"hT{i}", [128, 512], BF16) for i in range(32)]
    w1s = [sb(f"w1s{i}", [128, 512], BF16) for i in range(ND)]
    w2s = [sb(f"w2s{i}", [128, D], BF16) for i in range(32)]
    yt = sb("yt", [128, D], BF16)
    p_tr = nc.alloc_psum_tensor("p_tr", [128, 512], BF16)
    p_h = [nc.alloc_psum_tensor(f"p_h{i}", [128, 512], F32) for i in range(2)]
    p_y = [nc.alloc_psum_tensor(f"p_y{i}", [128, 512], F32) for i in range(4)]

    t, v, s = nc.tensor, nc.vector, nc.scalar
    kb.sdma(idss[:, :], idc[:, :])
    kb.barrier()

    for e in range(NEC):
        kb.wait(nc.gpsimd, "pe")
        for i in range(32):
            kb.gdma(w2s[i][:, :], w2[e, 128 * i:128 * (i + 1), :])
        tr_ev = kb.mark("dve")
        for tl in range(20):
            sl0 = e * C + 128 * tl
            kb.wait(nc.sync, "dve")
            kb.sdma(code[:, :], idx2T[sl0:sl0 + 128, :])
            kb.wait(v, "sdma")
            kb.vop(v.reduce_sum, gidx[:, :], code[:, :], AX.X)
            kb.vop(v.tensor_single_scalar, gemp[:, :], gidx[:, :], 0.0, ALU.is_equal)
            kb.vop(v.tensor_scalar, gemp[:, :], gemp[:, :], 1e9, -1.0, ALU.mult, ALU.add)
            kb.vop(v.tensor_add, gidx[:, :], gidx[:, :], gemp[:, :])
            with nc.allow_low_precision(reason="index cast"):
                kb.vop(v.tensor_copy, gi32[:, :], gidx[:, :])
            kb.wait(v, "gdma")
            kb.wait(v, "pe")
            kb.vop(v.memset, xg[:, :], 0.0)
            kb.wait(nc.gpsimd, "dve")
            ins = nc.gpsimd.indirect_dma_start(
                xg[:, :], None, xga[:, :],
                bass.IndirectOffsetOnAxis(gi32[:, 0:1], 0),
                bounds_check=B * S - 1, oob_is_err=False)
            ins.then_inc(kb.sem["gdma"], 16)
            kb.cnt["gdma"] += 16
            kb.wait(t, "gdma")
            for g in range(2):
                kb.wait(t, "dve")
                for k_ in range(4):
                    dt = 4 * g + k_
                    ins = t.transpose(p_tr[:, 128 * k_:128 * (k_ + 1)],
                                      xg[:, 128 * dt:128 * (dt + 1)], idss[:, :])
                    if k_ == 3:
                        kb.tick("pe", ins)
                kb.wait(v, "pe")
                for k_ in range(4):
                    dt = 4 * g + k_
                    kb.vop(v.tensor_copy, xgT[dt][:, 128 * tl:128 * (tl + 1)],
                           p_tr[:, 128 * k_:128 * (k_ + 1)])
            tr_ev = kb.mark("dve")
        kb.barrier()
        for sc in range(5):
            ssl = slice(512 * sc, 512 * (sc + 1))
            for fq in range(8):
                kb.wait(nc.gpsimd, "pe")
                for i in range(ND):
                    kb.gdma(w1s[i][:, :], w1[e, 128 * i:128 * (i + 1),
                                             512 * fq:512 * (fq + 1)])
                kb.wait(t, "gdma")
                hev = kb.mark("act")
                for ft in range(4):
                    ph = p_h[ft % 2]
                    kb.wait_mark(t, hev)
                    for dt in range(ND):
                        ins = t.matmul(ph[:, :], w1s[dt][:, 128 * ft:128 * (ft + 1)],
                                       xgT[dt][:, ssl],
                                       start=(dt == 0), stop=(dt == ND - 1))
                    kb.tick("pe", ins)
                    kb.wait(s, "pe")
                    kb.sop(s.activation, hT[4 * fq + ft][:, :], ph[:, :], AF.Relu)
                    hev = kb.mark("act")
            kb.barrier()
            yev = [kb.mark("dve"), kb.mark("dve")]
            for st_ in range(4):
                pa, pb_ = p_y[2 * (st_ % 2)], p_y[2 * (st_ % 2) + 1]
                kb.wait_mark(t, yev[st_ % 2])
                for fl in range(32):
                    t.matmul(pa[:, :], hT[fl][:, 128 * st_:128 * (st_ + 1)],
                             w2s[fl][:, 0:512], start=(fl == 0), stop=(fl == 31))
                for fl in range(32):
                    ins = t.matmul(pb_[:, :], hT[fl][:, 128 * st_:128 * (st_ + 1)],
                                   w2s[fl][:, 512:1024], start=(fl == 0), stop=(fl == 31))
                kb.tick("pe", ins)
                kb.wait(v, "pe")
                kb.wait(v, "sdma")
                with nc.allow_low_precision(reason="bf16 expert out"):
                    kb.vop(v.tensor_copy, yt[:, 0:512], pa[:, :])
                    kb.vop(v.tensor_copy, yt[:, 512:1024], pb_[:, :])
                yev[st_ % 2] = kb.mark("dve")
                kb.wait(nc.sync, "dve")
                kb.sdma(o_y[e * C + 512 * sc + 128 * st_:
                            e * C + 512 * sc + 128 * (st_ + 1), :], yt[:, :])
            kb.barrier()
    kb.finish()
    return kb.nc


def build_l1_combine():
    kb = KB("l1comb")
    nc = kb.nc
    E, C = 16, 2560
    x = nc.dram_tensor("x", [TPC, D], F32, kind="ExternalInput").ap()
    ya = nc.dram_tensor("ya", [E * C, D], BF16, kind="ExternalInput").ap()
    slot = nc.dram_tensor("slot", [TPC, 1], F32, kind="ExternalInput").ap()
    w = nc.dram_tensor("w", [TPC, 1], F32, kind="ExternalInput").ap()
    out = nc.dram_tensor("out", [TPC, D], F32, kind="ExternalOutput").ap()

    sb = nc.alloc_sbuf_tensor
    xs = sb("xs", [128, D], F32)
    yg = sb("yg", [128, D], BF16)
    sc_ = sb("sc", [128, 1], F32)
    wc = sb("wc", [128, 1], F32)
    si32 = sb("si32", [128, 1], I32)
    os_ = sb("os", [128, D], F32)
    v = nc.vector
    for gt in range(TPC // 128):
        r = slice(128 * gt, 128 * (gt + 1))
        kb.wait(nc.sync, "dve")
        kb.sdma(xs[:, :], x[r, :])
        kb.sdma(sc_[:, :], slot[r, :])
        kb.sdma(wc[:, :], w[r, :])
        kb.wait(v, "sdma")
        kb.wait(v, "gdma")
        kb.vop(v.memset, yg[:, :], 0.0)
        with nc.allow_low_precision(reason="index cast"):
            kb.vop(v.tensor_copy, si32[:, :], sc_[:, :])
        kb.wait(nc.gpsimd, "dve")
        ins = nc.gpsimd.indirect_dma_start(
            yg[:, :], None, ya[:, :], bass.IndirectOffsetOnAxis(si32[:, 0:1], 0),
            bounds_check=E * C - 1, oob_is_err=False)
        ins.then_inc(kb.sem["gdma"], 16)
        kb.cnt["gdma"] += 16
        kb.wait(v, "gdma")
        kb.vop(v.scalar_tensor_tensor, os_[:, :], yg[:, :], wc[:, 0:1], xs[:, :],
               ALU.mult, ALU.add)
        kb.wait(nc.sync, "dve")
        kb.sdma(out[r, :], os_[:, :])
    kb.finish()
    return kb.nc

def _shard_rows(a):
    a = np.ascontiguousarray(a)
    return [a[i * TPC:(i + 1) * TPC] for i in range(NCORES)]


_NC_CACHE = {}


def _get_nc(name, builder):
    if name not in _NC_CACHE:
        _NC_CACHE[name] = builder()
    return _NC_CACHE[name]


def kernel(**inputs):
    x = np.ascontiguousarray(np.asarray(inputs["L0_attn_x"], dtype=np.float32).reshape(B * S, D))
    wqkvT = np.ascontiguousarray(np.asarray(inputs["L0_attn_W_qkv"], dtype=np.float32).T)
    woT = np.ascontiguousarray(np.asarray(inputs["L0_attn_W_o"], dtype=np.float32).T)
    core_ids = list(range(NCORES))

    nc1 = _get_nc("l0attn", build_l0_attn)
    xs = _shard_rows(x)
    res = run_bass_kernel_spmd(
        nc1, [{"x": xs[i], "wqkvT": wqkvT, "woT": woT} for i in range(NCORES)],
        core_ids=core_ids)
    x1 = [res.results[i]["x1"] for i in range(NCORES)]

    w1 = np.ascontiguousarray(np.asarray(inputs["L0_ffn_W1"], dtype=np.float32))
    b1 = np.asarray(inputs["L0_ffn_b1"], dtype=np.float32)
    b1r = np.ascontiguousarray(b1.reshape(8, 32, 128).transpose(0, 2, 1))
    w2 = np.ascontiguousarray(np.asarray(inputs["L0_ffn_W2"], dtype=np.float32))
    b2 = np.ascontiguousarray(np.asarray(inputs["L0_ffn_b2"], dtype=np.float32))
    wr = np.ascontiguousarray(np.asarray(inputs["L0_ffn_Wr"], dtype=np.float32))
    br = np.ascontiguousarray(np.asarray(inputs["L0_ffn_br"], dtype=np.float32).reshape(1, 8))
    nc2 = _get_nc("l0ffn", build_l0_ffn)
    res = run_bass_kernel_spmd(
        nc2, [{"x": x1[i], "wr": wr, "br": br, "w1": w1, "b1r": b1r, "w2": w2,
               "b2": b2} for i in range(NCORES)], core_ids=core_ids)
    x2 = [res.results[i]["x2"] for i in range(NCORES)]

    winT = np.ascontiguousarray(np.asarray(inputs["L1_attn_W_in"], dtype=np.float32).T)
    convw = np.ascontiguousarray(np.asarray(inputs["L1_attn_conv_w"], dtype=np.float32))
    convb = np.ascontiguousarray(np.asarray(inputs["L1_attn_conv_b"], dtype=np.float32).reshape(-1, 1))
    dtb = np.ascontiguousarray(np.asarray(inputs["L1_attn_dt_bias"], dtype=np.float32).reshape(-1, 1))
    alog = np.ascontiguousarray(np.asarray(inputs["L1_attn_A_log"], dtype=np.float32).reshape(-1, 1))
    dpar = np.ascontiguousarray(np.asarray(inputs["L1_attn_D_param"], dtype=np.float32).reshape(-1, 1))
    nwr = np.ascontiguousarray(np.asarray(inputs["L1_attn_norm_w"], dtype=np.float32).reshape(16, 128).T)
    woutT = np.ascontiguousarray(np.asarray(inputs["L1_attn_W_out"], dtype=np.float32).T)
    nc3 = _get_nc("mamba", build_l1_attn)
    res = run_bass_kernel_spmd(
        nc3, [{"x": x2[i], "winT": winT, "convw": convw, "convb": convb, "dtb": dtb,
               "alog": alog, "dpar": dpar, "nwr": nwr, "woutT": woutT}
              for i in range(NCORES)], core_ids=core_ids)
    x3 = [res.results[i]["x3"] for i in range(NCORES)]

    gwT = np.ascontiguousarray(np.asarray(inputs["L1_ffn_gate_w"], dtype=np.float32).T)
    nc4a = _get_nc("l1gate", build_l1_gate)
    res = run_bass_kernel_spmd(
        nc4a, [{"x": x3[i], "gwT": gwT} for i in range(NCORES)], core_ids=core_ids)
    r4a = res.results

    allcnt = np.ascontiguousarray(np.concatenate([r["o_cnt"] for r in r4a], axis=0))
    nc4b = _get_nc("l1slots", build_l1_slots)
    in_maps = []
    for i in range(NCORES):
        in_maps.append({
            "allcnt": allcnt,
            "coremask": np.ascontiguousarray((np.arange(NCORES) < i)
                                             .astype(np.float32).reshape(-1, 1)),
            "tokidx": np.ascontiguousarray(np.arange(i * TPC, (i + 1) * TPC,
                                                     dtype=np.float32).reshape(-1, 1)),
            "oh": r4a[i]["o_oh"], "loc": r4a[i]["o_loc"],
            "pmax": r4a[i]["o_pmax"], "eidx": r4a[i]["o_eidx"],
        })
    res = run_bass_kernel_spmd(nc4b, in_maps, core_ids=core_ids)
    r4b = res.results

    idxparts = np.stack([r4b[i]["o_idx"].reshape(16, 2560) for i in range(NCORES)])
    xga = np.ascontiguousarray(np.concatenate([r["o_xln"] for r in r4a], axis=0))
    w1f = np.asarray(inputs["L1_ffn_w1"], dtype=np.float32)
    w2f = np.asarray(inputs["L1_ffn_w2"], dtype=np.float32)
    nc4c = _get_nc("l1exp", build_l1_experts)
    in_maps = []
    for i in range(NCORES):
        idx2T = np.ascontiguousarray(
            idxparts[:, 2 * i:2 * i + 2, :].reshape(NCORES, -1).T)
        in_maps.append({"idx2T": idx2T, "xga": xga,
                        "w1": np.ascontiguousarray(w1f[2 * i:2 * i + 2]),
                        "w2": np.ascontiguousarray(w2f[2 * i:2 * i + 2])})
    res = run_bass_kernel_spmd(nc4c, in_maps, core_ids=core_ids)
    ya = np.ascontiguousarray(np.concatenate([res.results[i]["o_y"]
                                              for i in range(NCORES)], axis=0))

    nc4d = _get_nc("l1comb", build_l1_combine)
    res = run_bass_kernel_spmd(
        nc4d, [{"x": x3[i], "ya": ya, "slot": r4b[i]["o_slot"], "w": r4b[i]["o_w"]}
               for i in range(NCORES)], core_ids=core_ids)
    out = np.concatenate([res.results[i]["out"] for i in range(NCORES)], axis=0)
    return out.reshape(B, S, D)


# version 07
# speedup vs baseline: 1.0026x; 1.0026x over previous
import numpy as np
import ml_dtypes
import concourse.bass as bass
import concourse.mybir as mybir
from concourse.bass_utils import run_bass_kernel_spmd

F32 = mybir.dt.float32
BF16 = mybir.dt.bfloat16
I32 = mybir.dt.int32
AF = mybir.ActivationFunctionType
ALU = mybir.AluOpType
AX = mybir.AxisListType
BF = ml_dtypes.bfloat16

NCORES = 8
B, S, D = 16, 2048, 1024
BPC = B // NCORES
TPC = BPC * S
NTB = S // 128
ND = D // 128
NHEADS, DK = 16, 64
ROPE_THETA = 10000.0
LN_EPS = 1e-5


class KB:

    def __init__(self, name):
        self.nc = bass.Bass("TRN2", target_bir_lowering=False, name=name,
                            detect_race_conditions=False)
        self.sem = {}
        self.cnt = {}
        for nm in ("sdma", "gdma", "pe", "dve", "act", "pool"):
            self.sem[nm] = self.nc.alloc_semaphore("s_" + nm)
            self.cnt[nm] = 0

    def sdma(self, out, in_):
        if self.cnt["sdma"]:
            self.nc.sync.wait_ge(self.sem["sdma"], self.cnt["sdma"])
        self.nc.sync.dma_start(out=out, in_=in_).then_inc(self.sem["sdma"], 16)
        self.cnt["sdma"] += 16

    def gdma(self, out, in_):
        if self.cnt["gdma"]:
            self.nc.gpsimd.wait_ge(self.sem["gdma"], self.cnt["gdma"])
        self.nc.gpsimd.dma_start(out=out, in_=in_).then_inc(self.sem["gdma"], 16)
        self.cnt["gdma"] += 16

    def tick(self, nm, instr, k=1):
        instr.then_inc(self.sem[nm], k)
        self.cnt[nm] += k

    def vop(self, f, *args, **kw):
        self.wait(self.nc.vector, "dve")
        ins = f(*args, **kw)
        self.tick("dve", ins)
        return ins

    def sop(self, f, *args, **kw):
        self.wait(self.nc.scalar, "act")
        ins = f(*args, **kw)
        self.tick("act", ins)
        return ins

    def pop(self, f, *args, **kw):
        self.wait(self.nc.gpsimd, "pool")
        ins = f(*args, **kw)
        self.tick("pool", ins)
        return ins

    def mark(self, nm):
        return (nm, self.cnt[nm])

    def wait_mark(self, eng, mk):
        nm, v = mk
        if v > 0:
            eng.wait_ge(self.sem[nm], v)

    def wait(self, eng, *names):
        for nm in names:
            if self.cnt[nm] > 0:
                eng.wait_ge(self.sem[nm], self.cnt[nm])

    def barrier(self):
        nc = self.nc
        for eng in (nc.sync, nc.gpsimd, nc.tensor, nc.vector, nc.scalar):
            self.wait(eng, "sdma", "gdma", "pe", "dve", "act", "pool")

    def finish(self):
        self.wait(self.nc.sync, "sdma", "gdma")


def _rope_tables():
    pos = np.arange(S, dtype=np.float64)
    expo = np.arange(0, DK, 2, dtype=np.float64) / DK
    inv_freq = ROPE_THETA ** expo
    ang = pos[None, :] / inv_freq[:, None]
    cos = np.repeat(np.cos(ang), 2, axis=0)
    sin = np.repeat(np.sin(ang), 2, axis=0)
    sgn = np.where(np.arange(DK) % 2 == 0, -1.0, 1.0)[:, None]
    cc = np.concatenate([cos, cos], axis=0).astype(BF)
    ss = np.concatenate([sin * sgn, sin * sgn], axis=0).astype(BF)
    r = np.zeros((DK, DK), dtype=BF)
    for j in range(0, DK, 2):
        r[j, j + 1] = 1.0
        r[j + 1, j] = 1.0
    r2 = np.zeros((128, 128), dtype=BF)
    r2[0:64, 0:64] = r
    r2[64:128, 64:128] = r
    return cc, ss, r2


class LNStats:

    def __init__(self, nc, tag, sb=None):
        if sb is None:
            sb = nc.alloc_sbuf_tensor
        self.m = sb(f"ln_m_{tag}", [128, 1], F32)
        self.q = sb(f"ln_q_{tag}", [128, 1], F32)
        self.m2 = sb(f"ln_m2_{tag}", [128, 1], F32)
        self.v = sb(f"ln_v_{tag}", [128, 1], F32)
        self.r = sb(f"ln_r_{tag}", [128, 1], F32)
        self.b = sb(f"ln_b_{tag}", [128, 1], F32)


def ln_tile(kb, xt, out_bf, sq, st, epscol):
    nc = kb.nc
    v, s = nc.vector, nc.scalar
    kb.vop(v.reduce_sum, st.m[:, :], xt, AX.X)
    kb.vop(v.tensor_scalar_mul, st.m[:, :], st.m[:, :], 1.0 / D)
    kb.vop(v.tensor_tensor, st.m2[:, :], st.m[:, :], st.m[:, :], ALU.mult)
    kb.sop(s.activation, sq, xt, AF.Square, accum_out=st.q[:, :])
    kb.wait(v, "act")
    kb.vop(v.tensor_scalar, st.v[:, :], st.q[:, :], 1.0 / D, st.m2[:, 0:1],
           ALU.mult, ALU.subtract)
    kb.wait(s, "dve")
    kb.sop(s.activation, st.v[:, :], st.v[:, :], AF.Ln, bias=epscol[:, 0:1])
    kb.sop(s.activation, st.r[:, :], st.v[:, :], AF.Exp, scale=-0.5)
    kb.wait(v, "act")
    kb.vop(v.tensor_scalar, st.b[:, :], st.m[:, :], st.r[:, 0:1], -1.0,
           ALU.mult, ALU.mult)
    kb.wait(s, "dve")
    kb.sop(s.activation, out_bf, xt, AF.Identity, bias=st.b[:, 0:1], scale=st.r[:, 0:1])



def _scoped_allocs(nc, es, px):
    def sb(name, shape, dtype):
        return es.enter_context(nc.sbuf_tensor(px + name, shape, dtype))
    def ps(name, shape, dtype=F32):
        return es.enter_context(nc.psum_tensor(px + name, shape, dtype))
    return sb, ps


def build_l0_attn(debug_taps=False):
    import contextlib
    kb = KB("l0attn")
    nc = kb.nc
    x = nc.dram_tensor("x", [TPC, D], F32, kind="ExternalInput").ap()
    wqkvT = nc.dram_tensor("wqkvT", [D, 3 * D], F32, kind="ExternalInput").ap()
    woT = nc.dram_tensor("woT", [D, D], F32, kind="ExternalInput").ap()
    x1 = nc.dram_tensor("x1", [TPC, D], F32, kind="ExternalOutput").ap()
    with contextlib.ExitStack() as es:
        emit_l0_attn(kb, es, x, x1, wqkvT, woT, debug_taps=debug_taps)
    kb.finish()
    return kb.nc


def emit_l0_attn(kb, es, x, x1, wqkvT, woT, px="a_", debug_taps=False):
    nc = kb.nc
    cc_np, ss_np, r2_np = _rope_tables()
    consts = {
        "cc": cc_np, "ss": ss_np, "r2": r2_np,
        "ident": np.eye(128, dtype=BF),
        "upmask": (np.arange(128)[:, None] <= np.arange(128)[None, :]).astype(BF),
        "ones": np.ones((128, 128), dtype=BF),
    }
    cap = {k: nc.inline_tensor(v, px + "c_" + k).ap() for k, v in consts.items()}

    sb, ps = _scoped_allocs(nc, es, px)
    wq = [sb(f"wq{i}", [128, 3 * D], BF16) for i in range(ND)]
    wo = [sb(f"wo{i}", [128, D], BF16) for i in range(ND)]
    ccs = sb("ccs", [128, S], BF16)
    sss = sb("sss", [128, S], BF16)
    r2s = sb("r2s", [128, 128], BF16)
    ids = sb("ids", [128, 128], BF16)
    upm = sb("upm", [128, 128], BF16)
    ones = sb("ones1", [128, 128], BF16)
    xt = [sb(f"xt{i}", [128, D], F32) for i in range(2)]
    sq = sb("sq", [128, D], F32)
    xlnb = sb("xlnb", [128, D], BF16)
    xlnT = [sb(f"xlnT{i}", [128, S], BF16) for i in range(ND)]
    vaug = [sb(f"vaug{i}", [128, NHEADS * 65], BF16) for i in range(NTB)]
    qk_raw = sb("qk_raw", [128, S], BF16)
    q2 = sb("q2", [128, S], BF16)
    k2 = sb("k2", [128, S], BF16)
    rt = sb("rt", [128, 512], BF16)
    rt2 = sb("rt2", [128, 512], BF16)
    pt = sb("pt", [128, 512], BF16)
    rb = sb("rb", [128, 512], BF16)
    aoutT = [sb(f"aoutT{i}", [128, S], BF16) for i in range(ND)]
    wsum = sb("wsum", [128, D], F32)
    epscol = sb("epscol", [128, 1], F32)
    st = LNStats(nc, "a", sb)

    p_tr = ps("p_tr", [128, 512], BF16)
    p_qk = ps("p_qk", [128, 512], F32)
    p_rot = ps("p_rot", [128, 512], F32)
    p_s = ps("p_s", [128, 512], F32)
    p_o = ps("p_o", [128, 512], F32)
    p_b = ps("p_b", [128, 512], F32)
    p_w0 = ps("p_w0", [128, 512], F32)
    p_w1 = ps("p_w1", [128, 512], F32)

    kb.sdma(ccs[:, :], cap["cc"][:, :])
    kb.sdma(sss[:, :], cap["ss"][:, :])
    kb.sdma(r2s[:, :], cap["r2"][:, :])
    kb.sdma(ids[:, :], cap["ident"][:, :])
    kb.sdma(upm[:, :], cap["upmask"][:, :])
    kb.sdma(ones[:, :], cap["ones"][:, :])
    for i in range(ND):
        kb.gdma(wq[i][:, :], wqkvT[128 * i:128 * (i + 1), :])
        kb.gdma(wo[i][:, :], woT[128 * i:128 * (i + 1), :])
    kb.vop(nc.vector.memset, epscol[:, :], LN_EPS)
    kb.barrier()

    t, v, s = nc.tensor, nc.vector, nc.scalar
    for b in range(BPC):
        tok0 = b * S
        ln_done = [kb.mark("act"), kb.mark("act")]
        tr_evac = kb.mark("dve")
        v_evac = kb.mark("act")
        for tt in range(NTB):
            xb_ = xt[tt % 2]
            kb.wait_mark(nc.sync, ln_done[tt % 2])
            kb.sdma(xb_[:, :], x[tok0 + 128 * tt: tok0 + 128 * (tt + 1), :])
            kb.wait(v, "sdma")
            kb.wait(s, "sdma")
            kb.wait(s, "pe")
            ln_tile(kb, xb_[:, :], xlnb[:, :], sq[:, :], st, epscol)
            ln_done[tt % 2] = kb.mark("act")
            kb.wait(t, "act")
            for g in range(2):
                kb.wait_mark(t, tr_evac)
                for k_ in range(4):
                    dt = 4 * g + k_
                    ins = t.transpose(p_tr[:, 128 * k_:128 * (k_ + 1)],
                                      xlnb[:, 128 * dt:128 * (dt + 1)], ids[:, :])
                    if k_ == 3:
                        kb.tick("pe", ins)
                kb.wait(v, "pe")
                for k_ in range(4):
                    dt = 4 * g + k_
                    kb.vop(v.tensor_copy, xlnT[dt][:, 128 * tt:128 * (tt + 1)],
                           p_tr[:, 128 * k_:128 * (k_ + 1)])
                tr_evac = kb.mark("dve")
            kb.wait(t, "dve")
            for c in range(2):
                kb.wait_mark(t, v_evac)
                for dt in range(ND):
                    ins = t.matmul(p_qk[:, :], xlnT[dt][:, 128 * tt:128 * (tt + 1)],
                                   wq[dt][:, 2 * D + 512 * c: 2 * D + 512 * (c + 1)],
                                   start=(dt == 0), stop=(dt == ND - 1))
                kb.tick("pe", ins)
                kb.wait(s, "pe")
                dst = vaug[tt][:, :].rearrange("p (h w) -> p h w", w=65)[:, 8 * c:8 * (c + 1), 0:64]
                kb.sop(s.activation, dst, p_qk[:, :].rearrange("p (h w) -> p h w", w=64),
                       AF.Copy)
                v_evac = kb.mark("act")
            kb.wait(v, "act")
            kb.vop(v.memset, vaug[tt][:, :].rearrange("p (h w) -> p h w", w=65)[:, :, 64:65], 1.0)

        kb.barrier()

        po_used = kb.mark("dve")
        pb_used = kb.mark("act")
        qk_evac = kb.mark("act")
        rot_used = kb.mark("dve")
        for hp in range(NHEADS // 2):
            for dst, col0 in ((q2, 128 * hp), (k2, D + 128 * hp)):
                for jq in range(4):
                    kb.wait_mark(t, qk_evac)
                    for dt in range(ND):
                        ins = t.matmul(p_qk[:, :], wq[dt][:, col0:col0 + 128],
                                       xlnT[dt][:, 512 * jq:512 * (jq + 1)],
                                       start=(dt == 0), stop=(dt == ND - 1))
                    kb.tick("pe", ins)
                    kb.wait(s, "pe")
                    kb.wait(s, "dve")
                    kb.sop(s.activation, qk_raw[:, 512 * jq:512 * (jq + 1)],
                           p_qk[:, :], AF.Copy)
                    qk_evac = kb.mark("act")
                    kb.wait(t, "act")
                    kb.wait_mark(t, rot_used)
                    kb.tick("pe", t.matmul(p_rot[:, :], r2s[:, :],
                                           qk_raw[:, 512 * jq:512 * (jq + 1)],
                                           start=True, stop=True))
                    kb.wait(v, "pe")
                    kb.wait(v, "act")
                    sl = slice(512 * jq, 512 * (jq + 1))
                    kb.vop(v.tensor_tensor, rt[:, :], p_rot[:, :], sss[:, sl], ALU.mult)
                    kb.vop(v.tensor_tensor, rt2[:, :], qk_raw[:, sl], ccs[:, sl], ALU.mult)
                    kb.vop(v.tensor_add, dst[:, sl], rt[:, :], rt2[:, :])
                    rot_used = kb.mark("dve")

            for h01 in range(2):
                h = 2 * hp + h01
                base = 64 * h01
                for jq in range(4):
                    nk = 4 * jq + 4
                    for ki in range(nk):
                        kb.wait(t, "dve")
                        kb.wait(t, "act")
                        kb.tick("pe", t.matmul(p_s[:, :],
                                               k2[base:base + 64, 128 * ki:128 * (ki + 1)],
                                               q2[base:base + 64, 512 * jq:512 * (jq + 1)],
                                               start=True, stop=True))
                        kb.wait(s, "pe")
                        kb.sop(s.activation, pt[:, :], p_s[:, :], AF.Exp, scale=0.125)
                        ld = ki - 4 * jq
                        if ld >= 0:
                            kb.wait(v, "act")
                            if ld > 0:
                                kb.vop(v.memset, pt[:, 0:128 * ld], 0.0)
                            kb.vop(v.tensor_tensor, pt[:, 128 * ld:128 * (ld + 1)],
                                   pt[:, 128 * ld:128 * (ld + 1)],
                                   upm[:, :], ALU.mult)
                            kb.wait(t, "dve")
                        else:
                            kb.wait(t, "act")
                        if ki == 0:
                            kb.wait_mark(t, po_used)
                        ins = t.matmul(p_o[0:65, :], vaug[ki][:, 65 * h:65 * h + 65],
                                       pt[:, :], start=(ki == 0), stop=(ki == nk - 1))
                        if ki == nk - 1:
                            kb.tick("pe", ins)
                    kb.wait(v, "pe")
                    with nc.allow_low_precision(reason="bf16 softmax recip within tolerance"):
                        kb.vop(v.reciprocal, rb[64:65, 0:512], p_o[64:65, :])
                    kb.wait(t, "dve")
                    kb.wait_mark(t, pb_used)
                    kb.tick("pe", t.matmul(p_b[0:64, :], ones[64:65, 0:64],
                                           rb[64:65, 0:512], start=True, stop=True))
                    kb.wait(s, "pe")
                    kb.wait_mark(s, po_used)
                    kb.sop(s.activation, rb[0:64, 0:512], p_b[0:64, :], AF.Copy)
                    pb_used = kb.mark("act")
                    kb.wait(v, "act")
                    kb.vop(v.tensor_tensor,
                           aoutT[h // 2][64 * (h % 2):64 * (h % 2) + 64, 512 * jq:512 * (jq + 1)],
                           p_o[0:64, :], rb[0:64, 0:512], ALU.mult)
                    po_used = kb.mark("dve")

        kb.barrier()

        res_done = [kb.mark("dve"), kb.mark("dve")]
        for tt in range(NTB):
            xb_ = xt[tt % 2]
            kb.wait_mark(nc.sync, res_done[tt % 2])
            kb.sdma(xb_[:, :], x[tok0 + 128 * tt: tok0 + 128 * (tt + 1), :])
            kb.wait(t, "dve")
            for c, pw in ((0, p_w0), (1, p_w1)):
                for ft in range(ND):
                    ins = t.matmul(pw[:, :], aoutT[ft][:, 128 * tt:128 * (tt + 1)],
                                   wo[ft][:, 512 * c:512 * (c + 1)],
                                   start=(ft == 0), stop=(ft == ND - 1))
                kb.tick("pe", ins)
            kb.wait(v, "pe")
            kb.wait(v, "sdma")
            kb.vop(v.tensor_add, wsum[:, 0:512], xb_[:, 0:512], p_w0[:, :])
            kb.vop(v.tensor_add, wsum[:, 512:1024], xb_[:, 512:1024], p_w1[:, :])
            res_done[tt % 2] = kb.mark("dve")
            kb.wait(nc.sync, "dve")
            kb.sdma(x1[tok0 + 128 * tt: tok0 + 128 * (tt + 1), :], wsum[:, :])
        kb.barrier()

    if debug_taps:
        taps = {
            "dbg_xlnT0": xlnT[0], "dbg_q2": q2, "dbg_k2": k2,
            "dbg_vaug0": vaug[0], "dbg_aoutT0": aoutT[0],
        }
        for nm, tile in taps.items():
            shp = list(tile.shape) if hasattr(tile, "shape") else [128, S]
            o = nc.dram_tensor(nm, [128, shp[1]], BF16, kind="ExternalOutput").ap()
            kb.sdma(o[:, :], tile[:, :])
        kb.barrier()



def emit_ln_transpose(kb, src_tiles, xlnb, sq, st, epscol, xlnT, ids, p_tr, tr_state):
    nc = kb.nc
    t, v, s = nc.tensor, nc.vector, nc.scalar
    for tt, xb_ in enumerate(src_tiles):
        kb.wait(s, "pe")
        ln_tile(kb, xb_[:, :], xlnb[:, :], sq[:, :], st, epscol)
        kb.wait(t, "act")
        for g in range(2):
            kb.wait_mark(t, tr_state["tr_evac"])
            for k_ in range(4):
                dt = 4 * g + k_
                ins = t.transpose(p_tr[:, 128 * k_:128 * (k_ + 1)],
                                  xlnb[:, 128 * dt:128 * (dt + 1)], ids[:, :])
                if k_ == 3:
                    kb.tick("pe", ins)
            kb.wait(v, "pe")
            for k_ in range(4):
                dt = 4 * g + k_
                kb.vop(v.tensor_copy, xlnT[dt][:, 128 * tt:128 * (tt + 1)],
                       p_tr[:, 128 * k_:128 * (k_ + 1)])
            tr_state["tr_evac"] = kb.mark("dve")


def build_l0_ffn(debug_taps=False):
    import contextlib
    kb = KB("l0ffn")
    nc = kb.nc
    E, FF = 8, 4096
    x = nc.dram_tensor("x", [TPC, D], F32, kind="ExternalInput").ap()
    wr = nc.dram_tensor("wr", [D, E], F32, kind="ExternalInput").ap()
    br = nc.dram_tensor("br", [1, E], F32, kind="ExternalInput").ap()
    w1 = nc.dram_tensor("w1", [E, D, FF], F32, kind="ExternalInput").ap()
    b1r = nc.dram_tensor("b1r", [E, 128, FF // 128], F32, kind="ExternalInput").ap()
    w2 = nc.dram_tensor("w2", [E, FF, D], F32, kind="ExternalInput").ap()
    b2 = nc.dram_tensor("b2", [E, D], F32, kind="ExternalInput").ap()
    x2 = nc.dram_tensor("x2", [TPC, D], F32, kind="ExternalOutput").ap()
    with contextlib.ExitStack() as es:
        emit_l0_ffn(kb, es, x, x2, wr, br, w1, b1r, w2, b2, debug_taps=debug_taps)
    kb.finish()
    return kb.nc


def emit_l0_ffn(kb, es, x, x2, wr, br, w1, b1r, w2, b2, px="f_", debug_taps=False):
    nc = kb.nc
    E, FF = 8, 4096
    tie_np = (np.arange(8)[None, :] <= np.arange(8)[:, None]).astype(np.float32)
    consts = {
        "ident": np.eye(128, dtype=BF),
        "onesrow": np.ones((1, 128), dtype=BF),
        "tie": np.broadcast_to(tie_np.reshape(1, 64), (128, 64)).astype(np.float32),
    }
    cap = {k: nc.inline_tensor(np.ascontiguousarray(v), px + "c_" + k).ap() for k, v in consts.items()}

    sb, ps = _scoped_allocs(nc, es, px)
    ids = sb("ids", [128, 128], BF16)
    onesr = sb("onesr", [1, 128], BF16)
    ties = sb("ties", [128, 64], F32)
    wrs = [sb(f"wrs{i}", [128, E], BF16) for i in range(ND)]
    brs = sb("brs", [1, E], BF16)
    b2s = sb("b2s", [E, D], BF16)
    b1s = sb("b1s", [128, FF // 128], F32)
    acc = [sb(f"acc{i}", [128, D], F32) for i in range(NTB)]
    sq = sb("sq", [128, D], F32)
    xlnb = sb("xlnb", [128, D], BF16)
    xlnT = [sb(f"xlnT{i}", [128, S], BF16) for i in range(ND)]
    hT = [sb(f"hT{i}", [128, S], BF16) for i in range(8)]
    w1q = [sb(f"w1q{i}", [128, 1024], BF16) for i in range(ND)]
    w2q = [sb(f"w2q{i}", [128, D], BF16) for i in range(8)]
    lg = sb("lg", [128, E], F32)
    ga = [sb(f"ga{i}", [128, E], F32) for i in range(NTB)]
    gab = sb("gab", [128, E], BF16)
    gT = sb("gT", [E, S], BF16)
    mg = sb("mg", [128, 64], F32)
    eqg = sb("eqg", [128, 64], F32)
    cum = sb("cum", [128, 64], F32)
    kp = sb("kp", [128, E], F32)
    kp2 = sb("kp2", [128, E], F32)
    gmax = sb("gmax", [128, 1], F32)
    gsum = sb("gsum", [128, 1], F32)
    epscol = sb("epscol", [128, 1], F32)
    st = LNStats(nc, "f", sb)

    p_tr = ps("p_tr", [128, 512], BF16)
    p_g = ps("p_g", [128, E], F32)
    p_h0 = ps("p_h0", [128, 512], F32)
    p_h1 = ps("p_h1", [128, 512], F32)
    p_y0 = ps("p_y0", [128, 512], F32)
    p_y1 = ps("p_y1", [128, 512], F32)

    t, v, s = nc.tensor, nc.vector, nc.scalar
    kb.sdma(ids[:, :], cap["ident"][:, :])
    kb.sdma(onesr[:, :], cap["onesrow"][:, :])
    kb.sdma(ties[:, :], cap["tie"][:, :])
    kb.gdma(brs[:, :], br[:, :])
    kb.gdma(b2s[:, :], b2[:, :])
    for i in range(ND):
        kb.gdma(wrs[i][:, :], wr[128 * i:128 * (i + 1), :])
    kb.vop(v.memset, epscol[:, :], LN_EPS)
    kb.barrier()

    for b in range(BPC):
        tok0 = b * S
        for tt in range(NTB):
            kb.sdma(acc[tt][:, :], x[tok0 + 128 * tt: tok0 + 128 * (tt + 1), :])
        kb.barrier()
        tr_state = {"tr_evac": kb.mark("dve")}
        emit_ln_transpose(kb, [a for a in acc], xlnb, sq, st, epscol, xlnT, ids, p_tr, tr_state)
        kb.barrier()

        gt_evac = kb.mark("dve")
        for tt in range(NTB):
            kb.wait(t, "dve")
            for dt in range(ND):
                ins = t.matmul(p_g[:, :], xlnT[dt][:, 128 * tt:128 * (tt + 1)],
                               wrs[dt][:, :], start=(dt == 0), stop=False)
            kb.tick("pe", t.matmul(p_g[:, :], onesr[0:1, :], brs[0:1, :],
                                   start=False, stop=True))
            kb.wait(v, "pe")
            kb.vop(v.tensor_copy, lg[:, :], p_g[:, :])
            kb.vop(v.reduce_max, gmax[:, :], lg[:, :], AX.X, negate=True)
            kb.wait(s, "dve")
            kb.sop(s.activation, lg[:, :], lg[:, :], AF.Exp, bias=gmax[:, 0:1])
            kb.wait(v, "act")
            kb.vop(v.reduce_sum, gsum[:, :], lg[:, :], AX.X)
            kb.vop(v.reciprocal, gsum[:, :], gsum[:, :])
            kb.vop(v.tensor_scalar_mul, lg[:, :], lg[:, :], gsum[:, 0:1])
            A = lg[:, :].unsqueeze(1).broadcast_to((128, 8, 8))
            Bm = lg[:, :].unsqueeze(2).broadcast_to((128, 8, 8))
            mg3 = mg[:, :].rearrange("p (e f) -> p e f", f=8)
            eq3 = eqg[:, :].rearrange("p (e f) -> p e f", f=8)
            kb.vop(v.tensor_tensor, mg3, A, Bm, ALU.is_gt)
            kb.vop(v.tensor_tensor, eq3, A, Bm, ALU.is_equal)
            kb.vop(v.tensor_tensor, eqg[:, :], eqg[:, :], ties[:, :], ALU.mult)
            kb.vop(v.tensor_add, mg[:, :], mg[:, :], eqg[:, :])
            kb.vop(v.tensor_tensor, mg3, mg3, A, ALU.mult)
            kb.vop(v.tensor_reduce, cum[:, 0:8], mg3, AX.X, ALU.add)
            kb.vop(v.tensor_single_scalar, kp[:, :], cum[:, 0:8], 0.8, ALU.is_lt)
            kb.vop(v.tensor_tensor, kp2[:, :], cum[:, 0:8], lg[:, :], ALU.is_equal)
            kb.vop(v.tensor_max, kp[:, :], kp[:, :], kp2[:, :])
            kb.vop(v.tensor_tensor, ga[tt][:, :], lg[:, :], kp[:, :], ALU.mult)
            with nc.allow_low_precision(reason="bf16 gate copy for transpose"):
                kb.vop(v.tensor_copy, gab[:, :], ga[tt][:, :])
            kb.wait(t, "dve")
            kb.wait_mark(t, gt_evac)
            kb.tick("pe", t.transpose(p_tr[0:8, 0:128], gab[:, :], ids[:, :]))
            kb.wait(v, "pe")
            kb.vop(v.tensor_copy, gT[0:8, 128 * tt:128 * (tt + 1)], p_tr[0:8, 0:128])
            gt_evac = kb.mark("dve")
        kb.barrier()

        for tt in range(NTB):
            kb.wait(t, "dve")
            for c, py in ((0, p_y0), (1, p_y1)):
                kb.tick("pe", t.matmul(py[:, :], gT[0:8, 128 * tt:128 * (tt + 1)],
                                       b2s[:, 512 * c:512 * (c + 1)], start=True, stop=True))
            kb.wait(v, "pe")
            kb.vop(v.tensor_add, acc[tt][:, 0:512], acc[tt][:, 0:512], p_y0[:, :])
            kb.vop(v.tensor_add, acc[tt][:, 512:1024], acc[tt][:, 512:1024], p_y1[:, :])
        kb.barrier()

        for e in range(E):
            kb.sdma(b1s[:, :], b1r[e, :, :])
            for qq in range(4):
                f0 = 1024 * qq
                for i in range(ND):
                    kb.gdma(w1q[i][:, :], w1[e, 128 * i:128 * (i + 1), f0:f0 + 1024])
                for i in range(8):
                    kb.gdma(w2q[i][:, :], w2[e, f0 + 128 * i:f0 + 128 * (i + 1), :])
                kb.barrier()
                hb = [p_h0, p_h1]
                h_evac = [kb.mark("act"), kb.mark("act")]
                for ft in range(8):
                    for tc in range(4):
                        ph = hb[(ft * 4 + tc) % 2]
                        kb.wait_mark(t, h_evac[(ft * 4 + tc) % 2])
                        for dt in range(ND):
                            ins = t.matmul(ph[:, :], w1q[dt][:, 128 * ft:128 * (ft + 1)],
                                           xlnT[dt][:, 512 * tc:512 * (tc + 1)],
                                           start=(dt == 0), stop=(dt == ND - 1))
                        kb.tick("pe", ins)
                        kb.wait(s, "pe")
                        fcol = 8 * qq + ft
                        kb.sop(s.activation, hT[ft][:, 512 * tc:512 * (tc + 1)],
                               ph[:, :], AF.Relu, bias=b1s[:, fcol:fcol + 1])
                        h_evac[(ft * 4 + tc) % 2] = kb.mark("act")
                kb.barrier()
                y_done = kb.mark("dve")
                for tt in range(NTB):
                    kb.wait_mark(t, y_done)
                    for c, py in ((0, p_y0), (1, p_y1)):
                        for fl in range(8):
                            ins = t.matmul(py[:, :], hT[fl][:, 128 * tt:128 * (tt + 1)],
                                           w2q[fl][:, 512 * c:512 * (c + 1)],
                                           start=(fl == 0), stop=(fl == 7))
                        kb.tick("pe", ins)
                    kb.wait(v, "pe")
                    kb.vop(v.scalar_tensor_tensor, acc[tt][:, 0:512], p_y0[:, :],
                           ga[tt][:, e:e + 1], acc[tt][:, 0:512], ALU.mult, ALU.add)
                    kb.vop(v.scalar_tensor_tensor, acc[tt][:, 512:1024], p_y1[:, :],
                           ga[tt][:, e:e + 1], acc[tt][:, 512:1024], ALU.mult, ALU.add)
                    y_done = kb.mark("dve")
                kb.barrier()
        for tt in range(NTB):
            kb.wait(nc.sync, "dve")
            kb.sdma(x2[tok0 + 128 * tt: tok0 + 128 * (tt + 1), :], acc[tt][:, :])
        kb.barrier()



def emit_ln_transpose_stream(kb, x_ap, tok0, xt2, xlnb, sq, st, epscol, xlnT, ids,
                             p_tr, tr_state):
    nc = kb.nc
    t, v, s = nc.tensor, nc.vector, nc.scalar
    ln_done = [kb.mark("act"), kb.mark("act")]
    for tt in range(NTB):
        xb_ = xt2[tt % 2]
        kb.wait_mark(nc.sync, ln_done[tt % 2])
        kb.sdma(xb_[:, :], x_ap[tok0 + 128 * tt: tok0 + 128 * (tt + 1), :])
        kb.wait(v, "sdma")
        kb.wait(s, "sdma")
        kb.wait(s, "pe")
        ln_tile(kb, xb_[:, :], xlnb[:, :], sq[:, :], st, epscol)
        ln_done[tt % 2] = kb.mark("act")
        kb.wait(t, "act")
        for g in range(2):
            kb.wait_mark(t, tr_state["tr_evac"])
            for k_ in range(4):
                dt = 4 * g + k_
                ins = t.transpose(p_tr[:, 128 * k_:128 * (k_ + 1)],
                                  xlnb[:, 128 * dt:128 * (dt + 1)], ids[:, :])
                if k_ == 3:
                    kb.tick("pe", ins)
            kb.wait(v, "pe")
            for k_ in range(4):
                dt = 4 * g + k_
                kb.vop(v.tensor_copy, xlnT[dt][:, 128 * tt:128 * (tt + 1)],
                       p_tr[:, 128 * k_:128 * (k_ + 1)])
            tr_state["tr_evac"] = kb.mark("dve")


def build_l1_attn(debug_taps=False, stop=None):
    import contextlib
    kb = KB("mamba")
    nc = kb.nc
    x = nc.dram_tensor("x", [TPC, D], F32, kind="ExternalInput").ap()
    winT = nc.dram_tensor("winT", [D, 4384], F32, kind="ExternalInput").ap()
    convw = nc.dram_tensor("convw", [2304, 4], F32, kind="ExternalInput").ap()
    convb = nc.dram_tensor("convb", [2304, 1], F32, kind="ExternalInput").ap()
    dtb = nc.dram_tensor("dtb", [32, 1], F32, kind="ExternalInput").ap()
    alog = nc.dram_tensor("alog", [32, 1], F32, kind="ExternalInput").ap()
    dpar = nc.dram_tensor("dpar", [32, 1], F32, kind="ExternalInput").ap()
    nwr = nc.dram_tensor("nwr", [128, 16], F32, kind="ExternalInput").ap()
    woutT = nc.dram_tensor("woutT", [2 * D, D], F32, kind="ExternalInput").ap()
    x3 = nc.dram_tensor("x3", [TPC, D], F32, kind="ExternalOutput").ap()
    with contextlib.ExitStack() as es:
        emit_l1_attn(kb, es, x, x3, winT, convw, convb, dtb, alog, dpar, nwr, woutT,
                     debug_taps=debug_taps, stop=stop)
    kb.finish()
    return kb.nc


def emit_l1_attn(kb, es, x, x3, winT, convw, convb, dtb, alog, dpar, nwr, woutT,
                 px="m_", debug_taps=False, stop=None):
    nc = kb.nc
    DI = 2 * D
    NH, HP, NST = 32, 64, 128
    CONVD = DI + 2 * NST
    CH = 64
    NCH = S // CH
    PAD = 3
    seg = np.ones((NH, S), dtype=BF); seg[:, ::CH] = 0.0
    consts = {
        "ident": np.eye(128, dtype=BF),
        "identf": np.eye(128, dtype=np.float32),
        "onesrow": np.ones((1, 128), dtype=BF),
        "onesrowf": np.ones((1, 128), dtype=np.float32),
        "segmask": seg,
        "bigmask": (1e9 * (np.arange(64)[:, None] > np.arange(64)[None, :])).astype(np.float32),
        "upm64": (np.arange(64)[:, None] <= np.arange(64)[None, :]).astype(np.float32),
    }
    cap = {k: nc.inline_tensor(np.ascontiguousarray(v), px + "c_" + k).ap() for k, v in consts.items()}

    sb, ps = _scoped_allocs(nc, es, px)
    ids = sb("ids", [128, 128], BF16)
    idf = sb("idf", [128, 128], F32)
    onesr = sb("onesr", [1, 128], BF16)
    onesrf = sb("onesrf", [1, 128], F32)
    segs = sb("segs", [NH, S], BF16)
    bigm = sb("bigm", [64, 64], F32)
    upm64 = sb("upm64", [64, 64], F32)
    xt2 = [sb(f"xt{i}", [128, D], F32) for i in range(2)]
    sq = sb("sq", [128, D], F32)
    xlnb = sb("xlnb", [128, D], BF16)
    xlnT = [sb(f"xlnT{i}", [128, S], BF16) for i in range(ND)]
    winc = [sb(f"winc{i}", [128, 128], BF16) for i in range(ND)]
    xc = [sb(f"xc{i}", [128, PAD + S], BF16) for i in range(18)]
    cw = [sb(f"cw{i}", [128, 4], F32) for i in range(18)]
    cb = [sb(f"cb{i}", [128, 1], F32) for i in range(18)]
    cs1 = sb("cs1", [128, S], F32)
    dtv = sb("dtv", [NH, S], F32)
    dacs = sb("dacs", [NH, S], F32)
    dtbc = sb("dtbc", [NH, 1], F32)
    acol = sb("acol", [NH, 1], F32)
    onecol = sb("onecol", [NH, 1], F32)
    dpb = sb("dpb", [64, NH], F32)
    dacs_tm = sb("dacs_tm", [128, 512], F32)
    dtv_tm = sb("dtv_tm", [128, 512], F32)
    dtq = sb("dtq", [64, NH], F32)
    dc = sb("dc", [64, NH], F32)
    ea = sb("ea", [64, NH], F32)
    Gm = sb("Gm", [64, 64], BF16)
    dpx = sb("dpx", [64, DI], BF16)
    btm = sb("btm", [64, 128], BF16)
    cdrow = sb("cdrow", [1, NH], F32)
    yscr = sb("yscr", [64, 512], F32)
    zs = sb("zs", [128, S], BF16)
    rrowb = sb("rrowb", [1, S], BF16)
    rb128 = sb("rb128", [128, S], BF16)
    nws = sb("nws", [128, DI // 128], F32)
    wout = [sb(f"wout{i}", [128, D], BF16) for i in range(16)]
    epscol = sb("epscol", [128, 1], F32)
    onescol = sb("onescol", [128, 1], BF16)
    st = LNStats(nc, "m", sb)
    a0 = nc.lookup_mloc(xlnT[0]).addr
    at = lambda name, shape, dtype, offset: nc.alloc_sbuf_tensor_at(
        px + name, shape, dtype, offset=offset)
    A2f = at("A2f", [64, DI], F32, offset=a0)
    statef = at("statef", [128, DI], F32, offset=a0 + 8192)
    stateb = at("stateb", [128, DI], BF16, offset=a0 + 16384)
    x_tm = at("x_tm", [64, DI], BF16, offset=a0 + 20480)
    dtx = at("dtx", [64, DI], BF16, offset=a0 + 24576)
    MT = at("MT", [64, DI], BF16, offset=a0 + 28672)
    dtxd = x_tm
    zaddr = nc.lookup_mloc(zs).addr
    yh = at("yh", [64, DI], BF16, offset=zaddr)
    cdb64 = cs1
    dtm0 = nc.lookup_mloc(dacs_tm).addr
    assert nc.lookup_mloc(dtv_tm).addr == dtm0 + 2048
    wsum = at("wsum", [128, D], F32, offset=dtm0)
    zdram = nc.dram_tensor(px + "zdram", [DI, S], BF16, kind="Internal").ap()

    p_trb = ps("p_trb", [128, 512], BF16)
    p_trf = ps("p_trf", [128, 512], F32)
    p_ya = [ps(f"p_ya{i}", [128, 512], F32) for i in range(2)]
    p_yo = [ps(f"p_yo{i}", [128, 512], F32) for i in range(2)]
    p_st = [ps(f"p_st{i}", [128, 512], F32) for i in range(2)]

    t, v, s = nc.tensor, nc.vector, nc.scalar
    kb.sdma(ids[:, :], cap["ident"][:, :])
    kb.sdma(idf[:, :], cap["identf"][:, :])
    kb.sdma(onesr[:, :], cap["onesrow"][:, :])
    kb.sdma(onesrf[:, :], cap["onesrowf"][:, :])
    kb.sdma(segs[:, :], cap["segmask"][:, :])
    kb.sdma(bigm[:, :], cap["bigmask"][:, :])
    kb.sdma(upm64[:, :], cap["upm64"][:, :])
    kb.sdma(nws[:, :], nwr[:, :])
    for j in range(18):
        kb.sdma(cw[j][:, :], convw[128 * j:128 * (j + 1), :])
        kb.sdma(cb[j][:, :], convb[128 * j:128 * (j + 1), :])
    kb.sdma(dtbc[:, :], dtb[:, :])
    kb.sdma(acol[:, :], alog[:, :])
    for j in range(16):
        kb.gdma(wout[j][:, :], woutT[128 * j:128 * (j + 1), :])
    kb.vop(v.memset, epscol[:, :], LN_EPS)
    kb.vop(v.memset, onecol[:, :], 1.0)
    with nc.allow_low_precision(reason="ones"):
        kb.vop(v.memset, onescol[:, :], 1.0)
    kb.barrier()
    kb.sop(s.activation, acol[:, :], acol[:, :], AF.Exp)
    kb.wait(v, "act")
    kb.vop(v.tensor_scalar_mul, acol[:, :], acol[:, :], -1.0)
    kb.sdma(dtq[0:NH, 0:1], dpar[:, :])
    kb.barrier()
    kb.tick("pe", t.transpose(p_trf[0:1, 0:NH], dtq[0:NH, 0:1], idf[0:NH, 0:NH]))
    kb.wait(v, "pe")
    kb.vop(v.tensor_copy, cdrow[0:1, 0:NH], p_trf[0:1, 0:NH])
    kb.wait(t, "dve")
    kb.tick("pe", t.matmul(p_trf[0:64, 0:NH], onesrf[0:1, 0:64], cdrow[0:1, 0:NH],
                           start=True, stop=True))
    kb.wait(v, "pe")
    kb.vop(v.tensor_copy, dpb[:, :], p_trf[0:64, 0:NH])
    kb.barrier()

    def load_win_cols(c0, n):
        kb.wait(nc.gpsimd, "pe")
        for i in range(ND):
            kb.gdma(winc[i][:, 0:n], winT[128 * i:128 * (i + 1), c0:c0 + n])

    for b in range(BPC):
        tok0 = b * S
        tr_state = {"tr_evac": kb.mark("dve")}
        emit_ln_transpose_stream(kb, x, tok0, xt2, xlnb, sq, st, epscol, xlnT, ids,
                                 p_trb, tr_state)
        kb.barrier()
        if stop == "ln":
            break

        for j in range(18):
            kb.vop(v.memset, xc[j][:, 0:PAD], 0.0)
            load_win_cols(2 * D + 128 * j, 128)
            kb.barrier()
            ev = kb.mark("act")
            for tc in range(4):
                pq = p_st[tc % 2]
                kb.wait_mark(t, ev)
                for dt_ in range(ND):
                    ins = t.matmul(pq[:, :], winc[dt_][:, 0:128],
                                   xlnT[dt_][:, 512 * tc:512 * (tc + 1)],
                                   start=(dt_ == 0), stop=(dt_ == ND - 1))
                kb.tick("pe", ins)
                kb.wait(s, "pe")
                kb.sop(s.activation, xc[j][:, PAD + 512 * tc:PAD + 512 * (tc + 1)],
                       pq[:, :], AF.Copy)
                ev = kb.mark("act")
            kb.barrier()
        for j in range(16):
            load_win_cols(128 * j, 128)
            kb.barrier()
            ev = kb.mark("act")
            for tc in range(4):
                pq = p_ya[tc % 2]
                kb.wait_mark(t, ev)
                for dt_ in range(ND):
                    ins = t.matmul(pq[:, :], winc[dt_][:, 0:128],
                                   xlnT[dt_][:, 512 * tc:512 * (tc + 1)],
                                   start=(dt_ == 0), stop=(dt_ == ND - 1))
                kb.tick("pe", ins)
                kb.wait(s, "pe")
                kb.wait(s, "sdma")
                kb.sop(s.activation, zs[:, 512 * tc:512 * (tc + 1)], pq[:, :], AF.Silu)
                ev = kb.mark("act")
            kb.wait(nc.sync, "act")
            kb.sdma(zdram[128 * j:128 * (j + 1), :], zs[:, :])
        kb.barrier()
        load_win_cols(2 * D + 2304, 32)
        kb.barrier()
        ev = kb.mark("dve")
        for tc in range(4):
            pq = p_st[tc % 2]
            kb.wait_mark(t, ev)
            for dt_ in range(ND):
                ins = t.matmul(pq[0:NH, :], winc[dt_][:, 0:32],
                               xlnT[dt_][:, 512 * tc:512 * (tc + 1)],
                               start=(dt_ == 0), stop=(dt_ == ND - 1))
            kb.tick("pe", ins)
            kb.wait(v, "pe")
            kb.vop(v.tensor_copy, dtv[:, 512 * tc:512 * (tc + 1)], pq[0:NH, :])
            ev = kb.mark("dve")
        kb.barrier()

        if stop == "m1":
            break
        for j in range(18):
            kb.wait(v, "act")
            kb.vop(v.tensor_scalar_mul, cs1[:, :], xc[j][:, 0:S], cw[j][:, 0:1])
            for k_ in range(1, 4):
                kb.vop(v.scalar_tensor_tensor, cs1[:, :], xc[j][:, k_:k_ + S],
                       cw[j][:, k_:k_ + 1], cs1[:, :], ALU.mult, ALU.add)
            kb.wait(s, "dve")
            kb.sop(s.activation, xc[j][:, PAD:PAD + S], cs1[:, :], AF.Silu,
                   bias=cb[j][:, 0:1])
        kb.barrier()

        if stop == "m2":
            break
        kb.vop(v.tensor_scalar_add, dtv[:, :], dtv[:, :], dtbc[:, 0:1])
        kb.vop(v.tensor_scalar_max, dacs[:, :], dtv[:, :], 0.0)
        kb.wait(s, "dve")
        kb.sop(s.activation, cs1[0:NH, :], dtv[:, :], AF.Abs)
        kb.sop(s.activation, cs1[0:NH, :], cs1[0:NH, :], AF.Exp, scale=-1.0)
        kb.sop(s.activation, cs1[0:NH, :], cs1[0:NH, :], AF.Ln, bias=onecol[0:NH, 0:1])
        kb.wait(v, "act")
        kb.vop(v.tensor_add, dtv[:, :], dacs[:, :], cs1[0:NH, :])
        kb.vop(v.tensor_scalar_mul, cs1[0:NH, :], dtv[:, :], acol[:, 0:1])
        kb.vop(v.tensor_tensor_scan, dacs[:, :], segs[:, :], cs1[0:NH, :], 0.0,
               ALU.mult, ALU.add)
        ev = kb.mark("dve")
        for tt in range(NTB):
            kb.wait(t, "dve")
            kb.wait(t, "act")
            kb.wait_mark(t, ev)
            t.transpose(p_trf[:, 0:NH], dacs[0:NH, 128 * tt:128 * (tt + 1)],
                        idf[0:NH, 0:NH])
            kb.tick("pe", t.transpose(p_trf[:, 64:64 + NH],
                                      dtv[0:NH, 128 * tt:128 * (tt + 1)],
                                      idf[0:NH, 0:NH]))
            kb.wait(v, "pe")
            kb.vop(v.tensor_copy, dacs_tm[:, 32 * tt:32 * (tt + 1)], p_trf[:, 0:NH])
            kb.vop(v.tensor_copy, dtv_tm[:, 32 * tt:32 * (tt + 1)], p_trf[:, 64:64 + NH])
            ev = kb.mark("dve")
        kb.barrier()

        if stop == "m3":
            break
        kb.vop(v.memset, statef[:, :], 0.0)
        kb.vop(v.memset, stateb[:, :], 0.0)
        kb.barrier()

        for c in range(NCH):
            ttc, r0 = c // 2, 64 * (c % 2)
            cs_ = slice(PAD + CH * c, PAD + CH * (c + 1))
            ev = kb.mark("dve")
            for jg in range(4):
                kb.wait(t, "dve")
                kb.wait_mark(t, ev)
                for k_ in range(4):
                    j = 4 * jg + k_
                    ins = t.transpose(p_trb[0:64, 128 * k_:128 * (k_ + 1)],
                                      xc[j][:, cs_], ids[:, :])
                kb.tick("pe", ins)
                kb.wait(v, "pe")
                for k_ in range(4):
                    j = 4 * jg + k_
                    kb.vop(v.tensor_copy, x_tm[:, 128 * j:128 * (j + 1)],
                           p_trb[0:64, 128 * k_:128 * (k_ + 1)])
                ev = kb.mark("dve")
            kb.wait(t, "dve")
            t.transpose(p_trb[0:64, 0:128], xc[16][:, cs_], ids[:, :])
            t.matmul(p_trf[0:64, 0:64], xc[16][:, cs_], xc[17][:, cs_],
                     start=True, stop=True)
            kb.tick("pe", t.transpose(p_trf[0:1, 64:64 + NH],
                                      dacs[0:NH, CH * c + 63:CH * c + 64],
                                      idf[0:NH, 0:NH]))
            kb.wait(v, "pe")
            kb.vop(v.tensor_copy, btm[:, :], p_trb[0:64, 0:128])
            with nc.allow_low_precision(reason="bf16 G"):
                kb.vop(v.tensor_tensor, Gm[:, :], p_trf[0:64, 0:64], upm64[:, :], ALU.mult)
            kb.vop(v.tensor_copy, cdrow[0:1, 0:NH], p_trf[0:1, 64:64 + NH])
            kb.wait(s, "dve")
            kb.sop(s.activation, cdrow[0:1, 0:NH], cdrow[0:1, 0:NH], AF.Exp)
            kb.wait(v, "act")
            kb.vop(v.tensor_copy,
                   cdb64[0:1, 0:DI].rearrange("o (h q) -> o h q", q=HP),
                   cdrow[0:1, 0:NH].unsqueeze(2).broadcast_to((1, NH, HP)))
            kb.vop(v.tensor_copy, dtq[:, :], dtv_tm[r0:r0 + 64, 32 * ttc:32 * (ttc + 1)])
            x3d = x_tm[:, :].rearrange("p (h q) -> p h q", q=HP)
            kb.wait(v, "dve")
            with nc.allow_low_precision(reason="bf16 dtx"):
                kb.vop(v.tensor_tensor, dtx[:, :].rearrange("p (h q) -> p h q", q=HP),
                       x3d, dtq[:, :].unsqueeze(2).broadcast_to((64, NH, HP)), ALU.mult)
            kb.vop(v.tensor_copy, A2f[:, :].rearrange("p (h q) -> p h q", q=HP),
                   dacs_tm[r0:r0 + 64, 32 * ttc:32 * (ttc + 1)].unsqueeze(2)
                   .broadcast_to((64, NH, HP)))
            kb.wait(s, "dve")
            kb.sop(s.activation, ea[:, :],
                   dacs_tm[r0:r0 + 64, 32 * ttc:32 * (ttc + 1)], AF.Exp)
            evg = kb.mark("dve")
            for hg in range(4):
                kb.wait(t, "dve")
                kb.wait_mark(t, evg)
                for k_ in range(8):
                    h = 8 * hg + k_
                    ins = t.transpose(p_trf[0:64, 64 * k_:64 * (k_ + 1)],
                                      A2f[:, 64 * h:64 * (h + 1)], idf[0:64, 0:64])
                kb.tick("pe", ins)
                kb.wait(v, "pe")
                kb.vop(v.tensor_sub, A2f[:, 512 * hg:512 * (hg + 1)],
                       p_trf[0:64, :], A2f[:, 512 * hg:512 * (hg + 1)])
                kb.vop(v.tensor_sub,
                       A2f[:, 512 * hg:512 * (hg + 1)].rearrange("p (h q) -> p h q", q=HP),
                       A2f[:, 512 * hg:512 * (hg + 1)].rearrange("p (h q) -> p h q", q=HP),
                       bigm[:, :].unsqueeze(1).broadcast_to((64, 8, HP)))
                evg = kb.mark("dve")
            kb.wait(s, "dve")
            kb.sop(s.activation, A2f[:, :], A2f[:, :], AF.Exp)
            kb.wait(v, "act")
            with nc.allow_low_precision(reason="bf16 SSD L"):
                kb.vop(v.tensor_tensor, MT[:, :].rearrange("p (h q) -> p h q", q=HP),
                       A2f[:, :].rearrange("p (h q) -> p h q", q=HP),
                       Gm[:, :].unsqueeze(1).broadcast_to((64, NH, HP)), ALU.mult)
            kb.vop(v.tensor_copy, dc[:, :].unsqueeze(2),
                   A2f[:, :].rearrange("p (h q) -> p h q", q=HP)[:, :, 63:64])
            with nc.allow_low_precision(reason="bf16 dpx"):
                kb.vop(v.tensor_tensor, dpx[:, :].rearrange("p (h q) -> p h q", q=HP),
                       x3d, dpb[:, :].unsqueeze(2).broadcast_to((64, NH, HP)), ALU.mult)
            with nc.allow_low_precision(reason="bf16 dtxd"):
                kb.vop(v.tensor_tensor, dtxd[:, :].rearrange("p (h q) -> p h q", q=HP),
                       dtx[:, :].rearrange("p (h q) -> p h q", q=HP),
                       dc[:, :].unsqueeze(2).broadcast_to((64, NH, HP)), ALU.mult)
            for Hh in range(2):
                h0 = 16 * Hh
                kb.wait(t, "dve")
                for k_ in range(16):
                    h = h0 + k_
                    t.matmul(p_ya[k_ // 8][0:64, 64 * (k_ % 8):64 * (k_ % 8 + 1)],
                             MT[:, 64 * h:64 * (h + 1)], dtx[:, 64 * h:64 * (h + 1)],
                             start=True, stop=True)
                for k_ in range(16):
                    h = h0 + k_
                    ins = t.matmul(p_yo[k_ // 8][0:64, 64 * (k_ % 8):64 * (k_ % 8 + 1)],
                                   xc[17][:, cs_], stateb[:, 64 * h:64 * (h + 1)],
                                   start=True, stop=True)
                kb.tick("pe", ins)
                for sg_ in range(2):
                    ins = t.matmul(p_st[sg_][:, :], btm[:, :],
                                   dtxd[:, 1024 * Hh + 512 * sg_:1024 * Hh + 512 * (sg_ + 1)],
                                   start=True, stop=True)
                kb.tick("pe", ins)
                kb.wait(v, "pe")
                for sg_ in range(2):
                    so = 1024 * Hh + 512 * sg_
                    kb.vop(v.tensor_tensor, yscr[:, :].rearrange("p (h q) -> p h q", q=HP),
                           p_yo[sg_][0:64, :].rearrange("p (h q) -> p h q", q=HP),
                           ea[:, h0 + 8 * sg_:h0 + 8 * (sg_ + 1)].unsqueeze(2)
                           .broadcast_to((64, 8, HP)), ALU.mult)
                    kb.vop(v.tensor_add, yscr[:, :], yscr[:, :], p_ya[sg_][0:64, :])
                    with nc.allow_low_precision(reason="bf16 yh"):
                        kb.vop(v.tensor_add, yh[:, so:so + 512], yscr[:, :],
                               dpx[:, so:so + 512])
                kb.wait(t, "dve")
                for sg_ in range(2):
                    so = 1024 * Hh + 512 * sg_
                    ins = t.matmul(p_ya[sg_][:, :], onesrf[0:1, :],
                                   cdb64[0:1, so:so + 512], start=True, stop=True)
                kb.tick("pe", ins)
                kb.wait(v, "pe")
                for sg_ in range(2):
                    so = 1024 * Hh + 512 * sg_
                    sl4 = slice(so, so + 512)
                    kb.vop(v.tensor_tensor, statef[:, sl4], statef[:, sl4],
                           p_ya[sg_][:, :], ALU.mult)
                    kb.vop(v.tensor_add, statef[:, sl4], statef[:, sl4],
                           p_st[sg_][:, :])
                    with nc.allow_low_precision(reason="bf16 state copy"):
                        kb.vop(v.tensor_copy, stateb[:, sl4], statef[:, sl4])
            ev = kb.mark("dve")
            for jg in range(2):
                kb.wait(t, "dve")
                kb.wait_mark(t, ev)
                for k_ in range(8):
                    j = 8 * jg + k_
                    ins = t.transpose(p_trb[:, 64 * k_:64 * (k_ + 1)],
                                      yh[:, 128 * j:128 * (j + 1)], ids[0:64, 0:64])
                kb.tick("pe", ins)
                kb.wait(v, "pe")
                for k_ in range(8):
                    j = 8 * jg + k_
                    kb.vop(v.tensor_copy, xc[j][:, cs_], p_trb[:, 64 * k_:64 * (k_ + 1)])
                ev = kb.mark("dve")
        kb.barrier()

        if stop == "m4":
            break
        kb.vop(v.memset, cs1[0:1, 0:S], 0.0)
        for j in range(16):
            kb.wait(nc.sync, "pe")
            kb.wait(nc.sync, "dve")
            kb.sdma(zs[:, :], zdram[128 * j:128 * (j + 1), :])
            kb.wait(v, "sdma")
            with nc.allow_low_precision(reason="bf16 gated stream"):
                kb.vop(v.tensor_tensor, xc[j][:, PAD:PAD + S], xc[j][:, PAD:PAD + S],
                       zs[:, :], ALU.mult)
            kb.wait(s, "dve")
            kb.sop(s.activation, zs[:, :], xc[j][:, PAD:PAD + S], AF.Square)
            kb.wait(t, "act")
            for tc in range(4):
                kb.tick("pe", t.matmul(p_trf[0:1, 0:512], onescol[:, 0:1],
                                       zs[:, 512 * tc:512 * (tc + 1)],
                                       start=True, stop=True))
                kb.wait(v, "pe")
                kb.vop(v.tensor_add, cs1[0:1, 512 * tc:512 * (tc + 1)],
                       cs1[0:1, 512 * tc:512 * (tc + 1)], p_trf[0:1, 0:512])
                kb.wait(t, "dve")
        kb.barrier()
        if stop == "m5g":
            break
        kb.sop(s.activation, cs1[0:1, 0:S], cs1[0:1, 0:S], AF.Ln,
               scale=1.0 / DI, bias=epscol[0:1, 0:1])
        kb.sop(s.activation, cs1[0:1, 0:S], cs1[0:1, 0:S], AF.Exp, scale=-0.5)
        kb.wait(v, "act")
        with nc.allow_low_precision(reason="bf16 rstd"):
            kb.vop(v.tensor_copy, rrowb[0:1, :], cs1[0:1, 0:S])
        kb.wait(t, "dve")
        ev = kb.mark("dve")
        for tc in range(4):
            kb.wait_mark(t, ev)
            kb.tick("pe", t.matmul(p_trf[:, 0:512], onesr[0:1, :],
                                   rrowb[0:1, 512 * tc:512 * (tc + 1)],
                                   start=True, stop=True))
            kb.wait(v, "pe")
            with nc.allow_low_precision(reason="bf16 rstd bcast"):
                kb.vop(v.tensor_copy, rb128[:, 512 * tc:512 * (tc + 1)], p_trf[:, 0:512])
            ev = kb.mark("dve")
        for j in range(16):
            with nc.allow_low_precision(reason="bf16 gn"):
                kb.vop(v.scalar_tensor_tensor, xc[j][:, PAD:PAD + S],
                       xc[j][:, PAD:PAD + S], nws[:, j:j + 1], rb128[:, :],
                       ALU.mult, ALU.mult)
        kb.barrier()
        res_done = [kb.mark("dve"), kb.mark("dve")]
        for tt in range(NTB):
            xb_ = xt2[tt % 2]
            kb.wait_mark(nc.sync, res_done[tt % 2])
            kb.sdma(xb_[:, :], x[tok0 + 128 * tt: tok0 + 128 * (tt + 1), :])
            kb.wait(t, "dve")
            for cc_, pw in ((0, p_st[0]), (1, p_st[1])):
                for j in range(16):
                    ins = t.matmul(pw[:, :], xc[j][:, PAD + 128 * tt:PAD + 128 * (tt + 1)],
                                   wout[j][:, 512 * cc_:512 * (cc_ + 1)],
                                   start=(j == 0), stop=(j == 15))
                kb.tick("pe", ins)
            kb.wait(v, "pe")
            kb.wait(v, "sdma")
            kb.vop(v.tensor_add, wsum[:, 0:512], xb_[:, 0:512], p_st[0][:, :])
            kb.vop(v.tensor_add, wsum[:, 512:1024], xb_[:, 512:1024], p_st[1][:, :])
            res_done[tt % 2] = kb.mark("dve")
            kb.wait(nc.sync, "dve")
            kb.sdma(x3[tok0 + 128 * tt: tok0 + 128 * (tt + 1), :], wsum[:, :])
        kb.barrier()
        if stop == "m5w":
            break
    if debug_taps:
        groups = {
            "ln": {"d_xlnT0": xlnT[0]},
            "m1": {"d_xlnT0": xlnT[0], "d_xc5": xc[5], "d_dtv": dtv},
            "m2": {"d_xc5": xc[5], "d_xc16": xc[16], "d_xc17": xc[17]},
            "m3": {"d_dtv": dtv, "d_dacs": dacs, "d_dacs_tm": dacs_tm},
            "m4": {"d_xc5": xc[5], "d_yh": yh, "d_statef": statef, "d_xtm": x_tm,
                   "d_dtx": dtx, "d_MT": MT, "d_A2f": A2f, "d_Gm": Gm, "d_ea": ea,
                   "d_dc": dc, "d_dpb": dpb, "d_dacs_tm": dacs_tm},
            "m5g": {"d_xc5": xc[5], "d_cs1": cs1},
            "m5w": {"d_rb128": rb128, "d_xc5": xc[5], "d_cs1": cs1},
        }
        taps = groups.get(stop, {"d_xlnT0": xlnT[0]})
        for nm, tile in taps.items():
            shp = list(tile.shape)
            o = nc.dram_tensor(nm, shp, tile.dtype, kind="ExternalOutput").ap()
            kb.sdma(o[:, :], tile[:, :])
        kb.barrier()


def build_l1_gate():
    import contextlib
    kb = KB("l1gate")
    nc = kb.nc
    E = 16
    x = nc.dram_tensor("x", [TPC, D], F32, kind="ExternalInput").ap()
    gwT = nc.dram_tensor("gwT", [D, E], F32, kind="ExternalInput").ap()
    o_xln = nc.dram_tensor("o_xln", [TPC, D], BF16, kind="ExternalOutput").ap()
    o_oh = nc.dram_tensor("o_oh", [TPC, E], F32, kind="ExternalOutput").ap()
    o_loc = nc.dram_tensor("o_loc", [TPC, 1], F32, kind="ExternalOutput").ap()
    o_pmax = nc.dram_tensor("o_pmax", [TPC, 1], F32, kind="ExternalOutput").ap()
    o_eidx = nc.dram_tensor("o_eidx", [TPC, 1], F32, kind="ExternalOutput").ap()
    o_cnt = nc.dram_tensor("o_cnt", [1, E], F32, kind="ExternalOutput").ap()
    with contextlib.ExitStack() as es:
        emit_l1_gate(kb, es, x, o_xln, o_oh, o_loc, o_pmax, o_eidx, o_cnt, gwT)
    kb.finish()
    return kb.nc


def emit_l1_gate(kb, es, x, o_xln, o_oh, o_loc, o_pmax, o_eidx, o_cnt, gwT, px="g_"):
    nc = kb.nc
    E = 16
    consts = {
        "ident": np.eye(128, dtype=BF),
        "ustrict": (np.arange(128)[:, None] < np.arange(128)[None, :]).astype(BF),
        "iota16": np.broadcast_to(np.arange(16, dtype=np.float32), (128, 16)).copy(),
        "ones16": np.ones((128, 16), dtype=np.float32),
    }
    cap = {k: nc.inline_tensor(np.ascontiguousarray(v), px + "c_" + k).ap() for k, v in consts.items()}

    sb, ps = _scoped_allocs(nc, es, px)
    ids = sb("ids", [128, 128], BF16)
    ustr = sb("ustr", [128, 128], BF16)
    iot = sb("iot", [128, E], F32)
    on16 = sb("on16", [128, E], F32)
    onescol = sb("onescol", [128, 1], BF16)
    onesrf = sb("onesrf2", [1, 128], F32)
    gws = [sb(f"gws{i}", [128, E], BF16) for i in range(ND)]
    xt2 = [sb(f"xt{i}", [128, D], F32) for i in range(2)]
    sq = sb("sq", [128, D], F32)
    xlnb = sb("xlnb", [128, D], BF16)
    xlt = [sb(f"xlt{i}", [128, 128], BF16) for i in range(ND)]
    lg = sb("lg", [128, E], F32)
    ohb = sb("ohb", [128, E], BF16)
    ohf = sb("ohf", [128, E], F32)
    cum = sb("cum", [128, E], F32)
    loc = sb("loc", [128, E], F32)
    gmax = sb("gmax", [128, 1], F32)
    gsum = sb("gsum", [128, 1], F32)
    loccol = sb("loccol", [128, 1], F32)
    eidxc = sb("eidxc", [128, 1], F32)
    roff = sb("roff", [1, E], F32)
    epscol = sb("epscol", [128, 1], F32)
    st = LNStats(nc, "g", sb)

    p_tr = ps("p_tr", [128, 512], BF16)
    p_g = ps("p_g", [128, E], F32)
    p_w = ps("p_w", [128, E], F32)
    p_b = ps("p_b", [128, E], F32)
    p_r = ps("p_r", [1, E], F32)

    t, v, s = nc.tensor, nc.vector, nc.scalar
    kb.sdma(ids[:, :], cap["ident"][:, :])
    kb.sdma(ustr[:, :], cap["ustrict"][:, :])
    kb.sdma(iot[:, :], cap["iota16"][:, :])
    kb.sdma(on16[:, :], cap["ones16"][:, :])
    for i in range(ND):
        kb.gdma(gws[i][:, :], gwT[128 * i:128 * (i + 1), :])
    kb.vop(v.memset, epscol[:, :], LN_EPS)
    with nc.allow_low_precision(reason="ones"):
        kb.vop(v.memset, onescol[:, :], 1.0)
    kb.vop(v.memset, onesrf[:, :], 1.0)
    kb.vop(v.memset, roff[:, :], 0.0)
    kb.barrier()

    ln_done = [kb.mark("act"), kb.mark("act")]
    tr_evac = kb.mark("pe")
    for gt in range(TPC // 128):
        xb_ = xt2[gt % 2]
        kb.wait_mark(nc.sync, ln_done[gt % 2])
        kb.sdma(xb_[:, :], x[128 * gt:128 * (gt + 1), :])
        kb.wait(v, "sdma")
        kb.wait(s, "sdma")
        kb.wait(s, "pe")
        ln_tile(kb, xb_[:, :], xlnb[:, :], sq[:, :], st, epscol)
        ln_done[gt % 2] = kb.mark("act")
        kb.wait(nc.sync, "act")
        kb.sdma(o_xln[128 * gt:128 * (gt + 1), :], xlnb[:, :])
        kb.wait(t, "act")
        for g in range(2):
            kb.wait(t, "dve")
            for k_ in range(4):
                dt = 4 * g + k_
                ins = t.transpose(p_tr[:, 128 * k_:128 * (k_ + 1)],
                                  xlnb[:, 128 * dt:128 * (dt + 1)], ids[:, :])
            kb.tick("pe", ins)
            kb.wait(v, "pe")
            for k_ in range(4):
                dt = 4 * g + k_
                kb.vop(v.tensor_copy, xlt[dt][:, :], p_tr[:, 128 * k_:128 * (k_ + 1)])
        kb.wait(t, "dve")
        for dt in range(ND):
            ins = t.matmul(p_g[:, :], xlt[dt][:, :], gws[dt][:, :],
                           start=(dt == 0), stop=(dt == ND - 1))
        kb.tick("pe", ins)
        kb.wait(v, "pe")
        kb.vop(v.tensor_copy, lg[:, :], p_g[:, :])
        kb.vop(v.reduce_max, gmax[:, :], lg[:, :], AX.X, negate=True)
        kb.wait(s, "dve")
        kb.sop(s.activation, lg[:, :], lg[:, :], AF.Exp, bias=gmax[:, 0:1])
        kb.wait(v, "act")
        kb.vop(v.reduce_sum, gsum[:, :], lg[:, :], AX.X)
        kb.vop(v.reciprocal, gsum[:, :], gsum[:, :])
        kb.vop(v.tensor_scalar_mul, lg[:, :], lg[:, :], gsum[:, 0:1])
        kb.vop(v.reduce_max, gmax[:, :], lg[:, :], AX.X)
        kb.vop(v.tensor_scalar, ohf[:, :], lg[:, :], gmax[:, 0:1], None, ALU.is_equal)
        kb.vop(v.tensor_tensor_scan, cum[:, :], on16[:, :], ohf[:, :], 0.0,
               ALU.mult, ALU.add)
        kb.vop(v.tensor_single_scalar, cum[:, :], cum[:, :], 1.0, ALU.is_equal)
        kb.vop(v.tensor_tensor, ohf[:, :], ohf[:, :], cum[:, :], ALU.mult)
        with nc.allow_low_precision(reason="bf16 onehot"):
            kb.vop(v.tensor_copy, ohb[:, :], ohf[:, :])
        kb.vop(v.tensor_tensor, cum[:, :], ohf[:, :], iot[:, :], ALU.mult)
        kb.vop(v.reduce_sum, eidxc[:, :], cum[:, :], AX.X)
        kb.wait(t, "dve")
        t.matmul(p_w[:, :], ustr[:, :], ohb[:, :], start=True, stop=False)
        kb.tick("pe", t.matmul(p_w[:, :], onesrf[0:1, :], roff[0:1, :],
                               start=False, stop=True))
        kb.wait(v, "pe")
        kb.vop(v.tensor_tensor, loc[:, :], p_w[:, :], ohf[:, :], ALU.mult)
        kb.vop(v.reduce_sum, loccol[:, :], loc[:, :], AX.X)
        kb.wait(t, "dve")
        kb.tick("pe", t.matmul(p_r[0:1, :], onescol[:, 0:1], ohb[:, :],
                               start=True, stop=True))
        kb.wait(v, "pe")
        kb.vop(v.tensor_add, roff[0:1, :], roff[0:1, :], p_r[0:1, :])
        kb.wait(nc.sync, "dve")
        kb.sdma(o_oh[128 * gt:128 * (gt + 1), :], ohf[:, :])
        kb.sdma(o_loc[128 * gt:128 * (gt + 1), :], loccol[:, :])
        kb.sdma(o_pmax[128 * gt:128 * (gt + 1), :], gmax[:, :])
        kb.sdma(o_eidx[128 * gt:128 * (gt + 1), :], eidxc[:, :])
    kb.wait(nc.sync, "dve")
    kb.sdma(o_cnt[:, :], roff[:, :])


def build_l1_slots():
    kb = KB("l1slots")
    nc = kb.nc
    E, C = 16, 2560
    allcnt = nc.dram_tensor("allcnt", [NCORES, E], F32, kind="ExternalInput").ap()
    coremask = nc.dram_tensor("coremask", [NCORES, 1], F32, kind="ExternalInput").ap()
    tokidx = nc.dram_tensor("tokidx", [TPC, 1], F32, kind="ExternalInput").ap()
    oh = nc.dram_tensor("oh", [TPC, E], F32, kind="ExternalInput").ap()
    loc = nc.dram_tensor("loc", [TPC, 1], F32, kind="ExternalInput").ap()
    pmax = nc.dram_tensor("pmax", [TPC, 1], F32, kind="ExternalInput").ap()
    eidx = nc.dram_tensor("eidx", [TPC, 1], F32, kind="ExternalInput").ap()
    o_idx = nc.dram_tensor("o_idx", [E * C, 1], F32, kind="ExternalOutput").ap()
    o_w = nc.dram_tensor("o_w", [TPC, 1], F32, kind="ExternalOutput").ap()
    o_slot = nc.dram_tensor("o_slot", [TPC, 1], F32, kind="ExternalOutput").ap()

    sb = nc.alloc_sbuf_tensor
    acnt = sb("acnt", [NCORES, E], F32)
    cmsk = sb("cmsk", [NCORES, 1], F32)
    cobc = sb("cobc", [128, E], F32)
    ohs = sb("ohs", [128, E], F32)
    locs = sb("locs", [128, 1], F32)
    pms = sb("pms", [128, 1], F32)
    eis = sb("eis", [128, 1], F32)
    tks = sb("tks", [128, 1], F32)
    sel = sb("sel", [128, E], F32)
    gl = sb("gl", [128, 1], F32)
    kp = sb("kp", [128, 1], F32)
    wc = sb("wc", [128, 1], F32)
    sl2 = sb("sl2", [128, 1], F32)
    vals = sb("vals", [128, 1], F32)
    sidx = sb("sidx", [128, 1], I32)
    zt = sb("zt", [128, 320], F32)
    p_c = nc.alloc_psum_tensor("p_c", [128, E], F32)

    t, v, s = nc.tensor, nc.vector, nc.scalar
    kb.sdma(acnt[:, :], allcnt[:, :])
    kb.sdma(cmsk[:, :], coremask[:, :])
    kb.vop(v.memset, zt[:, :], 0.0)
    kb.barrier()
    kb.sdma(o_idx[:, :].rearrange("(p q) o -> p (q o)", p=128), zt[:, :])
    onesr = sb("onesr", [1, 128], F32)
    kb.vop(v.memset, onesr[:, :], 1.0)
    kb.tick("pe", t.matmul(p_c[0:1, :], cmsk[:, :], acnt[:, :], start=True, stop=True))
    kb.wait(v, "pe")
    kb.vop(v.tensor_copy, cobc[0:1, :], p_c[0:1, :])
    kb.wait(t, "dve")
    kb.tick("pe", t.matmul(p_c[:, :], onesr[0:1, :], cobc[0:1, :], start=True, stop=True))
    kb.wait(v, "pe")
    kb.vop(v.tensor_copy, cobc[:, :], p_c[:, :])
    kb.barrier()

    for gt in range(TPC // 128):
        r = slice(128 * gt, 128 * (gt + 1))
        kb.wait(nc.sync, "dve")
        kb.wait(nc.sync, "gdma")
        kb.sdma(ohs[:, :], oh[r, :])
        kb.sdma(locs[:, :], loc[r, :])
        kb.sdma(pms[:, :], pmax[r, :])
        kb.sdma(eis[:, :], eidx[r, :])
        kb.sdma(tks[:, :], tokidx[r, :])
        kb.wait(v, "sdma")
        kb.vop(v.tensor_tensor, sel[:, :], ohs[:, :], cobc[:, :], ALU.mult)
        kb.vop(v.reduce_sum, gl[:, :], sel[:, :], AX.X)
        kb.vop(v.tensor_add, gl[:, :], gl[:, :], locs[:, :])
        kb.vop(v.tensor_single_scalar, kp[:, :], gl[:, :], float(C), ALU.is_lt)
        kb.vop(v.tensor_tensor, wc[:, :], pms[:, :], kp[:, :], ALU.mult)
        kb.vop(v.tensor_scalar, sl2[:, :], eis[:, :], float(C), gl[:, 0:1],
               ALU.mult, ALU.add)
        kb.vop(v.tensor_scalar, kp[:, :], kp[:, :], -1e9, 1e9, ALU.mult, ALU.add)
        kb.vop(v.tensor_add, sl2[:, :], sl2[:, :], kp[:, :])
        kb.vop(v.tensor_scalar_add, vals[:, :], tks[:, :], 1.0)
        with nc.allow_low_precision(reason="int index cast"):
            kb.vop(v.tensor_copy, sidx[:, :], sl2[:, :])
        kb.wait(nc.sync, "dve")
        kb.sdma(o_w[r, :], wc[:, :])
        kb.sdma(o_slot[r, :], sl2[:, :])
        kb.wait(nc.gpsimd, "dve")
        kb.wait(nc.gpsimd, "sdma")
        ins = nc.gpsimd.indirect_dma_start(
            o_idx[:, :], bass.IndirectOffsetOnAxis(sidx[:, 0:1], 0),
            vals[:, 0:1], None, bounds_check=E * C - 1, oob_is_err=False)
        ins.then_inc(kb.sem["gdma"], 16)
        kb.cnt["gdma"] += 16
    kb.finish()
    return kb.nc


def build_l1_experts():
    kb = KB("l1exp")
    nc = kb.nc
    E, C, FF = 16, 2560, 4096
    NEC = 2
    idx2T = nc.dram_tensor("idx2T", [NEC * C, NCORES], F32, kind="ExternalInput").ap()
    xga = nc.dram_tensor("xga", [B * S, D], BF16, kind="ExternalInput").ap()
    w1 = nc.dram_tensor("w1", [NEC, D, FF], F32, kind="ExternalInput").ap()
    w2 = nc.dram_tensor("w2", [NEC, FF, D], F32, kind="ExternalInput").ap()
    o_y = nc.dram_tensor("o_y", [NEC * C, D], BF16, kind="ExternalOutput").ap()

    sb = nc.alloc_sbuf_tensor
    idc = nc.inline_tensor(np.eye(128, dtype=BF), "c_ident").ap()
    idss = sb("idss", [128, 128], BF16)
    code = sb("code", [128, NCORES], F32)
    gidx = sb("gidx", [128, 1], F32)
    gemp = sb("gemp", [128, 1], F32)
    gi32 = sb("gi32", [128, 1], I32)
    xg = sb("xg", [128, D], BF16)
    xgT = [sb(f"xgT{i}", [128, C], BF16) for i in range(ND)]
    hT = [sb(f"hT{i}", [128, 512], BF16) for i in range(32)]
    w1s = [sb(f"w1s{i}", [128, 512], BF16) for i in range(ND)]
    w2s = [sb(f"w2s{i}", [128, D], BF16) for i in range(32)]
    yt = sb("yt", [128, D], BF16)
    p_tr = nc.alloc_psum_tensor("p_tr", [128, 512], BF16)
    p_h = [nc.alloc_psum_tensor(f"p_h{i}", [128, 512], F32) for i in range(2)]
    p_y = [nc.alloc_psum_tensor(f"p_y{i}", [128, 512], F32) for i in range(4)]

    t, v, s = nc.tensor, nc.vector, nc.scalar
    kb.sdma(idss[:, :], idc[:, :])
    kb.barrier()

    for e in range(NEC):
        kb.wait(nc.gpsimd, "pe")
        for i in range(32):
            kb.gdma(w2s[i][:, :], w2[e, 128 * i:128 * (i + 1), :])
        tr_ev = kb.mark("dve")
        for tl in range(20):
            sl0 = e * C + 128 * tl
            kb.wait(nc.sync, "dve")
            kb.sdma(code[:, :], idx2T[sl0:sl0 + 128, :])
            kb.wait(v, "sdma")
            kb.vop(v.reduce_sum, gidx[:, :], code[:, :], AX.X)
            kb.vop(v.tensor_single_scalar, gemp[:, :], gidx[:, :], 0.0, ALU.is_equal)
            kb.vop(v.tensor_scalar, gemp[:, :], gemp[:, :], 1e9, -1.0, ALU.mult, ALU.add)
            kb.vop(v.tensor_add, gidx[:, :], gidx[:, :], gemp[:, :])
            with nc.allow_low_precision(reason="index cast"):
                kb.vop(v.tensor_copy, gi32[:, :], gidx[:, :])
            kb.wait(v, "gdma")
            kb.wait(v, "pe")
            kb.vop(v.memset, xg[:, :], 0.0)
            kb.wait(nc.gpsimd, "dve")
            ins = nc.gpsimd.indirect_dma_start(
                xg[:, :], None, xga[:, :],
                bass.IndirectOffsetOnAxis(gi32[:, 0:1], 0),
                bounds_check=B * S - 1, oob_is_err=False)
            ins.then_inc(kb.sem["gdma"], 16)
            kb.cnt["gdma"] += 16
            kb.wait(t, "gdma")
            for g in range(2):
                kb.wait(t, "dve")
                for k_ in range(4):
                    dt = 4 * g + k_
                    ins = t.transpose(p_tr[:, 128 * k_:128 * (k_ + 1)],
                                      xg[:, 128 * dt:128 * (dt + 1)], idss[:, :])
                    if k_ == 3:
                        kb.tick("pe", ins)
                kb.wait(v, "pe")
                for k_ in range(4):
                    dt = 4 * g + k_
                    kb.vop(v.tensor_copy, xgT[dt][:, 128 * tl:128 * (tl + 1)],
                           p_tr[:, 128 * k_:128 * (k_ + 1)])
            tr_ev = kb.mark("dve")
        kb.barrier()
        for sc in range(5):
            ssl = slice(512 * sc, 512 * (sc + 1))
            for fq in range(8):
                kb.wait(nc.gpsimd, "pe")
                for i in range(ND):
                    kb.gdma(w1s[i][:, :], w1[e, 128 * i:128 * (i + 1),
                                             512 * fq:512 * (fq + 1)])
                kb.wait(t, "gdma")
                hev = kb.mark("act")
                for ft in range(4):
                    ph = p_h[ft % 2]
                    kb.wait_mark(t, hev)
                    for dt in range(ND):
                        ins = t.matmul(ph[:, :], w1s[dt][:, 128 * ft:128 * (ft + 1)],
                                       xgT[dt][:, ssl],
                                       start=(dt == 0), stop=(dt == ND - 1))
                    kb.tick("pe", ins)
                    kb.wait(s, "pe")
                    kb.sop(s.activation, hT[4 * fq + ft][:, :], ph[:, :], AF.Relu)
                    hev = kb.mark("act")
            kb.barrier()
            yev = [kb.mark("dve"), kb.mark("dve")]
            for st_ in range(4):
                pa, pb_ = p_y[2 * (st_ % 2)], p_y[2 * (st_ % 2) + 1]
                kb.wait_mark(t, yev[st_ % 2])
                for fl in range(32):
                    t.matmul(pa[:, :], hT[fl][:, 128 * st_:128 * (st_ + 1)],
                             w2s[fl][:, 0:512], start=(fl == 0), stop=(fl == 31))
                for fl in range(32):
                    ins = t.matmul(pb_[:, :], hT[fl][:, 128 * st_:128 * (st_ + 1)],
                                   w2s[fl][:, 512:1024], start=(fl == 0), stop=(fl == 31))
                kb.tick("pe", ins)
                kb.wait(v, "pe")
                kb.wait(v, "sdma")
                with nc.allow_low_precision(reason="bf16 expert out"):
                    kb.vop(v.tensor_copy, yt[:, 0:512], pa[:, :])
                    kb.vop(v.tensor_copy, yt[:, 512:1024], pb_[:, :])
                yev[st_ % 2] = kb.mark("dve")
                kb.wait(nc.sync, "dve")
                kb.sdma(o_y[e * C + 512 * sc + 128 * st_:
                            e * C + 512 * sc + 128 * (st_ + 1), :], yt[:, :])
            kb.barrier()
    kb.finish()
    return kb.nc


def build_l1_combine():
    kb = KB("l1comb")
    nc = kb.nc
    E, C = 16, 2560
    x = nc.dram_tensor("x", [TPC, D], F32, kind="ExternalInput").ap()
    ya = nc.dram_tensor("ya", [E * C, D], BF16, kind="ExternalInput").ap()
    slot = nc.dram_tensor("slot", [TPC, 1], F32, kind="ExternalInput").ap()
    w = nc.dram_tensor("w", [TPC, 1], F32, kind="ExternalInput").ap()
    out = nc.dram_tensor("out", [TPC, D], F32, kind="ExternalOutput").ap()

    sb = nc.alloc_sbuf_tensor
    xs = sb("xs", [128, D], F32)
    yg = sb("yg", [128, D], BF16)
    sc_ = sb("sc", [128, 1], F32)
    wc = sb("wc", [128, 1], F32)
    si32 = sb("si32", [128, 1], I32)
    os_ = sb("os", [128, D], F32)
    v = nc.vector
    for gt in range(TPC // 128):
        r = slice(128 * gt, 128 * (gt + 1))
        kb.wait(nc.sync, "dve")
        kb.sdma(xs[:, :], x[r, :])
        kb.sdma(sc_[:, :], slot[r, :])
        kb.sdma(wc[:, :], w[r, :])
        kb.wait(v, "sdma")
        kb.wait(v, "gdma")
        kb.vop(v.memset, yg[:, :], 0.0)
        with nc.allow_low_precision(reason="index cast"):
            kb.vop(v.tensor_copy, si32[:, :], sc_[:, :])
        kb.wait(nc.gpsimd, "dve")
        ins = nc.gpsimd.indirect_dma_start(
            yg[:, :], None, ya[:, :], bass.IndirectOffsetOnAxis(si32[:, 0:1], 0),
            bounds_check=E * C - 1, oob_is_err=False)
        ins.then_inc(kb.sem["gdma"], 16)
        kb.cnt["gdma"] += 16
        kb.wait(v, "gdma")
        kb.vop(v.scalar_tensor_tensor, os_[:, :], yg[:, :], wc[:, 0:1], xs[:, :],
               ALU.mult, ALU.add)
        kb.wait(nc.sync, "dve")
        kb.sdma(out[r, :], os_[:, :])
    kb.finish()
    return kb.nc


def build_fused():
    import contextlib
    kb = KB("fused")
    nc = kb.nc
    x = nc.dram_tensor("x", [TPC, D], F32, kind="ExternalInput").ap()
    wqkvT = nc.dram_tensor("wqkvT", [D, 3 * D], F32, kind="ExternalInput").ap()
    woT = nc.dram_tensor("woT", [D, D], F32, kind="ExternalInput").ap()
    wr = nc.dram_tensor("wr", [D, 8], F32, kind="ExternalInput").ap()
    br = nc.dram_tensor("br", [1, 8], F32, kind="ExternalInput").ap()
    w1 = nc.dram_tensor("w1", [8, D, 4096], F32, kind="ExternalInput").ap()
    b1r = nc.dram_tensor("b1r", [8, 128, 32], F32, kind="ExternalInput").ap()
    w2 = nc.dram_tensor("w2", [8, 4096, D], F32, kind="ExternalInput").ap()
    b2 = nc.dram_tensor("b2", [8, D], F32, kind="ExternalInput").ap()
    winT = nc.dram_tensor("winT", [D, 4384], F32, kind="ExternalInput").ap()
    convw = nc.dram_tensor("convw", [2304, 4], F32, kind="ExternalInput").ap()
    convb = nc.dram_tensor("convb", [2304, 1], F32, kind="ExternalInput").ap()
    dtb = nc.dram_tensor("dtb", [32, 1], F32, kind="ExternalInput").ap()
    alog = nc.dram_tensor("alog", [32, 1], F32, kind="ExternalInput").ap()
    dpar = nc.dram_tensor("dpar", [32, 1], F32, kind="ExternalInput").ap()
    nwr = nc.dram_tensor("nwr", [128, 16], F32, kind="ExternalInput").ap()
    woutT = nc.dram_tensor("woutT", [2 * D, D], F32, kind="ExternalInput").ap()
    gwT = nc.dram_tensor("gwT", [D, 16], F32, kind="ExternalInput").ap()

    x1 = nc.dram_tensor("x1", [TPC, D], F32, kind="Internal").ap()
    x2 = nc.dram_tensor("x2", [TPC, D], F32, kind="Internal").ap()
    x3 = nc.dram_tensor("x3", [TPC, D], F32, kind="ExternalOutput").ap()
    o_xln = nc.dram_tensor("o_xln", [TPC, D], BF16, kind="ExternalOutput").ap()
    o_oh = nc.dram_tensor("o_oh", [TPC, 16], F32, kind="ExternalOutput").ap()
    o_loc = nc.dram_tensor("o_loc", [TPC, 1], F32, kind="ExternalOutput").ap()
    o_pmax = nc.dram_tensor("o_pmax", [TPC, 1], F32, kind="ExternalOutput").ap()
    o_eidx = nc.dram_tensor("o_eidx", [TPC, 1], F32, kind="ExternalOutput").ap()
    o_cnt = nc.dram_tensor("o_cnt", [1, 16], F32, kind="ExternalOutput").ap()

    with contextlib.ExitStack() as es:
        emit_l0_attn(kb, es, x, x1, wqkvT, woT)
    kb.barrier()
    with contextlib.ExitStack() as es:
        emit_l0_ffn(kb, es, x1, x2, wr, br, w1, b1r, w2, b2)
    kb.barrier()
    with contextlib.ExitStack() as es:
        emit_l1_attn(kb, es, x2, x3, winT, convw, convb, dtb, alog, dpar, nwr, woutT)
    kb.barrier()
    with contextlib.ExitStack() as es:
        emit_l1_gate(kb, es, x3, o_xln, o_oh, o_loc, o_pmax, o_eidx, o_cnt, gwT)
    kb.finish()
    return kb.nc

def _shard_rows(a):
    a = np.ascontiguousarray(a)
    return [a[i * TPC:(i + 1) * TPC] for i in range(NCORES)]


_NC_CACHE = {}


def _get_nc(name, builder):
    if name not in _NC_CACHE:
        _NC_CACHE[name] = builder()
    return _NC_CACHE[name]


def kernel(**inputs):
    x = np.ascontiguousarray(np.asarray(inputs["L0_attn_x"], dtype=np.float32).reshape(B * S, D))
    wqkvT = np.ascontiguousarray(np.asarray(inputs["L0_attn_W_qkv"], dtype=np.float32).T)
    woT = np.ascontiguousarray(np.asarray(inputs["L0_attn_W_o"], dtype=np.float32).T)
    core_ids = list(range(NCORES))

    w1 = np.ascontiguousarray(np.asarray(inputs["L0_ffn_W1"], dtype=np.float32))
    b1 = np.asarray(inputs["L0_ffn_b1"], dtype=np.float32)
    b1r = np.ascontiguousarray(b1.reshape(8, 32, 128).transpose(0, 2, 1))
    w2 = np.ascontiguousarray(np.asarray(inputs["L0_ffn_W2"], dtype=np.float32))
    b2 = np.ascontiguousarray(np.asarray(inputs["L0_ffn_b2"], dtype=np.float32))
    wr = np.ascontiguousarray(np.asarray(inputs["L0_ffn_Wr"], dtype=np.float32))
    br = np.ascontiguousarray(np.asarray(inputs["L0_ffn_br"], dtype=np.float32).reshape(1, 8))
    winT = np.ascontiguousarray(np.asarray(inputs["L1_attn_W_in"], dtype=np.float32).T)
    convw = np.ascontiguousarray(np.asarray(inputs["L1_attn_conv_w"], dtype=np.float32))
    convb = np.ascontiguousarray(np.asarray(inputs["L1_attn_conv_b"], dtype=np.float32).reshape(-1, 1))
    dtb = np.ascontiguousarray(np.asarray(inputs["L1_attn_dt_bias"], dtype=np.float32).reshape(-1, 1))
    alog = np.ascontiguousarray(np.asarray(inputs["L1_attn_A_log"], dtype=np.float32).reshape(-1, 1))
    dpar = np.ascontiguousarray(np.asarray(inputs["L1_attn_D_param"], dtype=np.float32).reshape(-1, 1))
    nwr = np.ascontiguousarray(np.asarray(inputs["L1_attn_norm_w"], dtype=np.float32).reshape(16, 128).T)
    woutT = np.ascontiguousarray(np.asarray(inputs["L1_attn_W_out"], dtype=np.float32).T)
    gwT = np.ascontiguousarray(np.asarray(inputs["L1_ffn_gate_w"], dtype=np.float32).T)
    ncA = _get_nc("fused", build_fused)
    xs = _shard_rows(x)
    res = run_bass_kernel_spmd(
        ncA, [{"x": xs[i], "wqkvT": wqkvT, "woT": woT, "wr": wr, "br": br,
               "w1": w1, "b1r": b1r, "w2": w2, "b2": b2, "winT": winT,
               "convw": convw, "convb": convb, "dtb": dtb, "alog": alog,
               "dpar": dpar, "nwr": nwr, "woutT": woutT, "gwT": gwT}
              for i in range(NCORES)], core_ids=core_ids)
    r4a = res.results
    x3 = [r4a[i]["x3"] for i in range(NCORES)]

    allcnt = np.ascontiguousarray(np.concatenate([r["o_cnt"] for r in r4a], axis=0))
    nc4b = _get_nc("l1slots", build_l1_slots)
    in_maps = []
    for i in range(NCORES):
        in_maps.append({
            "allcnt": allcnt,
            "coremask": np.ascontiguousarray((np.arange(NCORES) < i)
                                             .astype(np.float32).reshape(-1, 1)),
            "tokidx": np.ascontiguousarray(np.arange(i * TPC, (i + 1) * TPC,
                                                     dtype=np.float32).reshape(-1, 1)),
            "oh": r4a[i]["o_oh"], "loc": r4a[i]["o_loc"],
            "pmax": r4a[i]["o_pmax"], "eidx": r4a[i]["o_eidx"],
        })
    res = run_bass_kernel_spmd(nc4b, in_maps, core_ids=core_ids)
    r4b = res.results

    idxparts = np.stack([r4b[i]["o_idx"].reshape(16, 2560) for i in range(NCORES)])
    xga = np.ascontiguousarray(np.concatenate([r["o_xln"] for r in r4a], axis=0))
    w1f = np.asarray(inputs["L1_ffn_w1"], dtype=np.float32)
    w2f = np.asarray(inputs["L1_ffn_w2"], dtype=np.float32)
    nc4c = _get_nc("l1exp", build_l1_experts)
    in_maps = []
    for i in range(NCORES):
        idx2T = np.ascontiguousarray(
            idxparts[:, 2 * i:2 * i + 2, :].reshape(NCORES, -1).T)
        in_maps.append({"idx2T": idx2T, "xga": xga,
                        "w1": np.ascontiguousarray(w1f[2 * i:2 * i + 2]),
                        "w2": np.ascontiguousarray(w2f[2 * i:2 * i + 2])})
    res = run_bass_kernel_spmd(nc4c, in_maps, core_ids=core_ids)
    ya = np.ascontiguousarray(np.concatenate([res.results[i]["o_y"]
                                              for i in range(NCORES)], axis=0))

    nc4d = _get_nc("l1comb", build_l1_combine)
    res = run_bass_kernel_spmd(
        nc4d, [{"x": x3[i], "ya": ya, "slot": r4b[i]["o_slot"], "w": r4b[i]["o_w"]}
               for i in range(NCORES)], core_ids=core_ids)
    out = np.concatenate([res.results[i]["out"] for i in range(NCORES)], axis=0)
    return out.reshape(B, S, D)


# version 08 (final)
# speedup vs baseline: 1.1541x; 1.1510x over previous
import numpy as np
import ml_dtypes
import concourse.bass as bass
import concourse.mybir as mybir
from concourse.bass_utils import run_bass_kernel_spmd

F32 = mybir.dt.float32
BF16 = mybir.dt.bfloat16
I32 = mybir.dt.int32
AF = mybir.ActivationFunctionType
ALU = mybir.AluOpType
AX = mybir.AxisListType
BF = ml_dtypes.bfloat16

NCORES = 8
B, S, D = 16, 2048, 1024
BPC = B // NCORES
TPC = BPC * S
NTB = S // 128
ND = D // 128
NHEADS, DK = 16, 64
ROPE_THETA = 10000.0
LN_EPS = 1e-5


class KB:

    def __init__(self, name):
        self.nc = bass.Bass("TRN2", target_bir_lowering=False, name=name,
                            detect_race_conditions=False)
        self.sem = {}
        self.cnt = {}
        for nm in ("sdma", "gdma", "pe", "dve", "act", "pool"):
            self.sem[nm] = self.nc.alloc_semaphore("s_" + nm)
            self.cnt[nm] = 0

    def sdma(self, out, in_):
        if self.cnt["sdma"]:
            self.nc.sync.wait_ge(self.sem["sdma"], self.cnt["sdma"])
        self.nc.sync.dma_start(out=out, in_=in_).then_inc(self.sem["sdma"], 16)
        self.cnt["sdma"] += 16

    def gdma(self, out, in_):
        if self.cnt["gdma"]:
            self.nc.gpsimd.wait_ge(self.sem["gdma"], self.cnt["gdma"])
        self.nc.gpsimd.dma_start(out=out, in_=in_).then_inc(self.sem["gdma"], 16)
        self.cnt["gdma"] += 16

    def tick(self, nm, instr, k=1):
        instr.then_inc(self.sem[nm], k)
        self.cnt[nm] += k

    def vop(self, f, *args, **kw):
        self.wait(self.nc.vector, "dve")
        ins = f(*args, **kw)
        self.tick("dve", ins)
        return ins

    def sop(self, f, *args, **kw):
        self.wait(self.nc.scalar, "act")
        ins = f(*args, **kw)
        self.tick("act", ins)
        return ins

    def pop(self, f, *args, **kw):
        self.wait(self.nc.gpsimd, "pool")
        ins = f(*args, **kw)
        self.tick("pool", ins)
        return ins

    def mark(self, nm):
        return (nm, self.cnt[nm])

    def wait_mark(self, eng, mk):
        nm, v = mk
        if v > 0:
            eng.wait_ge(self.sem[nm], v)

    def wait(self, eng, *names):
        for nm in names:
            if self.cnt[nm] > 0:
                eng.wait_ge(self.sem[nm], self.cnt[nm])

    def barrier(self):
        nc = self.nc
        for eng in (nc.sync, nc.gpsimd, nc.tensor, nc.vector, nc.scalar):
            self.wait(eng, "sdma", "gdma", "pe", "dve", "act", "pool")

    def finish(self):
        self.wait(self.nc.sync, "sdma", "gdma")


def _rope_tables():
    pos = np.arange(S, dtype=np.float64)
    expo = np.arange(0, DK, 2, dtype=np.float64) / DK
    inv_freq = ROPE_THETA ** expo
    ang = pos[None, :] / inv_freq[:, None]
    cos = np.repeat(np.cos(ang), 2, axis=0)
    sin = np.repeat(np.sin(ang), 2, axis=0)
    sgn = np.where(np.arange(DK) % 2 == 0, -1.0, 1.0)[:, None]
    cc = np.concatenate([cos, cos], axis=0).astype(BF)
    ss = np.concatenate([sin * sgn, sin * sgn], axis=0).astype(BF)
    r = np.zeros((DK, DK), dtype=BF)
    for j in range(0, DK, 2):
        r[j, j + 1] = 1.0
        r[j + 1, j] = 1.0
    r2 = np.zeros((128, 128), dtype=BF)
    r2[0:64, 0:64] = r
    r2[64:128, 64:128] = r
    return cc, ss, r2


class LNStats:

    def __init__(self, nc, tag, sb=None):
        if sb is None:
            sb = nc.alloc_sbuf_tensor
        self.m = sb(f"ln_m_{tag}", [128, 1], F32)
        self.q = sb(f"ln_q_{tag}", [128, 1], F32)
        self.m2 = sb(f"ln_m2_{tag}", [128, 1], F32)
        self.v = sb(f"ln_v_{tag}", [128, 1], F32)
        self.r = sb(f"ln_r_{tag}", [128, 1], F32)
        self.b = sb(f"ln_b_{tag}", [128, 1], F32)


def ln_tile(kb, xt, out_bf, sq, st, epscol):
    nc = kb.nc
    v, s = nc.vector, nc.scalar
    kb.vop(v.reduce_sum, st.m[:, :], xt, AX.X)
    kb.vop(v.tensor_scalar_mul, st.m[:, :], st.m[:, :], 1.0 / D)
    kb.vop(v.tensor_tensor, st.m2[:, :], st.m[:, :], st.m[:, :], ALU.mult)
    kb.sop(s.activation, sq, xt, AF.Square, accum_out=st.q[:, :])
    kb.wait(v, "act")
    kb.vop(v.tensor_scalar, st.v[:, :], st.q[:, :], 1.0 / D, st.m2[:, 0:1],
           ALU.mult, ALU.subtract)
    kb.wait(s, "dve")
    kb.sop(s.activation, st.v[:, :], st.v[:, :], AF.Ln, bias=epscol[:, 0:1])
    kb.sop(s.activation, st.r[:, :], st.v[:, :], AF.Exp, scale=-0.5)
    kb.wait(v, "act")
    kb.vop(v.tensor_scalar, st.b[:, :], st.m[:, :], st.r[:, 0:1], -1.0,
           ALU.mult, ALU.mult)
    kb.wait(s, "dve")
    kb.sop(s.activation, out_bf, xt, AF.Identity, bias=st.b[:, 0:1], scale=st.r[:, 0:1])



def _scoped_allocs(nc, es, px):
    def sb(name, shape, dtype):
        return es.enter_context(nc.sbuf_tensor(px + name, shape, dtype))
    def ps(name, shape, dtype=F32):
        return es.enter_context(nc.psum_tensor(px + name, shape, dtype))
    return sb, ps


def build_l0_attn(debug_taps=False):
    import contextlib
    kb = KB("l0attn")
    nc = kb.nc
    x = nc.dram_tensor("x", [TPC, D], F32, kind="ExternalInput").ap()
    wqkvT = nc.dram_tensor("wqkvT", [D, 3 * D], F32, kind="ExternalInput").ap()
    woT = nc.dram_tensor("woT", [D, D], F32, kind="ExternalInput").ap()
    x1 = nc.dram_tensor("x1", [TPC, D], F32, kind="ExternalOutput").ap()
    with contextlib.ExitStack() as es:
        emit_l0_attn(kb, es, x, x1, wqkvT, woT, debug_taps=debug_taps)
    kb.finish()
    return kb.nc


def emit_l0_attn(kb, es, x, x1, wqkvT, woT, px="a_", debug_taps=False):
    nc = kb.nc
    cc_np, ss_np, r2_np = _rope_tables()
    consts = {
        "cc": cc_np, "ss": ss_np, "r2": r2_np,
        "ident": np.eye(128, dtype=BF),
        "upmask": (np.arange(128)[:, None] <= np.arange(128)[None, :]).astype(BF),
        "ones": np.ones((128, 128), dtype=BF),
    }
    cap = {k: nc.inline_tensor(v, px + "c_" + k).ap() for k, v in consts.items()}

    sb, ps = _scoped_allocs(nc, es, px)
    wq = [sb(f"wq{i}", [128, 3 * D], BF16) for i in range(ND)]
    wo = [sb(f"wo{i}", [128, D], BF16) for i in range(ND)]
    ccs = sb("ccs", [128, S], BF16)
    sss = sb("sss", [128, S], BF16)
    r2s = sb("r2s", [128, 128], BF16)
    ids = sb("ids", [128, 128], BF16)
    upm = sb("upm", [128, 128], BF16)
    ones = sb("ones1", [128, 128], BF16)
    xt = [sb(f"xt{i}", [128, D], F32) for i in range(2)]
    sq = sb("sq", [128, D], F32)
    xlnb = sb("xlnb", [128, D], BF16)
    xlnT = [sb(f"xlnT{i}", [128, S], BF16) for i in range(ND)]
    vaug = [sb(f"vaug{i}", [128, NHEADS * 65], BF16) for i in range(NTB)]
    qk_raw = sb("qk_raw", [128, S], BF16)
    q2 = sb("q2", [128, S], BF16)
    k2 = sb("k2", [128, S], BF16)
    rt = sb("rt", [128, 512], BF16)
    rt2 = sb("rt2", [128, 512], BF16)
    pt = sb("pt", [128, 512], BF16)
    rb = sb("rb", [128, 512], BF16)
    aoutT = [sb(f"aoutT{i}", [128, S], BF16) for i in range(ND)]
    wsum = sb("wsum", [128, D], F32)
    epscol = sb("epscol", [128, 1], F32)
    st = LNStats(nc, "a", sb)

    p_tr = ps("p_tr", [128, 512], BF16)
    p_qk = ps("p_qk", [128, 512], F32)
    p_rot = ps("p_rot", [128, 512], F32)
    p_s = ps("p_s", [128, 512], F32)
    p_o = ps("p_o", [128, 512], F32)
    p_b = ps("p_b", [128, 512], F32)
    p_w0 = ps("p_w0", [128, 512], F32)
    p_w1 = ps("p_w1", [128, 512], F32)

    kb.sdma(ccs[:, :], cap["cc"][:, :])
    kb.sdma(sss[:, :], cap["ss"][:, :])
    kb.sdma(r2s[:, :], cap["r2"][:, :])
    kb.sdma(ids[:, :], cap["ident"][:, :])
    kb.sdma(upm[:, :], cap["upmask"][:, :])
    kb.sdma(ones[:, :], cap["ones"][:, :])
    for i in range(ND):
        kb.gdma(wq[i][:, :], wqkvT[128 * i:128 * (i + 1), :])
        kb.gdma(wo[i][:, :], woT[128 * i:128 * (i + 1), :])
    kb.vop(nc.vector.memset, epscol[:, :], LN_EPS)
    kb.barrier()

    t, v, s = nc.tensor, nc.vector, nc.scalar
    for b in range(BPC):
        tok0 = b * S
        ln_done = [kb.mark("act"), kb.mark("act")]
        tr_evac = kb.mark("dve")
        v_evac = kb.mark("act")
        for tt in range(NTB):
            xb_ = xt[tt % 2]
            kb.wait_mark(nc.sync, ln_done[tt % 2])
            kb.sdma(xb_[:, :], x[tok0 + 128 * tt: tok0 + 128 * (tt + 1), :])
            kb.wait(v, "sdma")
            kb.wait(s, "sdma")
            kb.wait(s, "pe")
            ln_tile(kb, xb_[:, :], xlnb[:, :], sq[:, :], st, epscol)
            ln_done[tt % 2] = kb.mark("act")
            kb.wait(t, "act")
            for g in range(2):
                kb.wait_mark(t, tr_evac)
                for k_ in range(4):
                    dt = 4 * g + k_
                    ins = t.transpose(p_tr[:, 128 * k_:128 * (k_ + 1)],
                                      xlnb[:, 128 * dt:128 * (dt + 1)], ids[:, :])
                    if k_ == 3:
                        kb.tick("pe", ins)
                kb.wait(v, "pe")
                for k_ in range(4):
                    dt = 4 * g + k_
                    kb.vop(v.tensor_copy, xlnT[dt][:, 128 * tt:128 * (tt + 1)],
                           p_tr[:, 128 * k_:128 * (k_ + 1)])
                tr_evac = kb.mark("dve")
            kb.wait(t, "dve")
            for c in range(2):
                kb.wait_mark(t, v_evac)
                for dt in range(ND):
                    ins = t.matmul(p_qk[:, :], xlnT[dt][:, 128 * tt:128 * (tt + 1)],
                                   wq[dt][:, 2 * D + 512 * c: 2 * D + 512 * (c + 1)],
                                   start=(dt == 0), stop=(dt == ND - 1))
                kb.tick("pe", ins)
                kb.wait(s, "pe")
                dst = vaug[tt][:, :].rearrange("p (h w) -> p h w", w=65)[:, 8 * c:8 * (c + 1), 0:64]
                kb.sop(s.activation, dst, p_qk[:, :].rearrange("p (h w) -> p h w", w=64),
                       AF.Copy)
                v_evac = kb.mark("act")
            kb.wait(v, "act")
            kb.vop(v.memset, vaug[tt][:, :].rearrange("p (h w) -> p h w", w=65)[:, :, 64:65], 1.0)

        kb.barrier()

        po_used = kb.mark("dve")
        pb_used = kb.mark("act")
        qk_evac = kb.mark("act")
        rot_used = kb.mark("dve")
        for hp in range(NHEADS // 2):
            for dst, col0 in ((q2, 128 * hp), (k2, D + 128 * hp)):
                for jq in range(4):
                    kb.wait_mark(t, qk_evac)
                    for dt in range(ND):
                        ins = t.matmul(p_qk[:, :], wq[dt][:, col0:col0 + 128],
                                       xlnT[dt][:, 512 * jq:512 * (jq + 1)],
                                       start=(dt == 0), stop=(dt == ND - 1))
                    kb.tick("pe", ins)
                    kb.wait(s, "pe")
                    kb.wait(s, "dve")
                    kb.sop(s.activation, qk_raw[:, 512 * jq:512 * (jq + 1)],
                           p_qk[:, :], AF.Copy)
                    qk_evac = kb.mark("act")
                    kb.wait(t, "act")
                    kb.wait_mark(t, rot_used)
                    kb.tick("pe", t.matmul(p_rot[:, :], r2s[:, :],
                                           qk_raw[:, 512 * jq:512 * (jq + 1)],
                                           start=True, stop=True))
                    kb.wait(v, "pe")
                    kb.wait(v, "act")
                    sl = slice(512 * jq, 512 * (jq + 1))
                    kb.vop(v.tensor_tensor, rt[:, :], p_rot[:, :], sss[:, sl], ALU.mult)
                    kb.vop(v.tensor_tensor, rt2[:, :], qk_raw[:, sl], ccs[:, sl], ALU.mult)
                    kb.vop(v.tensor_add, dst[:, sl], rt[:, :], rt2[:, :])
                    rot_used = kb.mark("dve")

            for h01 in range(2):
                h = 2 * hp + h01
                base = 64 * h01
                for jq in range(4):
                    nk = 4 * jq + 4
                    for ki in range(nk):
                        kb.wait(t, "dve")
                        kb.wait(t, "act")
                        kb.tick("pe", t.matmul(p_s[:, :],
                                               k2[base:base + 64, 128 * ki:128 * (ki + 1)],
                                               q2[base:base + 64, 512 * jq:512 * (jq + 1)],
                                               start=True, stop=True))
                        kb.wait(s, "pe")
                        kb.sop(s.activation, pt[:, :], p_s[:, :], AF.Exp, scale=0.125)
                        ld = ki - 4 * jq
                        if ld >= 0:
                            kb.wait(v, "act")
                            if ld > 0:
                                kb.vop(v.memset, pt[:, 0:128 * ld], 0.0)
                            kb.vop(v.tensor_tensor, pt[:, 128 * ld:128 * (ld + 1)],
                                   pt[:, 128 * ld:128 * (ld + 1)],
                                   upm[:, :], ALU.mult)
                            kb.wait(t, "dve")
                        else:
                            kb.wait(t, "act")
                        if ki == 0:
                            kb.wait_mark(t, po_used)
                        ins = t.matmul(p_o[0:65, :], vaug[ki][:, 65 * h:65 * h + 65],
                                       pt[:, :], start=(ki == 0), stop=(ki == nk - 1))
                        if ki == nk - 1:
                            kb.tick("pe", ins)
                    kb.wait(v, "pe")
                    with nc.allow_low_precision(reason="bf16 softmax recip within tolerance"):
                        kb.vop(v.reciprocal, rb[64:65, 0:512], p_o[64:65, :])
                    kb.wait(t, "dve")
                    kb.wait_mark(t, pb_used)
                    kb.tick("pe", t.matmul(p_b[0:64, :], ones[64:65, 0:64],
                                           rb[64:65, 0:512], start=True, stop=True))
                    kb.wait(s, "pe")
                    kb.wait_mark(s, po_used)
                    kb.sop(s.activation, rb[0:64, 0:512], p_b[0:64, :], AF.Copy)
                    pb_used = kb.mark("act")
                    kb.wait(v, "act")
                    kb.vop(v.tensor_tensor,
                           aoutT[h // 2][64 * (h % 2):64 * (h % 2) + 64, 512 * jq:512 * (jq + 1)],
                           p_o[0:64, :], rb[0:64, 0:512], ALU.mult)
                    po_used = kb.mark("dve")

        kb.barrier()

        res_done = [kb.mark("dve"), kb.mark("dve")]
        for tt in range(NTB):
            xb_ = xt[tt % 2]
            kb.wait_mark(nc.sync, res_done[tt % 2])
            kb.sdma(xb_[:, :], x[tok0 + 128 * tt: tok0 + 128 * (tt + 1), :])
            kb.wait(t, "dve")
            for c, pw in ((0, p_w0), (1, p_w1)):
                for ft in range(ND):
                    ins = t.matmul(pw[:, :], aoutT[ft][:, 128 * tt:128 * (tt + 1)],
                                   wo[ft][:, 512 * c:512 * (c + 1)],
                                   start=(ft == 0), stop=(ft == ND - 1))
                kb.tick("pe", ins)
            kb.wait(v, "pe")
            kb.wait(v, "sdma")
            kb.vop(v.tensor_add, wsum[:, 0:512], xb_[:, 0:512], p_w0[:, :])
            kb.vop(v.tensor_add, wsum[:, 512:1024], xb_[:, 512:1024], p_w1[:, :])
            res_done[tt % 2] = kb.mark("dve")
            kb.wait(nc.sync, "dve")
            kb.sdma(x1[tok0 + 128 * tt: tok0 + 128 * (tt + 1), :], wsum[:, :])
        kb.barrier()

    if debug_taps:
        taps = {
            "dbg_xlnT0": xlnT[0], "dbg_q2": q2, "dbg_k2": k2,
            "dbg_vaug0": vaug[0], "dbg_aoutT0": aoutT[0],
        }
        for nm, tile in taps.items():
            shp = list(tile.shape) if hasattr(tile, "shape") else [128, S]
            o = nc.dram_tensor(nm, [128, shp[1]], BF16, kind="ExternalOutput").ap()
            kb.sdma(o[:, :], tile[:, :])
        kb.barrier()



def emit_ln_transpose(kb, src_tiles, xlnb, sq, st, epscol, xlnT, ids, p_tr, tr_state):
    nc = kb.nc
    t, v, s = nc.tensor, nc.vector, nc.scalar
    for tt, xb_ in enumerate(src_tiles):
        kb.wait(s, "pe")
        ln_tile(kb, xb_[:, :], xlnb[:, :], sq[:, :], st, epscol)
        kb.wait(t, "act")
        for g in range(2):
            kb.wait_mark(t, tr_state["tr_evac"])
            for k_ in range(4):
                dt = 4 * g + k_
                ins = t.transpose(p_tr[:, 128 * k_:128 * (k_ + 1)],
                                  xlnb[:, 128 * dt:128 * (dt + 1)], ids[:, :])
                if k_ == 3:
                    kb.tick("pe", ins)
            kb.wait(v, "pe")
            for k_ in range(4):
                dt = 4 * g + k_
                kb.vop(v.tensor_copy, xlnT[dt][:, 128 * tt:128 * (tt + 1)],
                       p_tr[:, 128 * k_:128 * (k_ + 1)])
            tr_state["tr_evac"] = kb.mark("dve")


def build_l0_ffn(debug_taps=False):
    import contextlib
    kb = KB("l0ffn")
    nc = kb.nc
    E, FF = 8, 4096
    x = nc.dram_tensor("x", [TPC, D], F32, kind="ExternalInput").ap()
    wr = nc.dram_tensor("wr", [D, E], F32, kind="ExternalInput").ap()
    br = nc.dram_tensor("br", [1, E], F32, kind="ExternalInput").ap()
    w1 = nc.dram_tensor("w1", [E, D, FF], F32, kind="ExternalInput").ap()
    b1r = nc.dram_tensor("b1r", [E, 128, FF // 128], F32, kind="ExternalInput").ap()
    w2 = nc.dram_tensor("w2", [E, FF, D], F32, kind="ExternalInput").ap()
    b2 = nc.dram_tensor("b2", [E, D], F32, kind="ExternalInput").ap()
    x2 = nc.dram_tensor("x2", [TPC, D], F32, kind="ExternalOutput").ap()
    with contextlib.ExitStack() as es:
        emit_l0_ffn(kb, es, x, x2, wr, br, w1, b1r, w2, b2, debug_taps=debug_taps)
    kb.finish()
    return kb.nc


def emit_l0_ffn(kb, es, x, x2, wr, br, w1, b1r, w2, b2, px="f_", debug_taps=False):
    nc = kb.nc
    E, FF = 8, 4096
    tie_np = (np.arange(8)[None, :] <= np.arange(8)[:, None]).astype(np.float32)
    consts = {
        "ident": np.eye(128, dtype=BF),
        "onesrow": np.ones((1, 128), dtype=BF),
        "tie": np.broadcast_to(tie_np.reshape(1, 64), (128, 64)).astype(np.float32),
    }
    cap = {k: nc.inline_tensor(np.ascontiguousarray(v), px + "c_" + k).ap() for k, v in consts.items()}

    sb, ps = _scoped_allocs(nc, es, px)
    ids = sb("ids", [128, 128], BF16)
    onesr = sb("onesr", [1, 128], BF16)
    ties = sb("ties", [128, 64], F32)
    wrs = [sb(f"wrs{i}", [128, E], BF16) for i in range(ND)]
    brs = sb("brs", [1, E], BF16)
    b2s = sb("b2s", [E, D], BF16)
    b1s = sb("b1s", [128, FF // 128], F32)
    acc = [sb(f"acc{i}", [128, D], F32) for i in range(NTB)]
    sq = sb("sq", [128, D], F32)
    xlnb = sb("xlnb", [128, D], BF16)
    xlnT = [sb(f"xlnT{i}", [128, S], BF16) for i in range(ND)]
    hT = [sb(f"hT{i}", [128, S], BF16) for i in range(8)]
    w1q = [[sb(f"w1q{p}_{i}", [128, 1024], BF16) for i in range(ND)] for p in range(2)]
    w2q = [[sb(f"w2q{p}_{i}", [128, D], BF16) for i in range(8)] for p in range(2)]
    lg = sb("lg", [128, E], F32)
    ga = [sb(f"ga{i}", [128, E], F32) for i in range(NTB)]
    gab = sb("gab", [128, E], BF16)
    gT = sb("gT", [E, S], BF16)
    mg = sb("mg", [128, 64], F32)
    eqg = sb("eqg", [128, 64], F32)
    cum = sb("cum", [128, 64], F32)
    kp = sb("kp", [128, E], F32)
    kp2 = sb("kp2", [128, E], F32)
    gmax = sb("gmax", [128, 1], F32)
    gsum = sb("gsum", [128, 1], F32)
    epscol = sb("epscol", [128, 1], F32)
    st = LNStats(nc, "f", sb)

    p_tr = ps("p_tr", [128, 512], BF16)
    p_g = ps("p_g", [128, E], F32)
    p_h0 = ps("p_h0", [128, 512], F32)
    p_h1 = ps("p_h1", [128, 512], F32)
    p_y0 = ps("p_y0", [128, 512], F32)
    p_y1 = ps("p_y1", [128, 512], F32)

    t, v, s = nc.tensor, nc.vector, nc.scalar
    kb.sdma(ids[:, :], cap["ident"][:, :])
    kb.sdma(onesr[:, :], cap["onesrow"][:, :])
    kb.sdma(ties[:, :], cap["tie"][:, :])
    kb.gdma(brs[:, :], br[:, :])
    kb.gdma(b2s[:, :], b2[:, :])
    for i in range(ND):
        kb.gdma(wrs[i][:, :], wr[128 * i:128 * (i + 1), :])
    kb.vop(v.memset, epscol[:, :], LN_EPS)
    kb.barrier()

    for b in range(BPC):
        tok0 = b * S
        for tt in range(NTB):
            kb.sdma(acc[tt][:, :], x[tok0 + 128 * tt: tok0 + 128 * (tt + 1), :])
        kb.barrier()
        tr_state = {"tr_evac": kb.mark("dve")}
        emit_ln_transpose(kb, [a for a in acc], xlnb, sq, st, epscol, xlnT, ids, p_tr, tr_state)
        kb.barrier()

        gt_evac = kb.mark("dve")
        for tt in range(NTB):
            kb.wait(t, "dve")
            for dt in range(ND):
                ins = t.matmul(p_g[:, :], xlnT[dt][:, 128 * tt:128 * (tt + 1)],
                               wrs[dt][:, :], start=(dt == 0), stop=False)
            kb.tick("pe", t.matmul(p_g[:, :], onesr[0:1, :], brs[0:1, :],
                                   start=False, stop=True))
            kb.wait(v, "pe")
            kb.vop(v.tensor_copy, lg[:, :], p_g[:, :])
            kb.vop(v.reduce_max, gmax[:, :], lg[:, :], AX.X, negate=True)
            kb.wait(s, "dve")
            kb.sop(s.activation, lg[:, :], lg[:, :], AF.Exp, bias=gmax[:, 0:1])
            kb.wait(v, "act")
            kb.vop(v.reduce_sum, gsum[:, :], lg[:, :], AX.X)
            kb.vop(v.reciprocal, gsum[:, :], gsum[:, :])
            kb.vop(v.tensor_scalar_mul, lg[:, :], lg[:, :], gsum[:, 0:1])
            A = lg[:, :].unsqueeze(1).broadcast_to((128, 8, 8))
            Bm = lg[:, :].unsqueeze(2).broadcast_to((128, 8, 8))
            mg3 = mg[:, :].rearrange("p (e f) -> p e f", f=8)
            eq3 = eqg[:, :].rearrange("p (e f) -> p e f", f=8)
            kb.vop(v.tensor_tensor, mg3, A, Bm, ALU.is_gt)
            kb.vop(v.tensor_tensor, eq3, A, Bm, ALU.is_equal)
            kb.vop(v.tensor_tensor, eqg[:, :], eqg[:, :], ties[:, :], ALU.mult)
            kb.vop(v.tensor_add, mg[:, :], mg[:, :], eqg[:, :])
            kb.vop(v.tensor_tensor, mg3, mg3, A, ALU.mult)
            kb.vop(v.tensor_reduce, cum[:, 0:8], mg3, AX.X, ALU.add)
            kb.vop(v.tensor_single_scalar, kp[:, :], cum[:, 0:8], 0.8, ALU.is_lt)
            kb.vop(v.tensor_tensor, kp2[:, :], cum[:, 0:8], lg[:, :], ALU.is_equal)
            kb.vop(v.tensor_max, kp[:, :], kp[:, :], kp2[:, :])
            kb.vop(v.tensor_tensor, ga[tt][:, :], lg[:, :], kp[:, :], ALU.mult)
            with nc.allow_low_precision(reason="bf16 gate copy for transpose"):
                kb.vop(v.tensor_copy, gab[:, :], ga[tt][:, :])
            kb.wait(t, "dve")
            kb.wait_mark(t, gt_evac)
            kb.tick("pe", t.transpose(p_tr[0:8, 0:128], gab[:, :], ids[:, :]))
            kb.wait(v, "pe")
            kb.vop(v.tensor_copy, gT[0:8, 128 * tt:128 * (tt + 1)], p_tr[0:8, 0:128])
            gt_evac = kb.mark("dve")
        kb.barrier()

        for tt in range(NTB):
            kb.wait(t, "dve")
            for c, py in ((0, p_y0), (1, p_y1)):
                kb.tick("pe", t.matmul(py[:, :], gT[0:8, 128 * tt:128 * (tt + 1)],
                                       b2s[:, 512 * c:512 * (c + 1)], start=True, stop=True))
            kb.wait(v, "pe")
            kb.vop(v.tensor_add, acc[tt][:, 0:512], acc[tt][:, 0:512], p_y0[:, :])
            kb.vop(v.tensor_add, acc[tt][:, 512:1024], acc[tt][:, 512:1024], p_y1[:, :])
        kb.barrier()

        iters = [(e_, q_) for e_ in range(E) for q_ in range(4)]

        def ffn_issue(e_, qq_, par_):
            f0_ = 1024 * qq_
            kb.wait(nc.gpsimd, "pe")
            for i_ in range(ND):
                kb.gdma(w1q[par_][i_][:, :],
                        w1[e_, 128 * i_:128 * (i_ + 1), f0_:f0_ + 1024])
            for i_ in range(8):
                kb.gdma(w2q[par_][i_][:, :],
                        w2[e_, f0_ + 128 * i_:f0_ + 128 * (i_ + 1), :])
            return kb.mark("gdma")

        ld_mark = [None, None]
        ld_mark[0] = ffn_issue(iters[0][0], iters[0][1], 0)
        cur_e = -1
        for k_, (e, qq) in enumerate(iters):
            par = k_ % 2
            if e != cur_e:
                kb.wait(nc.sync, "act")
                kb.sdma(b1s[:, :], b1r[e, :, :])
                cur_e = e
            if k_ + 1 < len(iters):
                ld_mark[1 - par] = ffn_issue(iters[k_ + 1][0], iters[k_ + 1][1], 1 - par)
            kb.wait_mark(t, ld_mark[par])
            kb.wait(s, "sdma")
            hb = [p_h0, p_h1]
            h_evac = [kb.mark("act"), kb.mark("act")]
            for ft in range(8):
                for tc in range(4):
                    ph = hb[(ft * 4 + tc) % 2]
                    kb.wait_mark(t, h_evac[(ft * 4 + tc) % 2])
                    kb.wait(t, "dve")
                    for dt in range(ND):
                        ins = t.matmul(ph[:, :], w1q[par][dt][:, 128 * ft:128 * (ft + 1)],
                                       xlnT[dt][:, 512 * tc:512 * (tc + 1)],
                                       start=(dt == 0), stop=(dt == ND - 1))
                    kb.tick("pe", ins)
                    kb.wait(s, "pe")
                    fcol = 8 * qq + ft
                    kb.sop(s.activation, hT[ft][:, 512 * tc:512 * (tc + 1)],
                           ph[:, :], AF.Relu, bias=b1s[:, fcol:fcol + 1])
                    h_evac[(ft * 4 + tc) % 2] = kb.mark("act")
            kb.wait(t, "act")
            y_done = kb.mark("dve")
            for tt in range(NTB):
                kb.wait_mark(t, y_done)
                for c, py in ((0, p_y0), (1, p_y1)):
                    for fl in range(8):
                        ins = t.matmul(py[:, :], hT[fl][:, 128 * tt:128 * (tt + 1)],
                                       w2q[par][fl][:, 512 * c:512 * (c + 1)],
                                       start=(fl == 0), stop=(fl == 7))
                    kb.tick("pe", ins)
                kb.wait(v, "pe")
                kb.vop(v.scalar_tensor_tensor, acc[tt][:, 0:512], p_y0[:, :],
                       ga[tt][:, e:e + 1], acc[tt][:, 0:512], ALU.mult, ALU.add)
                kb.vop(v.scalar_tensor_tensor, acc[tt][:, 512:1024], p_y1[:, :],
                       ga[tt][:, e:e + 1], acc[tt][:, 512:1024], ALU.mult, ALU.add)
                y_done = kb.mark("dve")
        kb.barrier()
        for tt in range(NTB):
            kb.wait(nc.sync, "dve")
            kb.sdma(x2[tok0 + 128 * tt: tok0 + 128 * (tt + 1), :], acc[tt][:, :])
        kb.barrier()



def emit_ln_transpose_stream(kb, x_ap, tok0, xt2, xlnb, sq, st, epscol, xlnT, ids,
                             p_tr, tr_state):
    nc = kb.nc
    t, v, s = nc.tensor, nc.vector, nc.scalar
    ln_done = [kb.mark("act"), kb.mark("act")]
    for tt in range(NTB):
        xb_ = xt2[tt % 2]
        kb.wait_mark(nc.sync, ln_done[tt % 2])
        kb.sdma(xb_[:, :], x_ap[tok0 + 128 * tt: tok0 + 128 * (tt + 1), :])
        kb.wait(v, "sdma")
        kb.wait(s, "sdma")
        kb.wait(s, "pe")
        ln_tile(kb, xb_[:, :], xlnb[:, :], sq[:, :], st, epscol)
        ln_done[tt % 2] = kb.mark("act")
        kb.wait(t, "act")
        for g in range(2):
            kb.wait_mark(t, tr_state["tr_evac"])
            for k_ in range(4):
                dt = 4 * g + k_
                ins = t.transpose(p_tr[:, 128 * k_:128 * (k_ + 1)],
                                  xlnb[:, 128 * dt:128 * (dt + 1)], ids[:, :])
                if k_ == 3:
                    kb.tick("pe", ins)
            kb.wait(v, "pe")
            for k_ in range(4):
                dt = 4 * g + k_
                kb.vop(v.tensor_copy, xlnT[dt][:, 128 * tt:128 * (tt + 1)],
                       p_tr[:, 128 * k_:128 * (k_ + 1)])
            tr_state["tr_evac"] = kb.mark("dve")


def build_l1_attn(debug_taps=False, stop=None):
    import contextlib
    kb = KB("mamba")
    nc = kb.nc
    x = nc.dram_tensor("x", [TPC, D], F32, kind="ExternalInput").ap()
    winT = nc.dram_tensor("winT", [D, 4384], F32, kind="ExternalInput").ap()
    convw = nc.dram_tensor("convw", [2304, 4], F32, kind="ExternalInput").ap()
    convb = nc.dram_tensor("convb", [2304, 1], F32, kind="ExternalInput").ap()
    dtb = nc.dram_tensor("dtb", [32, 1], F32, kind="ExternalInput").ap()
    alog = nc.dram_tensor("alog", [32, 1], F32, kind="ExternalInput").ap()
    dpar = nc.dram_tensor("dpar", [32, 1], F32, kind="ExternalInput").ap()
    nwr = nc.dram_tensor("nwr", [128, 16], F32, kind="ExternalInput").ap()
    woutT = nc.dram_tensor("woutT", [2 * D, D], F32, kind="ExternalInput").ap()
    x3 = nc.dram_tensor("x3", [TPC, D], F32, kind="ExternalOutput").ap()
    with contextlib.ExitStack() as es:
        emit_l1_attn(kb, es, x, x3, winT, convw, convb, dtb, alog, dpar, nwr, woutT,
                     debug_taps=debug_taps, stop=stop)
    kb.finish()
    return kb.nc


def emit_l1_attn(kb, es, x, x3, winT, convw, convb, dtb, alog, dpar, nwr, woutT,
                 px="m_", debug_taps=False, stop=None):
    nc = kb.nc
    DI = 2 * D
    NH, HP, NST = 32, 64, 128
    CONVD = DI + 2 * NST
    CH = 64
    NCH = S // CH
    PAD = 3
    seg = np.ones((NH, S), dtype=BF); seg[:, ::CH] = 0.0
    consts = {
        "ident": np.eye(128, dtype=BF),
        "identf": np.eye(128, dtype=np.float32),
        "onesrow": np.ones((1, 128), dtype=BF),
        "onesrowf": np.ones((1, 128), dtype=np.float32),
        "segmask": seg,
        "bigmask": (1e9 * (np.arange(64)[:, None] > np.arange(64)[None, :])).astype(np.float32),
        "upm64": (np.arange(64)[:, None] <= np.arange(64)[None, :]).astype(np.float32),
    }
    cap = {k: nc.inline_tensor(np.ascontiguousarray(v), px + "c_" + k).ap() for k, v in consts.items()}

    sb, ps = _scoped_allocs(nc, es, px)
    ids = sb("ids", [128, 128], BF16)
    idf = sb("idf", [128, 128], F32)
    onesr = sb("onesr", [1, 128], BF16)
    onesrf = sb("onesrf", [1, 128], F32)
    segs = sb("segs", [NH, S], BF16)
    bigm = sb("bigm", [64, 64], F32)
    upm64 = sb("upm64", [64, 64], F32)
    xt2 = [sb(f"xt{i}", [128, D], F32) for i in range(2)]
    sq = sb("sq", [128, D], F32)
    xlnb = sb("xlnb", [128, D], BF16)
    xlnT = [sb(f"xlnT{i}", [128, S], BF16) for i in range(ND)]
    winc = [[sb(f"winc{p}_{i}", [128, 128], BF16) for i in range(ND)] for p in range(2)]
    xc = [sb(f"xc{i}", [128, PAD + S], BF16) for i in range(18)]
    cw = [sb(f"cw{i}", [128, 4], F32) for i in range(18)]
    cb = [sb(f"cb{i}", [128, 1], F32) for i in range(18)]
    cs1 = sb("cs1", [128, S], F32)
    dtv = sb("dtv", [NH, S], F32)
    dacs = sb("dacs", [NH, S], F32)
    dtbc = sb("dtbc", [NH, 1], F32)
    acol = sb("acol", [NH, 1], F32)
    onecol = sb("onecol", [NH, 1], F32)
    dpb = sb("dpb", [64, NH], F32)
    dacs_tm = sb("dacs_tm", [128, 512], F32)
    dtv_tm = sb("dtv_tm", [128, 512], F32)
    dtq = sb("dtq", [64, NH], F32)
    dc = sb("dc", [64, NH], F32)
    ea = sb("ea", [64, NH], F32)
    Gm = sb("Gm", [64, 64], BF16)
    dpx = sb("dpx", [64, DI], BF16)
    btm = sb("btm", [64, 128], BF16)
    cdrow = sb("cdrow", [1, NH], F32)
    yscr = sb("yscr", [64, 512], F32)
    zs = sb("zs", [128, S], BF16)
    rb128 = sb("rb128", [128, S], BF16)
    nws = sb("nws", [128, DI // 128], F32)
    wout = [sb(f"wout{i}", [128, D], BF16) for i in range(16)]
    epscol = sb("epscol", [128, 1], F32)
    onescol = sb("onescol", [128, 1], BF16)
    st = LNStats(nc, "m", sb)
    a0 = nc.lookup_mloc(xlnT[0]).addr
    at = lambda name, shape, dtype, offset: nc.alloc_sbuf_tensor_at(
        px + name, shape, dtype, offset=offset)
    A2f = at("A2f", [64, DI], F32, offset=a0)
    statef = at("statef", [128, DI], F32, offset=a0 + 8192)
    stateb = at("stateb", [128, DI], BF16, offset=a0 + 16384)
    x_tm = at("x_tm", [64, DI], BF16, offset=a0 + 20480)
    dtx = at("dtx", [64, DI], BF16, offset=a0 + 24576)
    MT = at("MT", [64, DI], BF16, offset=a0 + 28672)
    dtxd = x_tm
    zaddr = nc.lookup_mloc(zs).addr
    yh = at("yh", [64, DI], BF16, offset=zaddr)
    cdb64 = cs1
    dtm0 = nc.lookup_mloc(dacs_tm).addr
    assert nc.lookup_mloc(dtv_tm).addr == dtm0 + 2048
    wsum = at("wsum", [128, D], F32, offset=dtm0)
    rrowb = at("rrowb", [1, S], BF16, offset=dtm0)
    zdram = nc.dram_tensor(px + "zdram", [DI, S], BF16, kind="Internal").ap()

    p_trb = ps("p_trb", [128, 512], BF16)
    p_trf = ps("p_trf", [128, 512], F32)
    p_ya = [ps(f"p_ya{i}", [128, 512], F32) for i in range(2)]
    p_yo = [ps(f"p_yo{i}", [128, 512], F32) for i in range(2)]
    p_st = [ps(f"p_st{i}", [128, 512], F32) for i in range(2)]

    t, v, s = nc.tensor, nc.vector, nc.scalar
    kb.sdma(ids[:, :], cap["ident"][:, :])
    kb.sdma(idf[:, :], cap["identf"][:, :])
    kb.sdma(onesr[:, :], cap["onesrow"][:, :])
    kb.sdma(onesrf[:, :], cap["onesrowf"][:, :])
    kb.sdma(segs[:, :], cap["segmask"][:, :])
    kb.sdma(bigm[:, :], cap["bigmask"][:, :])
    kb.sdma(upm64[:, :], cap["upm64"][:, :])
    kb.sdma(nws[:, :], nwr[:, :])
    for j in range(18):
        kb.sdma(cw[j][:, :], convw[128 * j:128 * (j + 1), :])
        kb.sdma(cb[j][:, :], convb[128 * j:128 * (j + 1), :])
    kb.sdma(dtbc[:, :], dtb[:, :])
    kb.sdma(acol[:, :], alog[:, :])
    for j in range(16):
        kb.gdma(wout[j][:, :], woutT[128 * j:128 * (j + 1), :])
    kb.vop(v.memset, epscol[:, :], LN_EPS)
    kb.vop(v.memset, onecol[:, :], 1.0)
    with nc.allow_low_precision(reason="ones"):
        kb.vop(v.memset, onescol[:, :], 1.0)
    kb.barrier()
    kb.sop(s.activation, acol[:, :], acol[:, :], AF.Exp)
    kb.wait(v, "act")
    kb.vop(v.tensor_scalar_mul, acol[:, :], acol[:, :], -1.0)
    kb.sdma(dtq[0:NH, 0:1], dpar[:, :])
    kb.barrier()
    kb.tick("pe", t.transpose(p_trf[0:1, 0:NH], dtq[0:NH, 0:1], idf[0:NH, 0:NH]))
    kb.wait(v, "pe")
    kb.vop(v.tensor_copy, cdrow[0:1, 0:NH], p_trf[0:1, 0:NH])
    kb.wait(t, "dve")
    kb.tick("pe", t.matmul(p_trf[0:64, 0:NH], onesrf[0:1, 0:64], cdrow[0:1, 0:NH],
                           start=True, stop=True))
    kb.wait(v, "pe")
    kb.vop(v.tensor_copy, dpb[:, :], p_trf[0:64, 0:NH])
    kb.barrier()

    def load_win_cols(c0, n, par):
        kb.wait(nc.gpsimd, "pe")
        for i in range(ND):
            kb.gdma(winc[par][i][:, 0:n], winT[128 * i:128 * (i + 1), c0:c0 + n])
        return kb.mark("gdma")

    for b in range(BPC):
        tok0 = b * S
        tr_state = {"tr_evac": kb.mark("dve")}
        emit_ln_transpose_stream(kb, x, tok0, xt2, xlnb, sq, st, epscol, xlnT, ids,
                                 p_trb, tr_state)
        kb.barrier()
        if stop == "ln":
            break

        ldm = [None, None]
        ldm[0] = load_win_cols(2 * D, 128, 0)
        for j in range(18):
            par = j % 2
            kb.vop(v.memset, xc[j][:, 0:PAD], 0.0)
            if j + 1 < 18:
                ldm[1 - par] = load_win_cols(2 * D + 128 * (j + 1), 128, 1 - par)
            kb.wait_mark(t, ldm[par])
            ev = kb.mark("act")
            for tc in range(4):
                pq = p_st[tc % 2]
                kb.wait_mark(t, ev)
                for dt_ in range(ND):
                    ins = t.matmul(pq[:, :], winc[par][dt_][:, 0:128],
                                   xlnT[dt_][:, 512 * tc:512 * (tc + 1)],
                                   start=(dt_ == 0), stop=(dt_ == ND - 1))
                kb.tick("pe", ins)
                kb.wait(s, "pe")
                kb.sop(s.activation, xc[j][:, PAD + 512 * tc:PAD + 512 * (tc + 1)],
                       pq[:, :], AF.Copy)
                ev = kb.mark("act")
        ldm[0] = load_win_cols(0, 128, 0)
        for j in range(16):
            par = j % 2
            if j + 1 < 16:
                ldm[1 - par] = load_win_cols(128 * (j + 1), 128, 1 - par)
            kb.wait_mark(t, ldm[par])
            ev = kb.mark("act")
            for tc in range(4):
                pq = p_ya[tc % 2]
                kb.wait_mark(t, ev)
                for dt_ in range(ND):
                    ins = t.matmul(pq[:, :], winc[par][dt_][:, 0:128],
                                   xlnT[dt_][:, 512 * tc:512 * (tc + 1)],
                                   start=(dt_ == 0), stop=(dt_ == ND - 1))
                kb.tick("pe", ins)
                kb.wait(s, "pe")
                kb.wait(s, "sdma")
                kb.sop(s.activation, zs[:, 512 * tc:512 * (tc + 1)], pq[:, :], AF.Silu)
                ev = kb.mark("act")
            kb.wait(nc.sync, "act")
            kb.sdma(zdram[128 * j:128 * (j + 1), :], zs[:, :])
        kb.barrier()
        load_win_cols(2 * D + 2304, 32, 0)
        kb.barrier()
        ev = kb.mark("dve")
        for tc in range(4):
            pq = p_st[tc % 2]
            kb.wait_mark(t, ev)
            for dt_ in range(ND):
                ins = t.matmul(pq[0:NH, :], winc[0][dt_][:, 0:32],
                               xlnT[dt_][:, 512 * tc:512 * (tc + 1)],
                               start=(dt_ == 0), stop=(dt_ == ND - 1))
            kb.tick("pe", ins)
            kb.wait(v, "pe")
            kb.vop(v.tensor_copy, dtv[:, 512 * tc:512 * (tc + 1)], pq[0:NH, :])
            ev = kb.mark("dve")
        kb.barrier()

        if stop == "m1":
            break
        for j in range(18):
            kb.wait(v, "act")
            kb.vop(v.tensor_scalar_mul, cs1[:, :], xc[j][:, 0:S], cw[j][:, 0:1])
            for k_ in range(1, 4):
                kb.vop(v.scalar_tensor_tensor, cs1[:, :], xc[j][:, k_:k_ + S],
                       cw[j][:, k_:k_ + 1], cs1[:, :], ALU.mult, ALU.add)
            kb.wait(s, "dve")
            kb.sop(s.activation, xc[j][:, PAD:PAD + S], cs1[:, :], AF.Silu,
                   bias=cb[j][:, 0:1])
        kb.barrier()

        if stop == "m2":
            break
        kb.vop(v.tensor_scalar_add, dtv[:, :], dtv[:, :], dtbc[:, 0:1])
        kb.vop(v.tensor_scalar_max, dacs[:, :], dtv[:, :], 0.0)
        kb.wait(s, "dve")
        kb.sop(s.activation, cs1[0:NH, :], dtv[:, :], AF.Abs)
        kb.sop(s.activation, cs1[0:NH, :], cs1[0:NH, :], AF.Exp, scale=-1.0)
        kb.sop(s.activation, cs1[0:NH, :], cs1[0:NH, :], AF.Ln, bias=onecol[0:NH, 0:1])
        kb.wait(v, "act")
        kb.vop(v.tensor_add, dtv[:, :], dacs[:, :], cs1[0:NH, :])
        kb.vop(v.tensor_scalar_mul, cs1[0:NH, :], dtv[:, :], acol[:, 0:1])
        kb.vop(v.tensor_tensor_scan, dacs[:, :], segs[:, :], cs1[0:NH, :], 0.0,
               ALU.mult, ALU.add)
        ev = kb.mark("dve")
        for tt in range(NTB):
            kb.wait(t, "dve")
            kb.wait(t, "act")
            kb.wait_mark(t, ev)
            t.transpose(p_trf[:, 0:NH], dacs[0:NH, 128 * tt:128 * (tt + 1)],
                        idf[0:NH, 0:NH])
            kb.tick("pe", t.transpose(p_trf[:, 64:64 + NH],
                                      dtv[0:NH, 128 * tt:128 * (tt + 1)],
                                      idf[0:NH, 0:NH]))
            kb.wait(v, "pe")
            kb.vop(v.tensor_copy, dacs_tm[:, 32 * tt:32 * (tt + 1)], p_trf[:, 0:NH])
            kb.vop(v.tensor_copy, dtv_tm[:, 32 * tt:32 * (tt + 1)], p_trf[:, 64:64 + NH])
            ev = kb.mark("dve")
        kb.barrier()

        if stop == "m3":
            break
        kb.vop(v.memset, statef[:, :], 0.0)
        kb.vop(v.memset, stateb[:, :], 0.0)
        kb.barrier()

        for c in range(NCH):
            ttc, r0 = c // 2, 64 * (c % 2)
            cs_ = slice(PAD + CH * c, PAD + CH * (c + 1))
            ev = kb.mark("dve")
            for jg in range(4):
                kb.wait(t, "dve")
                kb.wait_mark(t, ev)
                for k_ in range(4):
                    j = 4 * jg + k_
                    ins = t.transpose(p_trb[0:64, 128 * k_:128 * (k_ + 1)],
                                      xc[j][:, cs_], ids[:, :])
                kb.tick("pe", ins)
                kb.wait(v, "pe")
                for k_ in range(4):
                    j = 4 * jg + k_
                    kb.vop(v.tensor_copy, x_tm[:, 128 * j:128 * (j + 1)],
                           p_trb[0:64, 128 * k_:128 * (k_ + 1)])
                ev = kb.mark("dve")
            kb.wait(t, "dve")
            t.transpose(p_trb[0:64, 0:128], xc[16][:, cs_], ids[:, :])
            t.matmul(p_trf[0:64, 0:64], xc[16][:, cs_], xc[17][:, cs_],
                     start=True, stop=True)
            kb.tick("pe", t.transpose(p_trf[0:1, 64:64 + NH],
                                      dacs[0:NH, CH * c + 63:CH * c + 64],
                                      idf[0:NH, 0:NH]))
            kb.wait(v, "pe")
            kb.vop(v.tensor_copy, btm[:, :], p_trb[0:64, 0:128])
            with nc.allow_low_precision(reason="bf16 G"):
                kb.vop(v.tensor_tensor, Gm[:, :], p_trf[0:64, 0:64], upm64[:, :], ALU.mult)
            kb.vop(v.tensor_copy, cdrow[0:1, 0:NH], p_trf[0:1, 64:64 + NH])
            kb.wait(s, "dve")
            kb.sop(s.activation, cdrow[0:1, 0:NH], cdrow[0:1, 0:NH], AF.Exp)
            kb.wait(v, "act")
            kb.vop(v.tensor_copy,
                   cdb64[0:1, 0:DI].rearrange("o (h q) -> o h q", q=HP),
                   cdrow[0:1, 0:NH].unsqueeze(2).broadcast_to((1, NH, HP)))
            kb.vop(v.tensor_copy, dtq[:, :], dtv_tm[r0:r0 + 64, 32 * ttc:32 * (ttc + 1)])
            x3d = x_tm[:, :].rearrange("p (h q) -> p h q", q=HP)
            kb.wait(v, "dve")
            with nc.allow_low_precision(reason="bf16 dtx"):
                kb.vop(v.tensor_tensor, dtx[:, :].rearrange("p (h q) -> p h q", q=HP),
                       x3d, dtq[:, :].unsqueeze(2).broadcast_to((64, NH, HP)), ALU.mult)
            kb.vop(v.tensor_copy, A2f[:, :].rearrange("p (h q) -> p h q", q=HP),
                   dacs_tm[r0:r0 + 64, 32 * ttc:32 * (ttc + 1)].unsqueeze(2)
                   .broadcast_to((64, NH, HP)))
            kb.wait(s, "dve")
            kb.sop(s.activation, ea[:, :],
                   dacs_tm[r0:r0 + 64, 32 * ttc:32 * (ttc + 1)], AF.Exp)
            evg = kb.mark("dve")
            for hg in range(4):
                kb.wait(t, "dve")
                kb.wait_mark(t, evg)
                for k_ in range(8):
                    h = 8 * hg + k_
                    ins = t.transpose(p_trf[0:64, 64 * k_:64 * (k_ + 1)],
                                      A2f[:, 64 * h:64 * (h + 1)], idf[0:64, 0:64])
                kb.tick("pe", ins)
                kb.wait(v, "pe")
                kb.vop(v.tensor_sub, A2f[:, 512 * hg:512 * (hg + 1)],
                       p_trf[0:64, :], A2f[:, 512 * hg:512 * (hg + 1)])
                kb.vop(v.tensor_sub,
                       A2f[:, 512 * hg:512 * (hg + 1)].rearrange("p (h q) -> p h q", q=HP),
                       A2f[:, 512 * hg:512 * (hg + 1)].rearrange("p (h q) -> p h q", q=HP),
                       bigm[:, :].unsqueeze(1).broadcast_to((64, 8, HP)))
                evg = kb.mark("dve")
            kb.wait(s, "dve")
            kb.sop(s.activation, A2f[:, :], A2f[:, :], AF.Exp)
            kb.wait(v, "act")
            with nc.allow_low_precision(reason="bf16 SSD L"):
                kb.vop(v.tensor_tensor, MT[:, :].rearrange("p (h q) -> p h q", q=HP),
                       A2f[:, :].rearrange("p (h q) -> p h q", q=HP),
                       Gm[:, :].unsqueeze(1).broadcast_to((64, NH, HP)), ALU.mult)
            kb.vop(v.tensor_copy, dc[:, :].unsqueeze(2),
                   A2f[:, :].rearrange("p (h q) -> p h q", q=HP)[:, :, 63:64])
            with nc.allow_low_precision(reason="bf16 dpx"):
                kb.vop(v.tensor_tensor, dpx[:, :].rearrange("p (h q) -> p h q", q=HP),
                       x3d, dpb[:, :].unsqueeze(2).broadcast_to((64, NH, HP)), ALU.mult)
            with nc.allow_low_precision(reason="bf16 dtxd"):
                kb.vop(v.tensor_tensor, dtxd[:, :].rearrange("p (h q) -> p h q", q=HP),
                       dtx[:, :].rearrange("p (h q) -> p h q", q=HP),
                       dc[:, :].unsqueeze(2).broadcast_to((64, NH, HP)), ALU.mult)
            for Hh in range(2):
                h0 = 16 * Hh
                kb.wait(t, "dve")
                for k_ in range(16):
                    h = h0 + k_
                    t.matmul(p_ya[k_ // 8][0:64, 64 * (k_ % 8):64 * (k_ % 8 + 1)],
                             MT[:, 64 * h:64 * (h + 1)], dtx[:, 64 * h:64 * (h + 1)],
                             start=True, stop=True)
                for k_ in range(16):
                    h = h0 + k_
                    ins = t.matmul(p_yo[k_ // 8][0:64, 64 * (k_ % 8):64 * (k_ % 8 + 1)],
                                   xc[17][:, cs_], stateb[:, 64 * h:64 * (h + 1)],
                                   start=True, stop=True)
                kb.tick("pe", ins)
                for sg_ in range(2):
                    ins = t.matmul(p_st[sg_][:, :], btm[:, :],
                                   dtxd[:, 1024 * Hh + 512 * sg_:1024 * Hh + 512 * (sg_ + 1)],
                                   start=True, stop=True)
                kb.tick("pe", ins)
                kb.wait(v, "pe")
                for sg_ in range(2):
                    so = 1024 * Hh + 512 * sg_
                    kb.vop(v.tensor_tensor, yscr[:, :].rearrange("p (h q) -> p h q", q=HP),
                           p_yo[sg_][0:64, :].rearrange("p (h q) -> p h q", q=HP),
                           ea[:, h0 + 8 * sg_:h0 + 8 * (sg_ + 1)].unsqueeze(2)
                           .broadcast_to((64, 8, HP)), ALU.mult)
                    kb.vop(v.tensor_add, yscr[:, :], yscr[:, :], p_ya[sg_][0:64, :])
                    with nc.allow_low_precision(reason="bf16 yh"):
                        kb.vop(v.tensor_add, yh[:, so:so + 512], yscr[:, :],
                               dpx[:, so:so + 512])
                kb.wait(t, "dve")
                for sg_ in range(2):
                    so = 1024 * Hh + 512 * sg_
                    ins = t.matmul(p_ya[sg_][:, :], onesrf[0:1, :],
                                   cdb64[0:1, so:so + 512], start=True, stop=True)
                kb.tick("pe", ins)
                kb.wait(v, "pe")
                for sg_ in range(2):
                    so = 1024 * Hh + 512 * sg_
                    sl4 = slice(so, so + 512)
                    kb.vop(v.tensor_tensor, statef[:, sl4], statef[:, sl4],
                           p_ya[sg_][:, :], ALU.mult)
                    kb.vop(v.tensor_add, statef[:, sl4], statef[:, sl4],
                           p_st[sg_][:, :])
                    with nc.allow_low_precision(reason="bf16 state copy"):
                        kb.vop(v.tensor_copy, stateb[:, sl4], statef[:, sl4])
            ev = kb.mark("dve")
            for jg in range(2):
                kb.wait(t, "dve")
                kb.wait_mark(t, ev)
                for k_ in range(8):
                    j = 8 * jg + k_
                    ins = t.transpose(p_trb[:, 64 * k_:64 * (k_ + 1)],
                                      yh[:, 128 * j:128 * (j + 1)], ids[0:64, 0:64])
                kb.tick("pe", ins)
                kb.wait(v, "pe")
                for k_ in range(8):
                    j = 8 * jg + k_
                    kb.vop(v.tensor_copy, xc[j][:, cs_], p_trb[:, 64 * k_:64 * (k_ + 1)])
                ev = kb.mark("dve")
        kb.barrier()

        if stop == "m4":
            break
        kb.vop(v.memset, cs1[0:1, 0:S], 0.0)
        for j in range(16):
            kb.wait(nc.sync, "pe")
            kb.wait(nc.sync, "dve")
            kb.sdma(zs[:, :], zdram[128 * j:128 * (j + 1), :])
            kb.wait(v, "sdma")
            with nc.allow_low_precision(reason="bf16 gated stream"):
                kb.vop(v.tensor_tensor, xc[j][:, PAD:PAD + S], xc[j][:, PAD:PAD + S],
                       zs[:, :], ALU.mult)
            kb.wait(s, "dve")
            kb.sop(s.activation, zs[:, :], xc[j][:, PAD:PAD + S], AF.Square)
            kb.wait(t, "act")
            for tc in range(4):
                kb.tick("pe", t.matmul(p_trf[0:1, 0:512], onescol[:, 0:1],
                                       zs[:, 512 * tc:512 * (tc + 1)],
                                       start=True, stop=True))
                kb.wait(v, "pe")
                kb.vop(v.tensor_add, cs1[0:1, 512 * tc:512 * (tc + 1)],
                       cs1[0:1, 512 * tc:512 * (tc + 1)], p_trf[0:1, 0:512])
                kb.wait(t, "dve")
        kb.barrier()
        if stop == "m5g":
            break
        kb.sop(s.activation, cs1[0:1, 0:S], cs1[0:1, 0:S], AF.Ln,
               scale=1.0 / DI, bias=epscol[0:1, 0:1])
        kb.sop(s.activation, cs1[0:1, 0:S], cs1[0:1, 0:S], AF.Exp, scale=-0.5)
        kb.wait(v, "act")
        with nc.allow_low_precision(reason="bf16 rstd"):
            kb.vop(v.tensor_copy, rrowb[0:1, :], cs1[0:1, 0:S])
        kb.wait(t, "dve")
        ev = kb.mark("dve")
        for tc in range(4):
            kb.wait_mark(t, ev)
            kb.tick("pe", t.matmul(p_trf[:, 0:512], onesr[0:1, :],
                                   rrowb[0:1, 512 * tc:512 * (tc + 1)],
                                   start=True, stop=True))
            kb.wait(v, "pe")
            with nc.allow_low_precision(reason="bf16 rstd bcast"):
                kb.vop(v.tensor_copy, rb128[:, 512 * tc:512 * (tc + 1)], p_trf[:, 0:512])
            ev = kb.mark("dve")
        for j in range(16):
            with nc.allow_low_precision(reason="bf16 gn"):
                kb.vop(v.scalar_tensor_tensor, xc[j][:, PAD:PAD + S],
                       xc[j][:, PAD:PAD + S], nws[:, j:j + 1], rb128[:, :],
                       ALU.mult, ALU.mult)
        kb.barrier()
        res_done = [kb.mark("dve"), kb.mark("dve")]
        for tt in range(NTB):
            xb_ = xt2[tt % 2]
            kb.wait_mark(nc.sync, res_done[tt % 2])
            kb.sdma(xb_[:, :], x[tok0 + 128 * tt: tok0 + 128 * (tt + 1), :])
            kb.wait(t, "dve")
            for cc_, pw in ((0, p_st[0]), (1, p_st[1])):
                for j in range(16):
                    ins = t.matmul(pw[:, :], xc[j][:, PAD + 128 * tt:PAD + 128 * (tt + 1)],
                                   wout[j][:, 512 * cc_:512 * (cc_ + 1)],
                                   start=(j == 0), stop=(j == 15))
                kb.tick("pe", ins)
            kb.wait(v, "pe")
            kb.wait(v, "sdma")
            kb.vop(v.tensor_add, wsum[:, 0:512], xb_[:, 0:512], p_st[0][:, :])
            kb.vop(v.tensor_add, wsum[:, 512:1024], xb_[:, 512:1024], p_st[1][:, :])
            res_done[tt % 2] = kb.mark("dve")
            kb.wait(nc.sync, "dve")
            kb.sdma(x3[tok0 + 128 * tt: tok0 + 128 * (tt + 1), :], wsum[:, :])
        kb.barrier()
        if stop == "m5w":
            break
    if debug_taps:
        groups = {
            "ln": {"d_xlnT0": xlnT[0]},
            "m1": {"d_xlnT0": xlnT[0], "d_xc5": xc[5], "d_dtv": dtv},
            "m2": {"d_xc5": xc[5], "d_xc16": xc[16], "d_xc17": xc[17]},
            "m3": {"d_dtv": dtv, "d_dacs": dacs, "d_dacs_tm": dacs_tm},
            "m4": {"d_xc5": xc[5], "d_yh": yh, "d_statef": statef, "d_xtm": x_tm,
                   "d_dtx": dtx, "d_MT": MT, "d_A2f": A2f, "d_Gm": Gm, "d_ea": ea,
                   "d_dc": dc, "d_dpb": dpb, "d_dacs_tm": dacs_tm},
            "m5g": {"d_xc5": xc[5], "d_cs1": cs1},
            "m5w": {"d_rb128": rb128, "d_xc5": xc[5], "d_cs1": cs1},
        }
        taps = groups.get(stop, {"d_xlnT0": xlnT[0]})
        for nm, tile in taps.items():
            shp = list(tile.shape)
            o = nc.dram_tensor(nm, shp, tile.dtype, kind="ExternalOutput").ap()
            kb.sdma(o[:, :], tile[:, :])
        kb.barrier()


def build_l1_gate():
    import contextlib
    kb = KB("l1gate")
    nc = kb.nc
    E = 16
    x = nc.dram_tensor("x", [TPC, D], F32, kind="ExternalInput").ap()
    gwT = nc.dram_tensor("gwT", [D, E], F32, kind="ExternalInput").ap()
    o_xln = nc.dram_tensor("o_xln", [TPC, D], BF16, kind="ExternalOutput").ap()
    o_oh = nc.dram_tensor("o_oh", [TPC, E], F32, kind="ExternalOutput").ap()
    o_loc = nc.dram_tensor("o_loc", [TPC, 1], F32, kind="ExternalOutput").ap()
    o_pmax = nc.dram_tensor("o_pmax", [TPC, 1], F32, kind="ExternalOutput").ap()
    o_eidx = nc.dram_tensor("o_eidx", [TPC, 1], F32, kind="ExternalOutput").ap()
    o_cnt = nc.dram_tensor("o_cnt", [1, E], F32, kind="ExternalOutput").ap()
    with contextlib.ExitStack() as es:
        emit_l1_gate(kb, es, x, o_xln, o_oh, o_loc, o_pmax, o_eidx, o_cnt, gwT)
    kb.finish()
    return kb.nc


def emit_l1_gate(kb, es, x, o_xln, o_oh, o_loc, o_pmax, o_eidx, o_cnt, gwT, px="g_"):
    nc = kb.nc
    E = 16
    consts = {
        "ident": np.eye(128, dtype=BF),
        "ustrict": (np.arange(128)[:, None] < np.arange(128)[None, :]).astype(BF),
        "iota16": np.broadcast_to(np.arange(16, dtype=np.float32), (128, 16)).copy(),
        "ones16": np.ones((128, 16), dtype=np.float32),
    }
    cap = {k: nc.inline_tensor(np.ascontiguousarray(v), px + "c_" + k).ap() for k, v in consts.items()}

    sb, ps = _scoped_allocs(nc, es, px)
    ids = sb("ids", [128, 128], BF16)
    ustr = sb("ustr", [128, 128], BF16)
    iot = sb("iot", [128, E], F32)
    on16 = sb("on16", [128, E], F32)
    onescol = sb("onescol", [128, 1], BF16)
    onesrf = sb("onesrf2", [1, 128], F32)
    gws = [sb(f"gws{i}", [128, E], BF16) for i in range(ND)]
    xt2 = [sb(f"xt{i}", [128, D], F32) for i in range(2)]
    sq = sb("sq", [128, D], F32)
    xlnb = sb("xlnb", [128, D], BF16)
    xlt = [sb(f"xlt{i}", [128, 128], BF16) for i in range(ND)]
    lg = sb("lg", [128, E], F32)
    ohb = sb("ohb", [128, E], BF16)
    ohf = sb("ohf", [128, E], F32)
    cum = sb("cum", [128, E], F32)
    loc = sb("loc", [128, E], F32)
    gmax = sb("gmax", [128, 1], F32)
    gsum = sb("gsum", [128, 1], F32)
    loccol = sb("loccol", [128, 1], F32)
    eidxc = sb("eidxc", [128, 1], F32)
    roff = sb("roff", [1, E], F32)
    epscol = sb("epscol", [128, 1], F32)
    st = LNStats(nc, "g", sb)

    p_tr = ps("p_tr", [128, 512], BF16)
    p_g = ps("p_g", [128, E], F32)
    p_w = ps("p_w", [128, E], F32)
    p_b = ps("p_b", [128, E], F32)
    p_r = ps("p_r", [1, E], F32)

    t, v, s = nc.tensor, nc.vector, nc.scalar
    kb.sdma(ids[:, :], cap["ident"][:, :])
    kb.sdma(ustr[:, :], cap["ustrict"][:, :])
    kb.sdma(iot[:, :], cap["iota16"][:, :])
    kb.sdma(on16[:, :], cap["ones16"][:, :])
    for i in range(ND):
        kb.gdma(gws[i][:, :], gwT[128 * i:128 * (i + 1), :])
    kb.vop(v.memset, epscol[:, :], LN_EPS)
    with nc.allow_low_precision(reason="ones"):
        kb.vop(v.memset, onescol[:, :], 1.0)
    kb.vop(v.memset, onesrf[:, :], 1.0)
    kb.vop(v.memset, roff[:, :], 0.0)
    kb.barrier()

    ln_done = [kb.mark("act"), kb.mark("act")]
    tr_evac = kb.mark("pe")
    for gt in range(TPC // 128):
        xb_ = xt2[gt % 2]
        kb.wait_mark(nc.sync, ln_done[gt % 2])
        kb.sdma(xb_[:, :], x[128 * gt:128 * (gt + 1), :])
        kb.wait(v, "sdma")
        kb.wait(s, "sdma")
        kb.wait(s, "pe")
        ln_tile(kb, xb_[:, :], xlnb[:, :], sq[:, :], st, epscol)
        ln_done[gt % 2] = kb.mark("act")
        kb.wait(nc.sync, "act")
        kb.sdma(o_xln[128 * gt:128 * (gt + 1), :], xlnb[:, :])
        kb.wait(t, "act")
        for g in range(2):
            kb.wait(t, "dve")
            for k_ in range(4):
                dt = 4 * g + k_
                ins = t.transpose(p_tr[:, 128 * k_:128 * (k_ + 1)],
                                  xlnb[:, 128 * dt:128 * (dt + 1)], ids[:, :])
            kb.tick("pe", ins)
            kb.wait(v, "pe")
            for k_ in range(4):
                dt = 4 * g + k_
                kb.vop(v.tensor_copy, xlt[dt][:, :], p_tr[:, 128 * k_:128 * (k_ + 1)])
        kb.wait(t, "dve")
        for dt in range(ND):
            ins = t.matmul(p_g[:, :], xlt[dt][:, :], gws[dt][:, :],
                           start=(dt == 0), stop=(dt == ND - 1))
        kb.tick("pe", ins)
        kb.wait(v, "pe")
        kb.vop(v.tensor_copy, lg[:, :], p_g[:, :])
        kb.vop(v.reduce_max, gmax[:, :], lg[:, :], AX.X, negate=True)
        kb.wait(s, "dve")
        kb.sop(s.activation, lg[:, :], lg[:, :], AF.Exp, bias=gmax[:, 0:1])
        kb.wait(v, "act")
        kb.vop(v.reduce_sum, gsum[:, :], lg[:, :], AX.X)
        kb.vop(v.reciprocal, gsum[:, :], gsum[:, :])
        kb.vop(v.tensor_scalar_mul, lg[:, :], lg[:, :], gsum[:, 0:1])
        kb.vop(v.reduce_max, gmax[:, :], lg[:, :], AX.X)
        kb.vop(v.tensor_scalar, ohf[:, :], lg[:, :], gmax[:, 0:1], None, ALU.is_equal)
        kb.vop(v.tensor_tensor_scan, cum[:, :], on16[:, :], ohf[:, :], 0.0,
               ALU.mult, ALU.add)
        kb.vop(v.tensor_single_scalar, cum[:, :], cum[:, :], 1.0, ALU.is_equal)
        kb.vop(v.tensor_tensor, ohf[:, :], ohf[:, :], cum[:, :], ALU.mult)
        with nc.allow_low_precision(reason="bf16 onehot"):
            kb.vop(v.tensor_copy, ohb[:, :], ohf[:, :])
        kb.vop(v.tensor_tensor, cum[:, :], ohf[:, :], iot[:, :], ALU.mult)
        kb.vop(v.reduce_sum, eidxc[:, :], cum[:, :], AX.X)
        kb.wait(t, "dve")
        t.matmul(p_w[:, :], ustr[:, :], ohb[:, :], start=True, stop=False)
        kb.tick("pe", t.matmul(p_w[:, :], onesrf[0:1, :], roff[0:1, :],
                               start=False, stop=True))
        kb.wait(v, "pe")
        kb.vop(v.tensor_tensor, loc[:, :], p_w[:, :], ohf[:, :], ALU.mult)
        kb.vop(v.reduce_sum, loccol[:, :], loc[:, :], AX.X)
        kb.wait(t, "dve")
        kb.tick("pe", t.matmul(p_r[0:1, :], onescol[:, 0:1], ohb[:, :],
                               start=True, stop=True))
        kb.wait(v, "pe")
        kb.vop(v.tensor_add, roff[0:1, :], roff[0:1, :], p_r[0:1, :])
        kb.wait(nc.sync, "dve")
        kb.sdma(o_oh[128 * gt:128 * (gt + 1), :], ohf[:, :])
        kb.sdma(o_loc[128 * gt:128 * (gt + 1), :], loccol[:, :])
        kb.sdma(o_pmax[128 * gt:128 * (gt + 1), :], gmax[:, :])
        kb.sdma(o_eidx[128 * gt:128 * (gt + 1), :], eidxc[:, :])
    kb.wait(nc.sync, "dve")
    kb.sdma(o_cnt[:, :], roff[:, :])


def build_l1_slots():
    kb = KB("l1slots")
    nc = kb.nc
    E, C = 16, 2560
    allcnt = nc.dram_tensor("allcnt", [NCORES, E], F32, kind="ExternalInput").ap()
    coremask = nc.dram_tensor("coremask", [NCORES, 1], F32, kind="ExternalInput").ap()
    tokidx = nc.dram_tensor("tokidx", [TPC, 1], F32, kind="ExternalInput").ap()
    oh = nc.dram_tensor("oh", [TPC, E], F32, kind="ExternalInput").ap()
    loc = nc.dram_tensor("loc", [TPC, 1], F32, kind="ExternalInput").ap()
    pmax = nc.dram_tensor("pmax", [TPC, 1], F32, kind="ExternalInput").ap()
    eidx = nc.dram_tensor("eidx", [TPC, 1], F32, kind="ExternalInput").ap()
    o_idx = nc.dram_tensor("o_idx", [E * C, 1], F32, kind="ExternalOutput").ap()
    o_w = nc.dram_tensor("o_w", [TPC, 1], F32, kind="ExternalOutput").ap()
    o_slot = nc.dram_tensor("o_slot", [TPC, 1], F32, kind="ExternalOutput").ap()

    sb = nc.alloc_sbuf_tensor
    acnt = sb("acnt", [NCORES, E], F32)
    cmsk = sb("cmsk", [NCORES, 1], F32)
    cobc = sb("cobc", [128, E], F32)
    ohs = sb("ohs", [128, E], F32)
    locs = sb("locs", [128, 1], F32)
    pms = sb("pms", [128, 1], F32)
    eis = sb("eis", [128, 1], F32)
    tks = sb("tks", [128, 1], F32)
    sel = sb("sel", [128, E], F32)
    gl = sb("gl", [128, 1], F32)
    kp = sb("kp", [128, 1], F32)
    wc = sb("wc", [128, 1], F32)
    sl2 = sb("sl2", [128, 1], F32)
    vals = sb("vals", [128, 1], F32)
    sidx = sb("sidx", [128, 1], I32)
    zt = sb("zt", [128, 320], F32)
    p_c = nc.alloc_psum_tensor("p_c", [128, E], F32)

    t, v, s = nc.tensor, nc.vector, nc.scalar
    kb.sdma(acnt[:, :], allcnt[:, :])
    kb.sdma(cmsk[:, :], coremask[:, :])
    kb.vop(v.memset, zt[:, :], 0.0)
    kb.barrier()
    kb.sdma(o_idx[:, :].rearrange("(p q) o -> p (q o)", p=128), zt[:, :])
    onesr = sb("onesr", [1, 128], F32)
    kb.vop(v.memset, onesr[:, :], 1.0)
    kb.tick("pe", t.matmul(p_c[0:1, :], cmsk[:, :], acnt[:, :], start=True, stop=True))
    kb.wait(v, "pe")
    kb.vop(v.tensor_copy, cobc[0:1, :], p_c[0:1, :])
    kb.wait(t, "dve")
    kb.tick("pe", t.matmul(p_c[:, :], onesr[0:1, :], cobc[0:1, :], start=True, stop=True))
    kb.wait(v, "pe")
    kb.vop(v.tensor_copy, cobc[:, :], p_c[:, :])
    kb.barrier()

    for gt in range(TPC // 128):
        r = slice(128 * gt, 128 * (gt + 1))
        kb.wait(nc.sync, "dve")
        kb.wait(nc.sync, "gdma")
        kb.sdma(ohs[:, :], oh[r, :])
        kb.sdma(locs[:, :], loc[r, :])
        kb.sdma(pms[:, :], pmax[r, :])
        kb.sdma(eis[:, :], eidx[r, :])
        kb.sdma(tks[:, :], tokidx[r, :])
        kb.wait(v, "sdma")
        kb.vop(v.tensor_tensor, sel[:, :], ohs[:, :], cobc[:, :], ALU.mult)
        kb.vop(v.reduce_sum, gl[:, :], sel[:, :], AX.X)
        kb.vop(v.tensor_add, gl[:, :], gl[:, :], locs[:, :])
        kb.vop(v.tensor_single_scalar, kp[:, :], gl[:, :], float(C), ALU.is_lt)
        kb.vop(v.tensor_tensor, wc[:, :], pms[:, :], kp[:, :], ALU.mult)
        kb.vop(v.tensor_scalar, sl2[:, :], eis[:, :], float(C), gl[:, 0:1],
               ALU.mult, ALU.add)
        kb.vop(v.tensor_scalar, kp[:, :], kp[:, :], -1e9, 1e9, ALU.mult, ALU.add)
        kb.vop(v.tensor_add, sl2[:, :], sl2[:, :], kp[:, :])
        kb.vop(v.tensor_scalar_add, vals[:, :], tks[:, :], 1.0)
        with nc.allow_low_precision(reason="int index cast"):
            kb.vop(v.tensor_copy, sidx[:, :], sl2[:, :])
        kb.wait(nc.sync, "dve")
        kb.sdma(o_w[r, :], wc[:, :])
        kb.sdma(o_slot[r, :], sl2[:, :])
        kb.wait(nc.gpsimd, "dve")
        kb.wait(nc.gpsimd, "sdma")
        ins = nc.gpsimd.indirect_dma_start(
            o_idx[:, :], bass.IndirectOffsetOnAxis(sidx[:, 0:1], 0),
            vals[:, 0:1], None, bounds_check=E * C - 1, oob_is_err=False)
        ins.then_inc(kb.sem["gdma"], 16)
        kb.cnt["gdma"] += 16
    kb.finish()
    return kb.nc


def build_l1_experts():
    kb = KB("l1exp")
    nc = kb.nc
    E, C, FF = 16, 2560, 4096
    NEC = 2
    idx2T = nc.dram_tensor("idx2T", [NEC * C, NCORES], F32, kind="ExternalInput").ap()
    xga = nc.dram_tensor("xga", [B * S, D], BF16, kind="ExternalInput").ap()
    w1 = nc.dram_tensor("w1", [NEC, D, FF], F32, kind="ExternalInput").ap()
    w2 = nc.dram_tensor("w2", [NEC, FF, D], F32, kind="ExternalInput").ap()
    o_y = nc.dram_tensor("o_y", [NEC * C, D], BF16, kind="ExternalOutput").ap()

    sb = nc.alloc_sbuf_tensor
    idc = nc.inline_tensor(np.eye(128, dtype=BF), "c_ident").ap()
    idss = sb("idss", [128, 128], BF16)
    code = sb("code", [128, NCORES], F32)
    gidx = sb("gidx", [128, 1], F32)
    gemp = sb("gemp", [128, 1], F32)
    gi32 = sb("gi32", [128, 1], I32)
    xg = sb("xg", [128, D], BF16)
    xgT = [sb(f"xgT{i}", [128, C], BF16) for i in range(ND)]
    hT = [sb(f"hT{i}", [128, 512], BF16) for i in range(32)]
    w1s = [sb(f"w1s{i}", [128, 512], BF16) for i in range(ND)]
    w2s = [sb(f"w2s{i}", [128, D], BF16) for i in range(32)]
    yt = sb("yt", [128, D], BF16)
    p_tr = nc.alloc_psum_tensor("p_tr", [128, 512], BF16)
    p_h = [nc.alloc_psum_tensor(f"p_h{i}", [128, 512], F32) for i in range(2)]
    p_y = [nc.alloc_psum_tensor(f"p_y{i}", [128, 512], F32) for i in range(4)]

    t, v, s = nc.tensor, nc.vector, nc.scalar
    kb.sdma(idss[:, :], idc[:, :])
    kb.barrier()

    for e in range(NEC):
        kb.wait(nc.gpsimd, "pe")
        for i in range(32):
            kb.gdma(w2s[i][:, :], w2[e, 128 * i:128 * (i + 1), :])
        tr_ev = kb.mark("dve")
        for tl in range(20):
            sl0 = e * C + 128 * tl
            kb.wait(nc.sync, "dve")
            kb.sdma(code[:, :], idx2T[sl0:sl0 + 128, :])
            kb.wait(v, "sdma")
            kb.vop(v.reduce_sum, gidx[:, :], code[:, :], AX.X)
            kb.vop(v.tensor_single_scalar, gemp[:, :], gidx[:, :], 0.0, ALU.is_equal)
            kb.vop(v.tensor_scalar, gemp[:, :], gemp[:, :], 1e9, -1.0, ALU.mult, ALU.add)
            kb.vop(v.tensor_add, gidx[:, :], gidx[:, :], gemp[:, :])
            with nc.allow_low_precision(reason="index cast"):
                kb.vop(v.tensor_copy, gi32[:, :], gidx[:, :])
            kb.wait(v, "gdma")
            kb.wait(v, "pe")
            kb.vop(v.memset, xg[:, :], 0.0)
            kb.wait(nc.gpsimd, "dve")
            ins = nc.gpsimd.indirect_dma_start(
                xg[:, :], None, xga[:, :],
                bass.IndirectOffsetOnAxis(gi32[:, 0:1], 0),
                bounds_check=B * S - 1, oob_is_err=False)
            ins.then_inc(kb.sem["gdma"], 16)
            kb.cnt["gdma"] += 16
            kb.wait(t, "gdma")
            for g in range(2):
                kb.wait(t, "dve")
                for k_ in range(4):
                    dt = 4 * g + k_
                    ins = t.transpose(p_tr[:, 128 * k_:128 * (k_ + 1)],
                                      xg[:, 128 * dt:128 * (dt + 1)], idss[:, :])
                    if k_ == 3:
                        kb.tick("pe", ins)
                kb.wait(v, "pe")
                for k_ in range(4):
                    dt = 4 * g + k_
                    kb.vop(v.tensor_copy, xgT[dt][:, 128 * tl:128 * (tl + 1)],
                           p_tr[:, 128 * k_:128 * (k_ + 1)])
            tr_ev = kb.mark("dve")
        kb.barrier()
        for sc in range(5):
            ssl = slice(512 * sc, 512 * (sc + 1))
            for fq in range(8):
                kb.wait(nc.gpsimd, "pe")
                for i in range(ND):
                    kb.gdma(w1s[i][:, :], w1[e, 128 * i:128 * (i + 1),
                                             512 * fq:512 * (fq + 1)])
                kb.wait(t, "gdma")
                hev = kb.mark("act")
                for ft in range(4):
                    ph = p_h[ft % 2]
                    kb.wait_mark(t, hev)
                    for dt in range(ND):
                        ins = t.matmul(ph[:, :], w1s[dt][:, 128 * ft:128 * (ft + 1)],
                                       xgT[dt][:, ssl],
                                       start=(dt == 0), stop=(dt == ND - 1))
                    kb.tick("pe", ins)
                    kb.wait(s, "pe")
                    kb.sop(s.activation, hT[4 * fq + ft][:, :], ph[:, :], AF.Relu)
                    hev = kb.mark("act")
            kb.barrier()
            yev = [kb.mark("dve"), kb.mark("dve")]
            for st_ in range(4):
                pa, pb_ = p_y[2 * (st_ % 2)], p_y[2 * (st_ % 2) + 1]
                kb.wait_mark(t, yev[st_ % 2])
                for fl in range(32):
                    t.matmul(pa[:, :], hT[fl][:, 128 * st_:128 * (st_ + 1)],
                             w2s[fl][:, 0:512], start=(fl == 0), stop=(fl == 31))
                for fl in range(32):
                    ins = t.matmul(pb_[:, :], hT[fl][:, 128 * st_:128 * (st_ + 1)],
                                   w2s[fl][:, 512:1024], start=(fl == 0), stop=(fl == 31))
                kb.tick("pe", ins)
                kb.wait(v, "pe")
                kb.wait(v, "sdma")
                with nc.allow_low_precision(reason="bf16 expert out"):
                    kb.vop(v.tensor_copy, yt[:, 0:512], pa[:, :])
                    kb.vop(v.tensor_copy, yt[:, 512:1024], pb_[:, :])
                yev[st_ % 2] = kb.mark("dve")
                kb.wait(nc.sync, "dve")
                kb.sdma(o_y[e * C + 512 * sc + 128 * st_:
                            e * C + 512 * sc + 128 * (st_ + 1), :], yt[:, :])
            kb.barrier()
    kb.finish()
    return kb.nc


def build_l1_combine():
    kb = KB("l1comb")
    nc = kb.nc
    E, C = 16, 2560
    x = nc.dram_tensor("x", [TPC, D], F32, kind="ExternalInput").ap()
    ya = nc.dram_tensor("ya", [E * C, D], BF16, kind="ExternalInput").ap()
    slot = nc.dram_tensor("slot", [TPC, 1], F32, kind="ExternalInput").ap()
    w = nc.dram_tensor("w", [TPC, 1], F32, kind="ExternalInput").ap()
    out = nc.dram_tensor("out", [TPC, D], F32, kind="ExternalOutput").ap()

    sb = nc.alloc_sbuf_tensor
    xs = sb("xs", [128, D], F32)
    yg = sb("yg", [128, D], BF16)
    sc_ = sb("sc", [128, 1], F32)
    wc = sb("wc", [128, 1], F32)
    si32 = sb("si32", [128, 1], I32)
    os_ = sb("os", [128, D], F32)
    v = nc.vector
    for gt in range(TPC // 128):
        r = slice(128 * gt, 128 * (gt + 1))
        kb.wait(nc.sync, "dve")
        kb.sdma(xs[:, :], x[r, :])
        kb.sdma(sc_[:, :], slot[r, :])
        kb.sdma(wc[:, :], w[r, :])
        kb.wait(v, "sdma")
        kb.wait(v, "gdma")
        kb.vop(v.memset, yg[:, :], 0.0)
        with nc.allow_low_precision(reason="index cast"):
            kb.vop(v.tensor_copy, si32[:, :], sc_[:, :])
        kb.wait(nc.gpsimd, "dve")
        ins = nc.gpsimd.indirect_dma_start(
            yg[:, :], None, ya[:, :], bass.IndirectOffsetOnAxis(si32[:, 0:1], 0),
            bounds_check=E * C - 1, oob_is_err=False)
        ins.then_inc(kb.sem["gdma"], 16)
        kb.cnt["gdma"] += 16
        kb.wait(v, "gdma")
        kb.vop(v.scalar_tensor_tensor, os_[:, :], yg[:, :], wc[:, 0:1], xs[:, :],
               ALU.mult, ALU.add)
        kb.wait(nc.sync, "dve")
        kb.sdma(out[r, :], os_[:, :])
    kb.finish()
    return kb.nc


def build_fused():
    import contextlib
    kb = KB("fused")
    nc = kb.nc
    x = nc.dram_tensor("x", [TPC, D], F32, kind="ExternalInput").ap()
    wqkvT = nc.dram_tensor("wqkvT", [D, 3 * D], F32, kind="ExternalInput").ap()
    woT = nc.dram_tensor("woT", [D, D], F32, kind="ExternalInput").ap()
    wr = nc.dram_tensor("wr", [D, 8], F32, kind="ExternalInput").ap()
    br = nc.dram_tensor("br", [1, 8], F32, kind="ExternalInput").ap()
    w1 = nc.dram_tensor("w1", [8, D, 4096], F32, kind="ExternalInput").ap()
    b1r = nc.dram_tensor("b1r", [8, 128, 32], F32, kind="ExternalInput").ap()
    w2 = nc.dram_tensor("w2", [8, 4096, D], F32, kind="ExternalInput").ap()
    b2 = nc.dram_tensor("b2", [8, D], F32, kind="ExternalInput").ap()
    winT = nc.dram_tensor("winT", [D, 4384], F32, kind="ExternalInput").ap()
    convw = nc.dram_tensor("convw", [2304, 4], F32, kind="ExternalInput").ap()
    convb = nc.dram_tensor("convb", [2304, 1], F32, kind="ExternalInput").ap()
    dtb = nc.dram_tensor("dtb", [32, 1], F32, kind="ExternalInput").ap()
    alog = nc.dram_tensor("alog", [32, 1], F32, kind="ExternalInput").ap()
    dpar = nc.dram_tensor("dpar", [32, 1], F32, kind="ExternalInput").ap()
    nwr = nc.dram_tensor("nwr", [128, 16], F32, kind="ExternalInput").ap()
    woutT = nc.dram_tensor("woutT", [2 * D, D], F32, kind="ExternalInput").ap()
    gwT = nc.dram_tensor("gwT", [D, 16], F32, kind="ExternalInput").ap()

    x1 = nc.dram_tensor("x1", [TPC, D], F32, kind="Internal").ap()
    x2 = nc.dram_tensor("x2", [TPC, D], F32, kind="Internal").ap()
    x3 = nc.dram_tensor("x3", [TPC, D], F32, kind="ExternalOutput").ap()
    o_xln = nc.dram_tensor("o_xln", [TPC, D], BF16, kind="ExternalOutput").ap()
    o_oh = nc.dram_tensor("o_oh", [TPC, 16], F32, kind="ExternalOutput").ap()
    o_loc = nc.dram_tensor("o_loc", [TPC, 1], F32, kind="ExternalOutput").ap()
    o_pmax = nc.dram_tensor("o_pmax", [TPC, 1], F32, kind="ExternalOutput").ap()
    o_eidx = nc.dram_tensor("o_eidx", [TPC, 1], F32, kind="ExternalOutput").ap()
    o_cnt = nc.dram_tensor("o_cnt", [1, 16], F32, kind="ExternalOutput").ap()

    with contextlib.ExitStack() as es:
        emit_l0_attn(kb, es, x, x1, wqkvT, woT)
    kb.barrier()
    with contextlib.ExitStack() as es:
        emit_l0_ffn(kb, es, x1, x2, wr, br, w1, b1r, w2, b2)
    kb.barrier()
    with contextlib.ExitStack() as es:
        emit_l1_attn(kb, es, x2, x3, winT, convw, convb, dtb, alog, dpar, nwr, woutT)
    kb.barrier()
    with contextlib.ExitStack() as es:
        emit_l1_gate(kb, es, x3, o_xln, o_oh, o_loc, o_pmax, o_eidx, o_cnt, gwT)
    kb.finish()
    return kb.nc

def _shard_rows(a):
    a = np.ascontiguousarray(a)
    return [a[i * TPC:(i + 1) * TPC] for i in range(NCORES)]


_NC_CACHE = {}


def _get_nc(name, builder):
    if name not in _NC_CACHE:
        _NC_CACHE[name] = builder()
    return _NC_CACHE[name]


def kernel(**inputs):
    x = np.ascontiguousarray(np.asarray(inputs["L0_attn_x"], dtype=np.float32).reshape(B * S, D))
    wqkvT = np.ascontiguousarray(np.asarray(inputs["L0_attn_W_qkv"], dtype=np.float32).T)
    woT = np.ascontiguousarray(np.asarray(inputs["L0_attn_W_o"], dtype=np.float32).T)
    core_ids = list(range(NCORES))

    w1 = np.ascontiguousarray(np.asarray(inputs["L0_ffn_W1"], dtype=np.float32))
    b1 = np.asarray(inputs["L0_ffn_b1"], dtype=np.float32)
    b1r = np.ascontiguousarray(b1.reshape(8, 32, 128).transpose(0, 2, 1))
    w2 = np.ascontiguousarray(np.asarray(inputs["L0_ffn_W2"], dtype=np.float32))
    b2 = np.ascontiguousarray(np.asarray(inputs["L0_ffn_b2"], dtype=np.float32))
    wr = np.ascontiguousarray(np.asarray(inputs["L0_ffn_Wr"], dtype=np.float32))
    br = np.ascontiguousarray(np.asarray(inputs["L0_ffn_br"], dtype=np.float32).reshape(1, 8))
    winT = np.ascontiguousarray(np.asarray(inputs["L1_attn_W_in"], dtype=np.float32).T)
    convw = np.ascontiguousarray(np.asarray(inputs["L1_attn_conv_w"], dtype=np.float32))
    convb = np.ascontiguousarray(np.asarray(inputs["L1_attn_conv_b"], dtype=np.float32).reshape(-1, 1))
    dtb = np.ascontiguousarray(np.asarray(inputs["L1_attn_dt_bias"], dtype=np.float32).reshape(-1, 1))
    alog = np.ascontiguousarray(np.asarray(inputs["L1_attn_A_log"], dtype=np.float32).reshape(-1, 1))
    dpar = np.ascontiguousarray(np.asarray(inputs["L1_attn_D_param"], dtype=np.float32).reshape(-1, 1))
    nwr = np.ascontiguousarray(np.asarray(inputs["L1_attn_norm_w"], dtype=np.float32).reshape(16, 128).T)
    woutT = np.ascontiguousarray(np.asarray(inputs["L1_attn_W_out"], dtype=np.float32).T)
    gwT = np.ascontiguousarray(np.asarray(inputs["L1_ffn_gate_w"], dtype=np.float32).T)
    ncA = _get_nc("fused", build_fused)
    xs = _shard_rows(x)
    res = run_bass_kernel_spmd(
        ncA, [{"x": xs[i], "wqkvT": wqkvT, "woT": woT, "wr": wr, "br": br,
               "w1": w1, "b1r": b1r, "w2": w2, "b2": b2, "winT": winT,
               "convw": convw, "convb": convb, "dtb": dtb, "alog": alog,
               "dpar": dpar, "nwr": nwr, "woutT": woutT, "gwT": gwT}
              for i in range(NCORES)], core_ids=core_ids)
    r4a = res.results
    x3 = [r4a[i]["x3"] for i in range(NCORES)]

    allcnt = np.ascontiguousarray(np.concatenate([r["o_cnt"] for r in r4a], axis=0))
    nc4b = _get_nc("l1slots", build_l1_slots)
    in_maps = []
    for i in range(NCORES):
        in_maps.append({
            "allcnt": allcnt,
            "coremask": np.ascontiguousarray((np.arange(NCORES) < i)
                                             .astype(np.float32).reshape(-1, 1)),
            "tokidx": np.ascontiguousarray(np.arange(i * TPC, (i + 1) * TPC,
                                                     dtype=np.float32).reshape(-1, 1)),
            "oh": r4a[i]["o_oh"], "loc": r4a[i]["o_loc"],
            "pmax": r4a[i]["o_pmax"], "eidx": r4a[i]["o_eidx"],
        })
    res = run_bass_kernel_spmd(nc4b, in_maps, core_ids=core_ids)
    r4b = res.results

    idxparts = np.stack([r4b[i]["o_idx"].reshape(16, 2560) for i in range(NCORES)])
    xga = np.ascontiguousarray(np.concatenate([r["o_xln"] for r in r4a], axis=0))
    w1f = np.asarray(inputs["L1_ffn_w1"], dtype=np.float32)
    w2f = np.asarray(inputs["L1_ffn_w2"], dtype=np.float32)
    nc4c = _get_nc("l1exp", build_l1_experts)
    in_maps = []
    for i in range(NCORES):
        idx2T = np.ascontiguousarray(
            idxparts[:, 2 * i:2 * i + 2, :].reshape(NCORES, -1).T)
        in_maps.append({"idx2T": idx2T, "xga": xga,
                        "w1": np.ascontiguousarray(w1f[2 * i:2 * i + 2]),
                        "w2": np.ascontiguousarray(w2f[2 * i:2 * i + 2])})
    res = run_bass_kernel_spmd(nc4c, in_maps, core_ids=core_ids)
    ya = np.ascontiguousarray(np.concatenate([res.results[i]["o_y"]
                                              for i in range(NCORES)], axis=0))

    nc4d = _get_nc("l1comb", build_l1_combine)
    res = run_bass_kernel_spmd(
        nc4d, [{"x": x3[i], "ya": ya, "slot": r4b[i]["o_slot"], "w": r4b[i]["o_w"]}
               for i in range(NCORES)], core_ids=core_ids)
    out = np.concatenate([res.results[i]["out"] for i in range(NCORES)], axis=0)
    return out.reshape(B, S, D)


# version 09
# speedup vs baseline: 1.2843x; 1.1129x over previous
import numpy as np
import ml_dtypes
import concourse.bass as bass
import concourse.mybir as mybir
from concourse.bass_utils import run_bass_kernel_spmd

F32 = mybir.dt.float32
BF16 = mybir.dt.bfloat16
I32 = mybir.dt.int32
AF = mybir.ActivationFunctionType
ALU = mybir.AluOpType
AX = mybir.AxisListType
BF = ml_dtypes.bfloat16

NCORES = 8
B, S, D = 16, 2048, 1024
BPC = B // NCORES
TPC = BPC * S
NTB = S // 128
ND = D // 128
NHEADS, DK = 16, 64
ROPE_THETA = 10000.0
LN_EPS = 1e-5


class KB:

    def __init__(self, name):
        self.nc = bass.Bass("TRN2", target_bir_lowering=False, name=name,
                            detect_race_conditions=False)
        self.sem = {}
        self.cnt = {}
        for nm in ("sdma0", "sdma1", "gdma0", "gdma1", "pe", "dve", "act", "pool"):
            self.sem[nm] = self.nc.alloc_semaphore("s_" + nm)
            self.cnt[nm] = 0
        self.rot = {"sdma": 0, "gdma": 0}

    def _expand(self, nm):
        return (nm + "0", nm + "1") if nm in ("sdma", "gdma") else (nm,)

    def _dma_slot(self, q):
        i = self.rot[q]
        self.rot[q] ^= 1
        nm = q + str(i)
        eng = self.nc.sync if q == "sdma" else self.nc.gpsimd
        if self.cnt[nm]:
            eng.wait_ge(self.sem[nm], self.cnt[nm])
        return nm

    def _dma_post(self, nm, ins):
        ins.then_inc(self.sem[nm], 16)
        self.cnt[nm] += 16

    def sdma(self, out, in_):
        nm = self._dma_slot("sdma")
        self._dma_post(nm, self.nc.sync.dma_start(out=out, in_=in_))

    def gdma(self, out, in_):
        nm = self._dma_slot("gdma")
        self._dma_post(nm, self.nc.gpsimd.dma_start(out=out, in_=in_))

    def tick(self, nm, instr, k=1):
        instr.then_inc(self.sem[nm], k)
        self.cnt[nm] += k

    def vop(self, f, *args, **kw):
        self.wait(self.nc.vector, "dve")
        ins = f(*args, **kw)
        self.tick("dve", ins)
        return ins

    def sop(self, f, *args, **kw):
        self.wait(self.nc.scalar, "act")
        ins = f(*args, **kw)
        self.tick("act", ins)
        return ins

    def pop(self, f, *args, **kw):
        self.wait(self.nc.gpsimd, "pool")
        ins = f(*args, **kw)
        self.tick("pool", ins)
        return ins

    def mark(self, nm):
        if nm in ("sdma", "gdma"):
            return (nm, tuple(self.cnt[n] for n in self._expand(nm)))
        return (nm, self.cnt[nm])

    def wait_mark(self, eng, mk):
        nm, v = mk
        if nm in ("sdma", "gdma"):
            for n, vi in zip(self._expand(nm), v):
                if vi > 0:
                    eng.wait_ge(self.sem[n], vi)
        elif v > 0:
            eng.wait_ge(self.sem[nm], v)

    def wait(self, eng, *names):
        for nm0 in names:
            for nm in self._expand(nm0):
                if self.cnt[nm] > 0:
                    eng.wait_ge(self.sem[nm], self.cnt[nm])

    def barrier(self):
        nc = self.nc
        for eng in (nc.sync, nc.gpsimd, nc.tensor, nc.vector, nc.scalar):
            self.wait(eng, "sdma", "gdma", "pe", "dve", "act", "pool")

    def finish(self):
        self.wait(self.nc.sync, "sdma", "gdma")


def _rope_tables():
    pos = np.arange(S, dtype=np.float64)
    expo = np.arange(0, DK, 2, dtype=np.float64) / DK
    inv_freq = ROPE_THETA ** expo
    ang = pos[None, :] / inv_freq[:, None]
    cos = np.repeat(np.cos(ang), 2, axis=0)
    sin = np.repeat(np.sin(ang), 2, axis=0)
    sgn = np.where(np.arange(DK) % 2 == 0, -1.0, 1.0)[:, None]
    cc = np.concatenate([cos, cos], axis=0).astype(BF)
    ss = np.concatenate([sin * sgn, sin * sgn], axis=0).astype(BF)
    r = np.zeros((DK, DK), dtype=BF)
    for j in range(0, DK, 2):
        r[j, j + 1] = 1.0
        r[j + 1, j] = 1.0
    r2 = np.zeros((128, 128), dtype=BF)
    r2[0:64, 0:64] = r
    r2[64:128, 64:128] = r
    return cc, ss, r2


class LNStats:

    def __init__(self, nc, tag, sb=None):
        if sb is None:
            sb = nc.alloc_sbuf_tensor
        self.m = sb(f"ln_m_{tag}", [128, 1], F32)
        self.q = sb(f"ln_q_{tag}", [128, 1], F32)
        self.m2 = sb(f"ln_m2_{tag}", [128, 1], F32)
        self.v = sb(f"ln_v_{tag}", [128, 1], F32)
        self.r = sb(f"ln_r_{tag}", [128, 1], F32)
        self.b = sb(f"ln_b_{tag}", [128, 1], F32)


def ln_tile(kb, xt, out_bf, sq, st, epscol):
    nc = kb.nc
    v, s = nc.vector, nc.scalar
    kb.vop(v.reduce_sum, st.m[:, :], xt, AX.X)
    kb.vop(v.tensor_scalar_mul, st.m[:, :], st.m[:, :], 1.0 / D)
    kb.vop(v.tensor_tensor, st.m2[:, :], st.m[:, :], st.m[:, :], ALU.mult)
    kb.sop(s.activation, sq, xt, AF.Square, accum_out=st.q[:, :])
    kb.wait(v, "act")
    kb.vop(v.tensor_scalar, st.v[:, :], st.q[:, :], 1.0 / D, st.m2[:, 0:1],
           ALU.mult, ALU.subtract)
    kb.wait(s, "dve")
    kb.sop(s.activation, st.v[:, :], st.v[:, :], AF.Ln, bias=epscol[:, 0:1])
    kb.sop(s.activation, st.r[:, :], st.v[:, :], AF.Exp, scale=-0.5)
    kb.wait(v, "act")
    kb.vop(v.tensor_scalar, st.b[:, :], st.m[:, :], st.r[:, 0:1], -1.0,
           ALU.mult, ALU.mult)
    kb.wait(s, "dve")
    kb.sop(s.activation, out_bf, xt, AF.Identity, bias=st.b[:, 0:1], scale=st.r[:, 0:1])



def _scoped_allocs(nc, es, px):
    def sb(name, shape, dtype):
        return es.enter_context(nc.sbuf_tensor(px + name, shape, dtype))
    def ps(name, shape, dtype=F32):
        return es.enter_context(nc.psum_tensor(px + name, shape, dtype))
    return sb, ps


def build_l0_attn(debug_taps=False):
    import contextlib
    kb = KB("l0attn")
    nc = kb.nc
    x = nc.dram_tensor("x", [TPC, D], F32, kind="ExternalInput").ap()
    wqkvT = nc.dram_tensor("wqkvT", [D, 3 * D], F32, kind="ExternalInput").ap()
    woT = nc.dram_tensor("woT", [D, D], F32, kind="ExternalInput").ap()
    x1 = nc.dram_tensor("x1", [TPC, D], F32, kind="ExternalOutput").ap()
    with contextlib.ExitStack() as es:
        emit_l0_attn(kb, es, x, x1, wqkvT, woT, debug_taps=debug_taps)
    kb.finish()
    return kb.nc


def emit_l0_attn(kb, es, x, x1, wqkvT, woT, px="a_", debug_taps=False):
    nc = kb.nc
    cc_np, ss_np, r2_np = _rope_tables()
    consts = {
        "cc": cc_np, "ss": ss_np, "r2": r2_np,
        "ident": np.eye(128, dtype=BF),
        "upmask": (np.arange(128)[:, None] <= np.arange(128)[None, :]).astype(BF),
        "ones": np.ones((128, 128), dtype=BF),
    }
    cap = {k: nc.inline_tensor(v, px + "c_" + k).ap() for k, v in consts.items()}

    sb, ps = _scoped_allocs(nc, es, px)
    wq = [sb(f"wq{i}", [128, 3 * D], BF16) for i in range(ND)]
    wo = [sb(f"wo{i}", [128, D], BF16) for i in range(ND)]
    ccs = sb("ccs", [128, S], BF16)
    sss = sb("sss", [128, S], BF16)
    r2s = sb("r2s", [128, 128], BF16)
    ids = sb("ids", [128, 128], BF16)
    upm = sb("upm", [128, 128], BF16)
    ones = sb("ones1", [128, 128], BF16)
    xt = [sb(f"xt{i}", [128, D], F32) for i in range(2)]
    sq = sb("sq", [128, D], F32)
    xlnb = sb("xlnb", [128, D], BF16)
    xlnT = [sb(f"xlnT{i}", [128, S], BF16) for i in range(ND)]
    vaug = [sb(f"vaug{i}", [128, NHEADS * 65], BF16) for i in range(NTB)]
    qk_raw = sb("qk_raw", [128, S], BF16)
    q2 = sb("q2", [128, S], BF16)
    k2 = sb("k2", [128, S], BF16)
    rt = sb("rt", [128, 512], BF16)
    rt2 = sb("rt2", [128, 512], BF16)
    pt = sb("pt", [128, 512], BF16)
    rb = sb("rb", [128, 512], BF16)
    aoutT = [sb(f"aoutT{i}", [128, S], BF16) for i in range(ND)]
    wsum = sb("wsum", [128, D], F32)
    epscol = sb("epscol", [128, 1], F32)
    st = LNStats(nc, "a", sb)

    p_tr = ps("p_tr", [128, 512], BF16)
    p_qk = ps("p_qk", [128, 512], F32)
    p_rot = ps("p_rot", [128, 512], F32)
    p_s = ps("p_s", [128, 512], F32)
    p_o = ps("p_o", [128, 512], F32)
    p_b = ps("p_b", [128, 512], F32)
    p_w0 = ps("p_w0", [128, 512], F32)
    p_w1 = ps("p_w1", [128, 512], F32)

    kb.sdma(ccs[:, :], cap["cc"][:, :])
    kb.sdma(sss[:, :], cap["ss"][:, :])
    kb.sdma(r2s[:, :], cap["r2"][:, :])
    kb.sdma(ids[:, :], cap["ident"][:, :])
    kb.sdma(upm[:, :], cap["upmask"][:, :])
    kb.sdma(ones[:, :], cap["ones"][:, :])
    for i in range(ND):
        kb.gdma(wq[i][:, :], wqkvT[128 * i:128 * (i + 1), :])
        kb.gdma(wo[i][:, :], woT[128 * i:128 * (i + 1), :])
    kb.vop(nc.vector.memset, epscol[:, :], LN_EPS)
    kb.barrier()

    t, v, s = nc.tensor, nc.vector, nc.scalar
    for b in range(BPC):
        tok0 = b * S
        ln_done = [kb.mark("act"), kb.mark("act")]
        tr_evac = kb.mark("dve")
        v_evac = kb.mark("act")
        for tt in range(NTB):
            xb_ = xt[tt % 2]
            kb.wait_mark(nc.sync, ln_done[tt % 2])
            kb.sdma(xb_[:, :], x[tok0 + 128 * tt: tok0 + 128 * (tt + 1), :])
            kb.wait(v, "sdma")
            kb.wait(s, "sdma")
            kb.wait(s, "pe")
            ln_tile(kb, xb_[:, :], xlnb[:, :], sq[:, :], st, epscol)
            ln_done[tt % 2] = kb.mark("act")
            kb.wait(t, "act")
            for g in range(2):
                kb.wait_mark(t, tr_evac)
                for k_ in range(4):
                    dt = 4 * g + k_
                    ins = t.transpose(p_tr[:, 128 * k_:128 * (k_ + 1)],
                                      xlnb[:, 128 * dt:128 * (dt + 1)], ids[:, :])
                    if k_ == 3:
                        kb.tick("pe", ins)
                kb.wait(v, "pe")
                for k_ in range(4):
                    dt = 4 * g + k_
                    kb.vop(v.tensor_copy, xlnT[dt][:, 128 * tt:128 * (tt + 1)],
                           p_tr[:, 128 * k_:128 * (k_ + 1)])
                tr_evac = kb.mark("dve")
            kb.wait(t, "dve")
            for c in range(2):
                kb.wait_mark(t, v_evac)
                for dt in range(ND):
                    ins = t.matmul(p_qk[:, :], xlnT[dt][:, 128 * tt:128 * (tt + 1)],
                                   wq[dt][:, 2 * D + 512 * c: 2 * D + 512 * (c + 1)],
                                   start=(dt == 0), stop=(dt == ND - 1))
                kb.tick("pe", ins)
                kb.wait(s, "pe")
                dst = vaug[tt][:, :].rearrange("p (h w) -> p h w", w=65)[:, 8 * c:8 * (c + 1), 0:64]
                kb.sop(s.activation, dst, p_qk[:, :].rearrange("p (h w) -> p h w", w=64),
                       AF.Copy)
                v_evac = kb.mark("act")
            kb.wait(v, "act")
            kb.vop(v.memset, vaug[tt][:, :].rearrange("p (h w) -> p h w", w=65)[:, :, 64:65], 1.0)

        kb.barrier()

        po_used = kb.mark("dve")
        pb_used = kb.mark("act")
        qk_evac = kb.mark("act")
        rot_used = kb.mark("dve")
        for hp in range(NHEADS // 2):
            for dst, col0 in ((q2, 128 * hp), (k2, D + 128 * hp)):
                for jq in range(4):
                    kb.wait_mark(t, qk_evac)
                    for dt in range(ND):
                        ins = t.matmul(p_qk[:, :], wq[dt][:, col0:col0 + 128],
                                       xlnT[dt][:, 512 * jq:512 * (jq + 1)],
                                       start=(dt == 0), stop=(dt == ND - 1))
                    kb.tick("pe", ins)
                    kb.wait(s, "pe")
                    kb.wait(s, "dve")
                    kb.sop(s.activation, qk_raw[:, 512 * jq:512 * (jq + 1)],
                           p_qk[:, :], AF.Copy)
                    qk_evac = kb.mark("act")
                    kb.wait(t, "act")
                    kb.wait_mark(t, rot_used)
                    kb.tick("pe", t.matmul(p_rot[:, :], r2s[:, :],
                                           qk_raw[:, 512 * jq:512 * (jq + 1)],
                                           start=True, stop=True))
                    kb.wait(v, "pe")
                    kb.wait(v, "act")
                    sl = slice(512 * jq, 512 * (jq + 1))
                    kb.vop(v.tensor_tensor, rt[:, :], p_rot[:, :], sss[:, sl], ALU.mult)
                    kb.vop(v.tensor_tensor, rt2[:, :], qk_raw[:, sl], ccs[:, sl], ALU.mult)
                    kb.vop(v.tensor_add, dst[:, sl], rt[:, :], rt2[:, :])
                    rot_used = kb.mark("dve")

            for h01 in range(2):
                h = 2 * hp + h01
                base = 64 * h01
                for jq in range(4):
                    nk = 4 * jq + 4
                    for ki in range(nk):
                        kb.wait(t, "dve")
                        kb.wait(t, "act")
                        kb.tick("pe", t.matmul(p_s[:, :],
                                               k2[base:base + 64, 128 * ki:128 * (ki + 1)],
                                               q2[base:base + 64, 512 * jq:512 * (jq + 1)],
                                               start=True, stop=True))
                        kb.wait(s, "pe")
                        kb.sop(s.activation, pt[:, :], p_s[:, :], AF.Exp, scale=0.125)
                        ld = ki - 4 * jq
                        if ld >= 0:
                            kb.wait(v, "act")
                            if ld > 0:
                                kb.vop(v.memset, pt[:, 0:128 * ld], 0.0)
                            kb.vop(v.tensor_tensor, pt[:, 128 * ld:128 * (ld + 1)],
                                   pt[:, 128 * ld:128 * (ld + 1)],
                                   upm[:, :], ALU.mult)
                            kb.wait(t, "dve")
                        else:
                            kb.wait(t, "act")
                        if ki == 0:
                            kb.wait_mark(t, po_used)
                        ins = t.matmul(p_o[0:65, :], vaug[ki][:, 65 * h:65 * h + 65],
                                       pt[:, :], start=(ki == 0), stop=(ki == nk - 1))
                        if ki == nk - 1:
                            kb.tick("pe", ins)
                    kb.wait(v, "pe")
                    with nc.allow_low_precision(reason="bf16 softmax recip within tolerance"):
                        kb.vop(v.reciprocal, rb[64:65, 0:512], p_o[64:65, :])
                    kb.wait(t, "dve")
                    kb.wait_mark(t, pb_used)
                    kb.tick("pe", t.matmul(p_b[0:64, :], ones[64:65, 0:64],
                                           rb[64:65, 0:512], start=True, stop=True))
                    kb.wait(s, "pe")
                    kb.wait_mark(s, po_used)
                    kb.sop(s.activation, rb[0:64, 0:512], p_b[0:64, :], AF.Copy)
                    pb_used = kb.mark("act")
                    kb.wait(v, "act")
                    kb.vop(v.tensor_tensor,
                           aoutT[h // 2][64 * (h % 2):64 * (h % 2) + 64, 512 * jq:512 * (jq + 1)],
                           p_o[0:64, :], rb[0:64, 0:512], ALU.mult)
                    po_used = kb.mark("dve")

        kb.barrier()

        res_done = [kb.mark("dve"), kb.mark("dve")]
        for tt in range(NTB):
            xb_ = xt[tt % 2]
            kb.wait_mark(nc.sync, res_done[tt % 2])
            kb.sdma(xb_[:, :], x[tok0 + 128 * tt: tok0 + 128 * (tt + 1), :])
            kb.wait(t, "dve")
            for c, pw in ((0, p_w0), (1, p_w1)):
                for ft in range(ND):
                    ins = t.matmul(pw[:, :], aoutT[ft][:, 128 * tt:128 * (tt + 1)],
                                   wo[ft][:, 512 * c:512 * (c + 1)],
                                   start=(ft == 0), stop=(ft == ND - 1))
                kb.tick("pe", ins)
            kb.wait(v, "pe")
            kb.wait(v, "sdma")
            kb.vop(v.tensor_add, wsum[:, 0:512], xb_[:, 0:512], p_w0[:, :])
            kb.vop(v.tensor_add, wsum[:, 512:1024], xb_[:, 512:1024], p_w1[:, :])
            res_done[tt % 2] = kb.mark("dve")
            kb.wait(nc.sync, "dve")
            kb.sdma(x1[tok0 + 128 * tt: tok0 + 128 * (tt + 1), :], wsum[:, :])
        kb.barrier()

    if debug_taps:
        taps = {
            "dbg_xlnT0": xlnT[0], "dbg_q2": q2, "dbg_k2": k2,
            "dbg_vaug0": vaug[0], "dbg_aoutT0": aoutT[0],
        }
        for nm, tile in taps.items():
            shp = list(tile.shape) if hasattr(tile, "shape") else [128, S]
            o = nc.dram_tensor(nm, [128, shp[1]], BF16, kind="ExternalOutput").ap()
            kb.sdma(o[:, :], tile[:, :])
        kb.barrier()



def emit_ln_transpose(kb, src_tiles, xlnb, sq, st, epscol, xlnT, ids, p_tr, tr_state):
    nc = kb.nc
    t, v, s = nc.tensor, nc.vector, nc.scalar
    for tt, xb_ in enumerate(src_tiles):
        kb.wait(s, "pe")
        ln_tile(kb, xb_[:, :], xlnb[:, :], sq[:, :], st, epscol)
        kb.wait(t, "act")
        for g in range(2):
            kb.wait_mark(t, tr_state["tr_evac"])
            for k_ in range(4):
                dt = 4 * g + k_
                ins = t.transpose(p_tr[:, 128 * k_:128 * (k_ + 1)],
                                  xlnb[:, 128 * dt:128 * (dt + 1)], ids[:, :])
                if k_ == 3:
                    kb.tick("pe", ins)
            kb.wait(v, "pe")
            for k_ in range(4):
                dt = 4 * g + k_
                kb.vop(v.tensor_copy, xlnT[dt][:, 128 * tt:128 * (tt + 1)],
                       p_tr[:, 128 * k_:128 * (k_ + 1)])
            tr_state["tr_evac"] = kb.mark("dve")


def build_l0_ffn(debug_taps=False):
    import contextlib
    kb = KB("l0ffn")
    nc = kb.nc
    E, FF = 8, 4096
    x = nc.dram_tensor("x", [TPC, D], F32, kind="ExternalInput").ap()
    wr = nc.dram_tensor("wr", [D, E], F32, kind="ExternalInput").ap()
    br = nc.dram_tensor("br", [1, E], F32, kind="ExternalInput").ap()
    w1 = nc.dram_tensor("w1", [E, D, FF], F32, kind="ExternalInput").ap()
    b1r = nc.dram_tensor("b1r", [E, 128, FF // 128], F32, kind="ExternalInput").ap()
    w2 = nc.dram_tensor("w2", [E, FF, D], F32, kind="ExternalInput").ap()
    b2 = nc.dram_tensor("b2", [E, D], F32, kind="ExternalInput").ap()
    x2 = nc.dram_tensor("x2", [TPC, D], F32, kind="ExternalOutput").ap()
    with contextlib.ExitStack() as es:
        emit_l0_ffn(kb, es, x, x2, wr, br, w1, b1r, w2, b2, debug_taps=debug_taps)
    kb.finish()
    return kb.nc


def emit_l0_ffn(kb, es, x, x2, wr, br, w1, b1r, w2, b2, px="f_", debug_taps=False):
    nc = kb.nc
    E, FF = 8, 4096
    tie_np = (np.arange(8)[None, :] <= np.arange(8)[:, None]).astype(np.float32)
    consts = {
        "ident": np.eye(128, dtype=BF),
        "onesrow": np.ones((1, 128), dtype=BF),
        "tie": np.broadcast_to(tie_np.reshape(1, 64), (128, 64)).astype(np.float32),
    }
    cap = {k: nc.inline_tensor(np.ascontiguousarray(v), px + "c_" + k).ap() for k, v in consts.items()}

    sb, ps = _scoped_allocs(nc, es, px)
    ids = sb("ids", [128, 128], BF16)
    onesr = sb("onesr", [1, 128], BF16)
    ties = sb("ties", [128, 64], F32)
    wrs = [sb(f"wrs{i}", [128, E], BF16) for i in range(ND)]
    brs = sb("brs", [1, E], BF16)
    b2s = sb("b2s", [E, D], BF16)
    b1s = sb("b1s", [128, FF // 128], F32)
    acc = [sb(f"acc{i}", [128, D], F32) for i in range(NTB)]
    sq = sb("sq", [128, D], F32)
    xlnb = sb("xlnb", [128, D], BF16)
    xlnT = [sb(f"xlnT{i}", [128, S], BF16) for i in range(ND)]
    hT = [sb(f"hT{i}", [128, S], BF16) for i in range(8)]
    w1q = [[sb(f"w1q{p}_{i}", [128, 1024], BF16) for i in range(ND)] for p in range(2)]
    w2q = [[sb(f"w2q{p}_{i}", [128, D], BF16) for i in range(8)] for p in range(2)]
    lg = sb("lg", [128, E], F32)
    ga = [sb(f"ga{i}", [128, E], F32) for i in range(NTB)]
    gab = sb("gab", [128, E], BF16)
    gT = sb("gT", [E, S], BF16)
    mg = sb("mg", [128, 64], F32)
    eqg = sb("eqg", [128, 64], F32)
    cum = sb("cum", [128, 64], F32)
    kp = sb("kp", [128, E], F32)
    kp2 = sb("kp2", [128, E], F32)
    gmax = sb("gmax", [128, 1], F32)
    gsum = sb("gsum", [128, 1], F32)
    epscol = sb("epscol", [128, 1], F32)
    st = LNStats(nc, "f", sb)

    p_tr = ps("p_tr", [128, 512], BF16)
    p_g = ps("p_g", [128, E], F32)
    p_h0 = ps("p_h0", [128, 512], F32)
    p_h1 = ps("p_h1", [128, 512], F32)
    p_y0 = ps("p_y0", [128, 512], F32)
    p_y1 = ps("p_y1", [128, 512], F32)

    t, v, s = nc.tensor, nc.vector, nc.scalar
    kb.sdma(ids[:, :], cap["ident"][:, :])
    kb.sdma(onesr[:, :], cap["onesrow"][:, :])
    kb.sdma(ties[:, :], cap["tie"][:, :])
    kb.gdma(brs[:, :], br[:, :])
    kb.gdma(b2s[:, :], b2[:, :])
    for i in range(ND):
        kb.gdma(wrs[i][:, :], wr[128 * i:128 * (i + 1), :])
    kb.vop(v.memset, epscol[:, :], LN_EPS)
    kb.barrier()

    for b in range(BPC):
        tok0 = b * S
        for tt in range(NTB):
            kb.sdma(acc[tt][:, :], x[tok0 + 128 * tt: tok0 + 128 * (tt + 1), :])
        kb.barrier()
        tr_state = {"tr_evac": kb.mark("dve")}
        emit_ln_transpose(kb, [a for a in acc], xlnb, sq, st, epscol, xlnT, ids, p_tr, tr_state)
        kb.barrier()

        gt_evac = kb.mark("dve")
        for tt in range(NTB):
            kb.wait(t, "dve")
            for dt in range(ND):
                ins = t.matmul(p_g[:, :], xlnT[dt][:, 128 * tt:128 * (tt + 1)],
                               wrs[dt][:, :], start=(dt == 0), stop=False)
            kb.tick("pe", t.matmul(p_g[:, :], onesr[0:1, :], brs[0:1, :],
                                   start=False, stop=True))
            kb.wait(v, "pe")
            kb.vop(v.tensor_copy, lg[:, :], p_g[:, :])
            kb.vop(v.reduce_max, gmax[:, :], lg[:, :], AX.X, negate=True)
            kb.wait(s, "dve")
            kb.sop(s.activation, lg[:, :], lg[:, :], AF.Exp, bias=gmax[:, 0:1])
            kb.wait(v, "act")
            kb.vop(v.reduce_sum, gsum[:, :], lg[:, :], AX.X)
            kb.vop(v.reciprocal, gsum[:, :], gsum[:, :])
            kb.vop(v.tensor_scalar_mul, lg[:, :], lg[:, :], gsum[:, 0:1])
            A = lg[:, :].unsqueeze(1).broadcast_to((128, 8, 8))
            Bm = lg[:, :].unsqueeze(2).broadcast_to((128, 8, 8))
            mg3 = mg[:, :].rearrange("p (e f) -> p e f", f=8)
            eq3 = eqg[:, :].rearrange("p (e f) -> p e f", f=8)
            kb.vop(v.tensor_tensor, mg3, A, Bm, ALU.is_gt)
            kb.vop(v.tensor_tensor, eq3, A, Bm, ALU.is_equal)
            kb.vop(v.tensor_tensor, eqg[:, :], eqg[:, :], ties[:, :], ALU.mult)
            kb.vop(v.tensor_add, mg[:, :], mg[:, :], eqg[:, :])
            kb.vop(v.tensor_tensor, mg3, mg3, A, ALU.mult)
            kb.vop(v.tensor_reduce, cum[:, 0:8], mg3, AX.X, ALU.add)
            kb.vop(v.tensor_single_scalar, kp[:, :], cum[:, 0:8], 0.8, ALU.is_lt)
            kb.vop(v.tensor_tensor, kp2[:, :], cum[:, 0:8], lg[:, :], ALU.is_equal)
            kb.vop(v.tensor_max, kp[:, :], kp[:, :], kp2[:, :])
            kb.vop(v.tensor_tensor, ga[tt][:, :], lg[:, :], kp[:, :], ALU.mult)
            with nc.allow_low_precision(reason="bf16 gate copy for transpose"):
                kb.vop(v.tensor_copy, gab[:, :], ga[tt][:, :])
            kb.wait(t, "dve")
            kb.wait_mark(t, gt_evac)
            kb.tick("pe", t.transpose(p_tr[0:8, 0:128], gab[:, :], ids[:, :]))
            kb.wait(v, "pe")
            kb.vop(v.tensor_copy, gT[0:8, 128 * tt:128 * (tt + 1)], p_tr[0:8, 0:128])
            gt_evac = kb.mark("dve")
        kb.barrier()

        for tt in range(NTB):
            kb.wait(t, "dve")
            for c, py in ((0, p_y0), (1, p_y1)):
                kb.tick("pe", t.matmul(py[:, :], gT[0:8, 128 * tt:128 * (tt + 1)],
                                       b2s[:, 512 * c:512 * (c + 1)], start=True, stop=True))
            kb.wait(v, "pe")
            kb.vop(v.tensor_add, acc[tt][:, 0:512], acc[tt][:, 0:512], p_y0[:, :])
            kb.vop(v.tensor_add, acc[tt][:, 512:1024], acc[tt][:, 512:1024], p_y1[:, :])
        kb.barrier()

        iters = [(e_, q_) for e_ in range(E) for q_ in range(4)]

        def ffn_issue(e_, qq_, par_):
            f0_ = 1024 * qq_
            kb.wait(nc.gpsimd, "pe")
            for i_ in range(ND):
                kb.gdma(w1q[par_][i_][:, :],
                        w1[e_, 128 * i_:128 * (i_ + 1), f0_:f0_ + 1024])
            for i_ in range(8):
                kb.gdma(w2q[par_][i_][:, :],
                        w2[e_, f0_ + 128 * i_:f0_ + 128 * (i_ + 1), :])
            return kb.mark("gdma")

        ld_mark = [None, None]
        ld_mark[0] = ffn_issue(iters[0][0], iters[0][1], 0)
        cur_e = -1
        for k_, (e, qq) in enumerate(iters):
            par = k_ % 2
            if e != cur_e:
                kb.wait(nc.sync, "act")
                kb.sdma(b1s[:, :], b1r[e, :, :])
                cur_e = e
            if k_ + 1 < len(iters):
                ld_mark[1 - par] = ffn_issue(iters[k_ + 1][0], iters[k_ + 1][1], 1 - par)
            kb.wait_mark(t, ld_mark[par])
            kb.wait(s, "sdma")
            hb = [p_h0, p_h1]
            h_evac = [kb.mark("act"), kb.mark("act")]
            for ft in range(8):
                for tc in range(4):
                    ph = hb[(ft * 4 + tc) % 2]
                    kb.wait_mark(t, h_evac[(ft * 4 + tc) % 2])
                    kb.wait(t, "dve")
                    for dt in range(ND):
                        ins = t.matmul(ph[:, :], w1q[par][dt][:, 128 * ft:128 * (ft + 1)],
                                       xlnT[dt][:, 512 * tc:512 * (tc + 1)],
                                       start=(dt == 0), stop=(dt == ND - 1))
                    kb.tick("pe", ins)
                    kb.wait(s, "pe")
                    fcol = 8 * qq + ft
                    kb.sop(s.activation, hT[ft][:, 512 * tc:512 * (tc + 1)],
                           ph[:, :], AF.Relu, bias=b1s[:, fcol:fcol + 1])
                    h_evac[(ft * 4 + tc) % 2] = kb.mark("act")
            kb.wait(t, "act")
            y_done = kb.mark("dve")
            for tt in range(NTB):
                kb.wait_mark(t, y_done)
                for c, py in ((0, p_y0), (1, p_y1)):
                    for fl in range(8):
                        ins = t.matmul(py[:, :], hT[fl][:, 128 * tt:128 * (tt + 1)],
                                       w2q[par][fl][:, 512 * c:512 * (c + 1)],
                                       start=(fl == 0), stop=(fl == 7))
                    kb.tick("pe", ins)
                kb.wait(v, "pe")
                kb.vop(v.scalar_tensor_tensor, acc[tt][:, 0:512], p_y0[:, :],
                       ga[tt][:, e:e + 1], acc[tt][:, 0:512], ALU.mult, ALU.add)
                kb.vop(v.scalar_tensor_tensor, acc[tt][:, 512:1024], p_y1[:, :],
                       ga[tt][:, e:e + 1], acc[tt][:, 512:1024], ALU.mult, ALU.add)
                y_done = kb.mark("dve")
        kb.barrier()
        for tt in range(NTB):
            kb.wait(nc.sync, "dve")
            kb.sdma(x2[tok0 + 128 * tt: tok0 + 128 * (tt + 1), :], acc[tt][:, :])
        kb.barrier()



def emit_ln_transpose_stream(kb, x_ap, tok0, xt2, xlnb, sq, st, epscol, xlnT, ids,
                             p_tr, tr_state):
    nc = kb.nc
    t, v, s = nc.tensor, nc.vector, nc.scalar
    ln_done = [kb.mark("act"), kb.mark("act")]
    for tt in range(NTB):
        xb_ = xt2[tt % 2]
        kb.wait_mark(nc.sync, ln_done[tt % 2])
        kb.sdma(xb_[:, :], x_ap[tok0 + 128 * tt: tok0 + 128 * (tt + 1), :])
        kb.wait(v, "sdma")
        kb.wait(s, "sdma")
        kb.wait(s, "pe")
        ln_tile(kb, xb_[:, :], xlnb[:, :], sq[:, :], st, epscol)
        ln_done[tt % 2] = kb.mark("act")
        kb.wait(t, "act")
        for g in range(2):
            kb.wait_mark(t, tr_state["tr_evac"])
            for k_ in range(4):
                dt = 4 * g + k_
                ins = t.transpose(p_tr[:, 128 * k_:128 * (k_ + 1)],
                                  xlnb[:, 128 * dt:128 * (dt + 1)], ids[:, :])
                if k_ == 3:
                    kb.tick("pe", ins)
            kb.wait(v, "pe")
            for k_ in range(4):
                dt = 4 * g + k_
                kb.vop(v.tensor_copy, xlnT[dt][:, 128 * tt:128 * (tt + 1)],
                       p_tr[:, 128 * k_:128 * (k_ + 1)])
            tr_state["tr_evac"] = kb.mark("dve")


def build_l1_attn(debug_taps=False, stop=None):
    import contextlib
    kb = KB("mamba")
    nc = kb.nc
    x = nc.dram_tensor("x", [TPC, D], F32, kind="ExternalInput").ap()
    winT = nc.dram_tensor("winT", [D, 4384], F32, kind="ExternalInput").ap()
    convw = nc.dram_tensor("convw", [2304, 4], F32, kind="ExternalInput").ap()
    convb = nc.dram_tensor("convb", [2304, 1], F32, kind="ExternalInput").ap()
    dtb = nc.dram_tensor("dtb", [32, 1], F32, kind="ExternalInput").ap()
    alog = nc.dram_tensor("alog", [32, 1], F32, kind="ExternalInput").ap()
    dpar = nc.dram_tensor("dpar", [32, 1], F32, kind="ExternalInput").ap()
    nwr = nc.dram_tensor("nwr", [128, 16], F32, kind="ExternalInput").ap()
    woutT = nc.dram_tensor("woutT", [2 * D, D], F32, kind="ExternalInput").ap()
    x3 = nc.dram_tensor("x3", [TPC, D], F32, kind="ExternalOutput").ap()
    with contextlib.ExitStack() as es:
        emit_l1_attn(kb, es, x, x3, winT, convw, convb, dtb, alog, dpar, nwr, woutT,
                     debug_taps=debug_taps, stop=stop)
    kb.finish()
    return kb.nc


def emit_l1_attn(kb, es, x, x3, winT, convw, convb, dtb, alog, dpar, nwr, woutT,
                 px="m_", debug_taps=False, stop=None):
    nc = kb.nc
    DI = 2 * D
    NH, HP, NST = 32, 64, 128
    CONVD = DI + 2 * NST
    CH = 64
    NCH = S // CH
    PAD = 3
    seg = np.ones((NH, S), dtype=BF); seg[:, ::CH] = 0.0
    consts = {
        "ident": np.eye(128, dtype=BF),
        "identf": np.eye(128, dtype=np.float32),
        "onesrow": np.ones((1, 128), dtype=BF),
        "onesrowf": np.ones((1, 128), dtype=np.float32),
        "segmask": seg,
        "bigmask": (1e9 * (np.arange(64)[:, None] > np.arange(64)[None, :])).astype(np.float32),
        "upm64": (np.arange(64)[:, None] <= np.arange(64)[None, :]).astype(np.float32),
    }
    cap = {k: nc.inline_tensor(np.ascontiguousarray(v), px + "c_" + k).ap() for k, v in consts.items()}

    sb, ps = _scoped_allocs(nc, es, px)
    ids = sb("ids", [128, 128], BF16)
    idf = sb("idf", [128, 128], F32)
    onesr = sb("onesr", [1, 128], BF16)
    onesrf = sb("onesrf", [1, 128], F32)
    segs = sb("segs", [NH, S], BF16)
    bigm = sb("bigm", [64, 64], F32)
    upm64 = sb("upm64", [64, 64], F32)
    xt2 = [sb(f"xt{i}", [128, D], F32) for i in range(2)]
    sq = sb("sq", [128, D], F32)
    xlnb = sb("xlnb", [128, D], BF16)
    xlnT = [sb(f"xlnT{i}", [128, S], BF16) for i in range(ND)]
    winc = [[sb(f"winc{p}_{i}", [128, 128], BF16) for i in range(ND)] for p in range(2)]
    xc = [sb(f"xc{i}", [128, PAD + S], BF16) for i in range(18)]
    cw = [sb(f"cw{i}", [128, 4], F32) for i in range(18)]
    cb = [sb(f"cb{i}", [128, 1], F32) for i in range(18)]
    cs1 = sb("cs1", [128, S], F32)
    dtv = sb("dtv", [NH, S], F32)
    dacs = sb("dacs", [NH, S], F32)
    dtbc = sb("dtbc", [NH, 1], F32)
    acol = sb("acol", [NH, 1], F32)
    onecol = sb("onecol", [NH, 1], F32)
    dpb = sb("dpb", [64, NH], F32)
    dacs_tm = sb("dacs_tm", [128, 512], F32)
    dtv_tm = sb("dtv_tm", [128, 512], F32)
    dtq = sb("dtq", [64, NH], F32)
    dc = sb("dc", [64, NH], F32)
    ea = sb("ea", [64, NH], F32)
    Gm = sb("Gm", [64, 64], BF16)
    dpx = sb("dpx", [64, DI], BF16)
    btm = sb("btm", [64, 128], BF16)
    cdrow = sb("cdrow", [1, NH], F32)
    yscr = sb("yscr", [64, 512], F32)
    zs = sb("zs", [128, S], BF16)
    rb128 = sb("rb128", [128, S], BF16)
    nws = sb("nws", [128, DI // 128], F32)
    wout = [sb(f"wout{i}", [128, D], BF16) for i in range(16)]
    epscol = sb("epscol", [128, 1], F32)
    onescol = sb("onescol", [128, 1], BF16)
    st = LNStats(nc, "m", sb)
    a0 = nc.lookup_mloc(xlnT[0]).addr
    at = lambda name, shape, dtype, offset: nc.alloc_sbuf_tensor_at(
        px + name, shape, dtype, offset=offset)
    A2f = at("A2f", [64, DI], F32, offset=a0)
    statef = at("statef", [128, DI], F32, offset=a0 + 8192)
    stateb = at("stateb", [128, DI], BF16, offset=a0 + 16384)
    x_tm = at("x_tm", [64, DI], BF16, offset=a0 + 20480)
    dtx = at("dtx", [64, DI], BF16, offset=a0 + 24576)
    MT = at("MT", [64, DI], BF16, offset=a0 + 28672)
    dtxd = x_tm
    zaddr = nc.lookup_mloc(zs).addr
    yh = at("yh", [64, DI], BF16, offset=zaddr)
    cdb64 = cs1
    dtm0 = nc.lookup_mloc(dacs_tm).addr
    assert nc.lookup_mloc(dtv_tm).addr == dtm0 + 2048
    wsum = at("wsum", [128, D], F32, offset=dtm0)
    rrowb = at("rrowb", [1, S], BF16, offset=dtm0)
    zdram = nc.dram_tensor(px + "zdram", [DI, S], BF16, kind="Internal").ap()

    p_trb = ps("p_trb", [128, 512], BF16)
    p_trf = ps("p_trf", [128, 512], F32)
    p_ya = [ps(f"p_ya{i}", [128, 512], F32) for i in range(2)]
    p_yo = [ps(f"p_yo{i}", [128, 512], F32) for i in range(2)]
    p_st = [ps(f"p_st{i}", [128, 512], F32) for i in range(2)]

    t, v, s = nc.tensor, nc.vector, nc.scalar
    kb.sdma(ids[:, :], cap["ident"][:, :])
    kb.sdma(idf[:, :], cap["identf"][:, :])
    kb.sdma(onesr[:, :], cap["onesrow"][:, :])
    kb.sdma(onesrf[:, :], cap["onesrowf"][:, :])
    kb.sdma(segs[:, :], cap["segmask"][:, :])
    kb.sdma(bigm[:, :], cap["bigmask"][:, :])
    kb.sdma(upm64[:, :], cap["upm64"][:, :])
    kb.sdma(nws[:, :], nwr[:, :])
    for j in range(18):
        kb.sdma(cw[j][:, :], convw[128 * j:128 * (j + 1), :])
        kb.sdma(cb[j][:, :], convb[128 * j:128 * (j + 1), :])
    kb.sdma(dtbc[:, :], dtb[:, :])
    kb.sdma(acol[:, :], alog[:, :])
    for j in range(16):
        kb.gdma(wout[j][:, :], woutT[128 * j:128 * (j + 1), :])
    kb.vop(v.memset, epscol[:, :], LN_EPS)
    kb.vop(v.memset, onecol[:, :], 1.0)
    with nc.allow_low_precision(reason="ones"):
        kb.vop(v.memset, onescol[:, :], 1.0)
    kb.barrier()
    kb.sop(s.activation, acol[:, :], acol[:, :], AF.Exp)
    kb.wait(v, "act")
    kb.vop(v.tensor_scalar_mul, acol[:, :], acol[:, :], -1.0)
    kb.sdma(dtq[0:NH, 0:1], dpar[:, :])
    kb.barrier()
    kb.tick("pe", t.transpose(p_trf[0:1, 0:NH], dtq[0:NH, 0:1], idf[0:NH, 0:NH]))
    kb.wait(v, "pe")
    kb.vop(v.tensor_copy, cdrow[0:1, 0:NH], p_trf[0:1, 0:NH])
    kb.wait(t, "dve")
    kb.tick("pe", t.matmul(p_trf[0:64, 0:NH], onesrf[0:1, 0:64], cdrow[0:1, 0:NH],
                           start=True, stop=True))
    kb.wait(v, "pe")
    kb.vop(v.tensor_copy, dpb[:, :], p_trf[0:64, 0:NH])
    kb.barrier()

    def load_win_cols(c0, n, par):
        kb.wait(nc.gpsimd, "pe")
        for i in range(ND):
            kb.gdma(winc[par][i][:, 0:n], winT[128 * i:128 * (i + 1), c0:c0 + n])
        return kb.mark("gdma")

    for b in range(BPC):
        tok0 = b * S
        tr_state = {"tr_evac": kb.mark("dve")}
        emit_ln_transpose_stream(kb, x, tok0, xt2, xlnb, sq, st, epscol, xlnT, ids,
                                 p_trb, tr_state)
        kb.barrier()
        if stop == "ln":
            break

        ldm = [None, None]
        ldm[0] = load_win_cols(2 * D, 128, 0)
        for j in range(18):
            par = j % 2
            kb.vop(v.memset, xc[j][:, 0:PAD], 0.0)
            if j + 1 < 18:
                ldm[1 - par] = load_win_cols(2 * D + 128 * (j + 1), 128, 1 - par)
            kb.wait_mark(t, ldm[par])
            ev = kb.mark("act")
            for tc in range(4):
                pq = p_st[tc % 2]
                kb.wait_mark(t, ev)
                for dt_ in range(ND):
                    ins = t.matmul(pq[:, :], winc[par][dt_][:, 0:128],
                                   xlnT[dt_][:, 512 * tc:512 * (tc + 1)],
                                   start=(dt_ == 0), stop=(dt_ == ND - 1))
                kb.tick("pe", ins)
                kb.wait(s, "pe")
                kb.sop(s.activation, xc[j][:, PAD + 512 * tc:PAD + 512 * (tc + 1)],
                       pq[:, :], AF.Copy)
                ev = kb.mark("act")
        ldm[0] = load_win_cols(0, 128, 0)
        for j in range(16):
            par = j % 2
            if j + 1 < 16:
                ldm[1 - par] = load_win_cols(128 * (j + 1), 128, 1 - par)
            kb.wait_mark(t, ldm[par])
            ev = kb.mark("act")
            for tc in range(4):
                pq = p_ya[tc % 2]
                kb.wait_mark(t, ev)
                for dt_ in range(ND):
                    ins = t.matmul(pq[:, :], winc[par][dt_][:, 0:128],
                                   xlnT[dt_][:, 512 * tc:512 * (tc + 1)],
                                   start=(dt_ == 0), stop=(dt_ == ND - 1))
                kb.tick("pe", ins)
                kb.wait(s, "pe")
                kb.wait(s, "sdma")
                kb.sop(s.activation, zs[:, 512 * tc:512 * (tc + 1)], pq[:, :], AF.Silu)
                ev = kb.mark("act")
            kb.wait(nc.sync, "act")
            kb.sdma(zdram[128 * j:128 * (j + 1), :], zs[:, :])
        kb.barrier()
        load_win_cols(2 * D + 2304, 32, 0)
        kb.barrier()
        ev = kb.mark("dve")
        for tc in range(4):
            pq = p_st[tc % 2]
            kb.wait_mark(t, ev)
            for dt_ in range(ND):
                ins = t.matmul(pq[0:NH, :], winc[0][dt_][:, 0:32],
                               xlnT[dt_][:, 512 * tc:512 * (tc + 1)],
                               start=(dt_ == 0), stop=(dt_ == ND - 1))
            kb.tick("pe", ins)
            kb.wait(v, "pe")
            kb.vop(v.tensor_copy, dtv[:, 512 * tc:512 * (tc + 1)], pq[0:NH, :])
            ev = kb.mark("dve")
        kb.barrier()

        if stop == "m1":
            break
        for j in range(18):
            kb.wait(v, "act")
            kb.vop(v.tensor_scalar_mul, cs1[:, :], xc[j][:, 0:S], cw[j][:, 0:1])
            for k_ in range(1, 4):
                kb.vop(v.scalar_tensor_tensor, cs1[:, :], xc[j][:, k_:k_ + S],
                       cw[j][:, k_:k_ + 1], cs1[:, :], ALU.mult, ALU.add)
            kb.wait(s, "dve")
            kb.sop(s.activation, xc[j][:, PAD:PAD + S], cs1[:, :], AF.Silu,
                   bias=cb[j][:, 0:1])
        kb.barrier()

        if stop == "m2":
            break
        kb.vop(v.tensor_scalar_add, dtv[:, :], dtv[:, :], dtbc[:, 0:1])
        kb.vop(v.tensor_scalar_max, dacs[:, :], dtv[:, :], 0.0)
        kb.wait(s, "dve")
        kb.sop(s.activation, cs1[0:NH, :], dtv[:, :], AF.Abs)
        kb.sop(s.activation, cs1[0:NH, :], cs1[0:NH, :], AF.Exp, scale=-1.0)
        kb.sop(s.activation, cs1[0:NH, :], cs1[0:NH, :], AF.Ln, bias=onecol[0:NH, 0:1])
        kb.wait(v, "act")
        kb.vop(v.tensor_add, dtv[:, :], dacs[:, :], cs1[0:NH, :])
        kb.vop(v.tensor_scalar_mul, cs1[0:NH, :], dtv[:, :], acol[:, 0:1])
        kb.vop(v.tensor_tensor_scan, dacs[:, :], segs[:, :], cs1[0:NH, :], 0.0,
               ALU.mult, ALU.add)
        ev = kb.mark("dve")
        for tt in range(NTB):
            kb.wait(t, "dve")
            kb.wait(t, "act")
            kb.wait_mark(t, ev)
            t.transpose(p_trf[:, 0:NH], dacs[0:NH, 128 * tt:128 * (tt + 1)],
                        idf[0:NH, 0:NH])
            kb.tick("pe", t.transpose(p_trf[:, 64:64 + NH],
                                      dtv[0:NH, 128 * tt:128 * (tt + 1)],
                                      idf[0:NH, 0:NH]))
            kb.wait(v, "pe")
            kb.vop(v.tensor_copy, dacs_tm[:, 32 * tt:32 * (tt + 1)], p_trf[:, 0:NH])
            kb.vop(v.tensor_copy, dtv_tm[:, 32 * tt:32 * (tt + 1)], p_trf[:, 64:64 + NH])
            ev = kb.mark("dve")
        kb.barrier()

        if stop == "m3":
            break
        kb.vop(v.memset, statef[:, :], 0.0)
        kb.vop(v.memset, stateb[:, :], 0.0)
        kb.barrier()

        for c in range(NCH):
            ttc, r0 = c // 2, 64 * (c % 2)
            cs_ = slice(PAD + CH * c, PAD + CH * (c + 1))
            ev = kb.mark("dve")
            for jg in range(4):
                kb.wait(t, "dve")
                kb.wait_mark(t, ev)
                for k_ in range(4):
                    j = 4 * jg + k_
                    ins = t.transpose(p_trb[0:64, 128 * k_:128 * (k_ + 1)],
                                      xc[j][:, cs_], ids[:, :])
                kb.tick("pe", ins)
                kb.wait(v, "pe")
                for k_ in range(4):
                    j = 4 * jg + k_
                    kb.vop(v.tensor_copy, x_tm[:, 128 * j:128 * (j + 1)],
                           p_trb[0:64, 128 * k_:128 * (k_ + 1)])
                ev = kb.mark("dve")
            kb.wait(t, "dve")
            t.transpose(p_trb[0:64, 0:128], xc[16][:, cs_], ids[:, :])
            t.matmul(p_trf[0:64, 0:64], xc[16][:, cs_], xc[17][:, cs_],
                     start=True, stop=True)
            kb.tick("pe", t.transpose(p_trf[0:1, 64:64 + NH],
                                      dacs[0:NH, CH * c + 63:CH * c + 64],
                                      idf[0:NH, 0:NH]))
            kb.wait(v, "pe")
            kb.vop(v.tensor_copy, btm[:, :], p_trb[0:64, 0:128])
            with nc.allow_low_precision(reason="bf16 G"):
                kb.vop(v.tensor_tensor, Gm[:, :], p_trf[0:64, 0:64], upm64[:, :], ALU.mult)
            kb.vop(v.tensor_copy, cdrow[0:1, 0:NH], p_trf[0:1, 64:64 + NH])
            kb.wait(s, "dve")
            kb.sop(s.activation, cdrow[0:1, 0:NH], cdrow[0:1, 0:NH], AF.Exp)
            kb.wait(v, "act")
            kb.vop(v.tensor_copy,
                   cdb64[0:1, 0:DI].rearrange("o (h q) -> o h q", q=HP),
                   cdrow[0:1, 0:NH].unsqueeze(2).broadcast_to((1, NH, HP)))
            kb.vop(v.tensor_copy, dtq[:, :], dtv_tm[r0:r0 + 64, 32 * ttc:32 * (ttc + 1)])
            x3d = x_tm[:, :].rearrange("p (h q) -> p h q", q=HP)
            kb.wait(v, "dve")
            with nc.allow_low_precision(reason="bf16 dtx"):
                kb.vop(v.tensor_tensor, dtx[:, :].rearrange("p (h q) -> p h q", q=HP),
                       x3d, dtq[:, :].unsqueeze(2).broadcast_to((64, NH, HP)), ALU.mult)
            kb.vop(v.tensor_copy, A2f[:, :].rearrange("p (h q) -> p h q", q=HP),
                   dacs_tm[r0:r0 + 64, 32 * ttc:32 * (ttc + 1)].unsqueeze(2)
                   .broadcast_to((64, NH, HP)))
            kb.wait(s, "dve")
            kb.sop(s.activation, ea[:, :],
                   dacs_tm[r0:r0 + 64, 32 * ttc:32 * (ttc + 1)], AF.Exp)
            evg = kb.mark("dve")
            for hg in range(4):
                kb.wait(t, "dve")
                kb.wait_mark(t, evg)
                for k_ in range(8):
                    h = 8 * hg + k_
                    ins = t.transpose(p_trf[0:64, 64 * k_:64 * (k_ + 1)],
                                      A2f[:, 64 * h:64 * (h + 1)], idf[0:64, 0:64])
                kb.tick("pe", ins)
                kb.wait(v, "pe")
                kb.vop(v.tensor_sub, A2f[:, 512 * hg:512 * (hg + 1)],
                       p_trf[0:64, :], A2f[:, 512 * hg:512 * (hg + 1)])
                kb.vop(v.tensor_sub,
                       A2f[:, 512 * hg:512 * (hg + 1)].rearrange("p (h q) -> p h q", q=HP),
                       A2f[:, 512 * hg:512 * (hg + 1)].rearrange("p (h q) -> p h q", q=HP),
                       bigm[:, :].unsqueeze(1).broadcast_to((64, 8, HP)))
                evg = kb.mark("dve")
            kb.wait(s, "dve")
            kb.sop(s.activation, A2f[:, :], A2f[:, :], AF.Exp)
            kb.wait(v, "act")
            with nc.allow_low_precision(reason="bf16 SSD L"):
                kb.vop(v.tensor_tensor, MT[:, :].rearrange("p (h q) -> p h q", q=HP),
                       A2f[:, :].rearrange("p (h q) -> p h q", q=HP),
                       Gm[:, :].unsqueeze(1).broadcast_to((64, NH, HP)), ALU.mult)
            kb.vop(v.tensor_copy, dc[:, :].unsqueeze(2),
                   A2f[:, :].rearrange("p (h q) -> p h q", q=HP)[:, :, 63:64])
            with nc.allow_low_precision(reason="bf16 dpx"):
                kb.vop(v.tensor_tensor, dpx[:, :].rearrange("p (h q) -> p h q", q=HP),
                       x3d, dpb[:, :].unsqueeze(2).broadcast_to((64, NH, HP)), ALU.mult)
            with nc.allow_low_precision(reason="bf16 dtxd"):
                kb.vop(v.tensor_tensor, dtxd[:, :].rearrange("p (h q) -> p h q", q=HP),
                       dtx[:, :].rearrange("p (h q) -> p h q", q=HP),
                       dc[:, :].unsqueeze(2).broadcast_to((64, NH, HP)), ALU.mult)
            for Hh in range(2):
                h0 = 16 * Hh
                kb.wait(t, "dve")
                for k_ in range(16):
                    h = h0 + k_
                    t.matmul(p_ya[k_ // 8][0:64, 64 * (k_ % 8):64 * (k_ % 8 + 1)],
                             MT[:, 64 * h:64 * (h + 1)], dtx[:, 64 * h:64 * (h + 1)],
                             start=True, stop=True)
                for k_ in range(16):
                    h = h0 + k_
                    ins = t.matmul(p_yo[k_ // 8][0:64, 64 * (k_ % 8):64 * (k_ % 8 + 1)],
                                   xc[17][:, cs_], stateb[:, 64 * h:64 * (h + 1)],
                                   start=True, stop=True)
                kb.tick("pe", ins)
                for sg_ in range(2):
                    ins = t.matmul(p_st[sg_][:, :], btm[:, :],
                                   dtxd[:, 1024 * Hh + 512 * sg_:1024 * Hh + 512 * (sg_ + 1)],
                                   start=True, stop=True)
                kb.tick("pe", ins)
                kb.wait(v, "pe")
                for sg_ in range(2):
                    so = 1024 * Hh + 512 * sg_
                    kb.vop(v.tensor_tensor, yscr[:, :].rearrange("p (h q) -> p h q", q=HP),
                           p_yo[sg_][0:64, :].rearrange("p (h q) -> p h q", q=HP),
                           ea[:, h0 + 8 * sg_:h0 + 8 * (sg_ + 1)].unsqueeze(2)
                           .broadcast_to((64, 8, HP)), ALU.mult)
                    kb.vop(v.tensor_add, yscr[:, :], yscr[:, :], p_ya[sg_][0:64, :])
                    with nc.allow_low_precision(reason="bf16 yh"):
                        kb.vop(v.tensor_add, yh[:, so:so + 512], yscr[:, :],
                               dpx[:, so:so + 512])
                kb.wait(t, "dve")
                for sg_ in range(2):
                    so = 1024 * Hh + 512 * sg_
                    ins = t.matmul(p_ya[sg_][:, :], onesrf[0:1, :],
                                   cdb64[0:1, so:so + 512], start=True, stop=True)
                kb.tick("pe", ins)
                kb.wait(v, "pe")
                for sg_ in range(2):
                    so = 1024 * Hh + 512 * sg_
                    sl4 = slice(so, so + 512)
                    kb.vop(v.tensor_tensor, statef[:, sl4], statef[:, sl4],
                           p_ya[sg_][:, :], ALU.mult)
                    kb.vop(v.tensor_add, statef[:, sl4], statef[:, sl4],
                           p_st[sg_][:, :])
                    with nc.allow_low_precision(reason="bf16 state copy"):
                        kb.vop(v.tensor_copy, stateb[:, sl4], statef[:, sl4])
            ev = kb.mark("dve")
            for jg in range(2):
                kb.wait(t, "dve")
                kb.wait_mark(t, ev)
                for k_ in range(8):
                    j = 8 * jg + k_
                    ins = t.transpose(p_trb[:, 64 * k_:64 * (k_ + 1)],
                                      yh[:, 128 * j:128 * (j + 1)], ids[0:64, 0:64])
                kb.tick("pe", ins)
                kb.wait(v, "pe")
                for k_ in range(8):
                    j = 8 * jg + k_
                    kb.vop(v.tensor_copy, xc[j][:, cs_], p_trb[:, 64 * k_:64 * (k_ + 1)])
                ev = kb.mark("dve")
        kb.barrier()

        if stop == "m4":
            break
        kb.vop(v.memset, cs1[0:1, 0:S], 0.0)
        for j in range(16):
            kb.wait(nc.sync, "pe")
            kb.wait(nc.sync, "dve")
            kb.sdma(zs[:, :], zdram[128 * j:128 * (j + 1), :])
            kb.wait(v, "sdma")
            with nc.allow_low_precision(reason="bf16 gated stream"):
                kb.vop(v.tensor_tensor, xc[j][:, PAD:PAD + S], xc[j][:, PAD:PAD + S],
                       zs[:, :], ALU.mult)
            kb.wait(s, "dve")
            kb.sop(s.activation, zs[:, :], xc[j][:, PAD:PAD + S], AF.Square)
            kb.wait(t, "act")
            for tc in range(4):
                kb.tick("pe", t.matmul(p_trf[0:1, 0:512], onescol[:, 0:1],
                                       zs[:, 512 * tc:512 * (tc + 1)],
                                       start=True, stop=True))
                kb.wait(v, "pe")
                kb.vop(v.tensor_add, cs1[0:1, 512 * tc:512 * (tc + 1)],
                       cs1[0:1, 512 * tc:512 * (tc + 1)], p_trf[0:1, 0:512])
                kb.wait(t, "dve")
        kb.barrier()
        if stop == "m5g":
            break
        kb.sop(s.activation, cs1[0:1, 0:S], cs1[0:1, 0:S], AF.Ln,
               scale=1.0 / DI, bias=epscol[0:1, 0:1])
        kb.sop(s.activation, cs1[0:1, 0:S], cs1[0:1, 0:S], AF.Exp, scale=-0.5)
        kb.wait(v, "act")
        with nc.allow_low_precision(reason="bf16 rstd"):
            kb.vop(v.tensor_copy, rrowb[0:1, :], cs1[0:1, 0:S])
        kb.wait(t, "dve")
        ev = kb.mark("dve")
        for tc in range(4):
            kb.wait_mark(t, ev)
            kb.tick("pe", t.matmul(p_trf[:, 0:512], onesr[0:1, :],
                                   rrowb[0:1, 512 * tc:512 * (tc + 1)],
                                   start=True, stop=True))
            kb.wait(v, "pe")
            with nc.allow_low_precision(reason="bf16 rstd bcast"):
                kb.vop(v.tensor_copy, rb128[:, 512 * tc:512 * (tc + 1)], p_trf[:, 0:512])
            ev = kb.mark("dve")
        for j in range(16):
            with nc.allow_low_precision(reason="bf16 gn"):
                kb.vop(v.scalar_tensor_tensor, xc[j][:, PAD:PAD + S],
                       xc[j][:, PAD:PAD + S], nws[:, j:j + 1], rb128[:, :],
                       ALU.mult, ALU.mult)
        kb.barrier()
        res_done = [kb.mark("dve"), kb.mark("dve")]
        for tt in range(NTB):
            xb_ = xt2[tt % 2]
            kb.wait_mark(nc.sync, res_done[tt % 2])
            kb.sdma(xb_[:, :], x[tok0 + 128 * tt: tok0 + 128 * (tt + 1), :])
            kb.wait(t, "dve")
            for cc_, pw in ((0, p_st[0]), (1, p_st[1])):
                for j in range(16):
                    ins = t.matmul(pw[:, :], xc[j][:, PAD + 128 * tt:PAD + 128 * (tt + 1)],
                                   wout[j][:, 512 * cc_:512 * (cc_ + 1)],
                                   start=(j == 0), stop=(j == 15))
                kb.tick("pe", ins)
            kb.wait(v, "pe")
            kb.wait(v, "sdma")
            kb.vop(v.tensor_add, wsum[:, 0:512], xb_[:, 0:512], p_st[0][:, :])
            kb.vop(v.tensor_add, wsum[:, 512:1024], xb_[:, 512:1024], p_st[1][:, :])
            res_done[tt % 2] = kb.mark("dve")
            kb.wait(nc.sync, "dve")
            kb.sdma(x3[tok0 + 128 * tt: tok0 + 128 * (tt + 1), :], wsum[:, :])
        kb.barrier()
        if stop == "m5w":
            break
    if debug_taps:
        groups = {
            "ln": {"d_xlnT0": xlnT[0]},
            "m1": {"d_xlnT0": xlnT[0], "d_xc5": xc[5], "d_dtv": dtv},
            "m2": {"d_xc5": xc[5], "d_xc16": xc[16], "d_xc17": xc[17]},
            "m3": {"d_dtv": dtv, "d_dacs": dacs, "d_dacs_tm": dacs_tm},
            "m4": {"d_xc5": xc[5], "d_yh": yh, "d_statef": statef, "d_xtm": x_tm,
                   "d_dtx": dtx, "d_MT": MT, "d_A2f": A2f, "d_Gm": Gm, "d_ea": ea,
                   "d_dc": dc, "d_dpb": dpb, "d_dacs_tm": dacs_tm},
            "m5g": {"d_xc5": xc[5], "d_cs1": cs1},
            "m5w": {"d_rb128": rb128, "d_xc5": xc[5], "d_cs1": cs1},
        }
        taps = groups.get(stop, {"d_xlnT0": xlnT[0]})
        for nm, tile in taps.items():
            shp = list(tile.shape)
            o = nc.dram_tensor(nm, shp, tile.dtype, kind="ExternalOutput").ap()
            kb.sdma(o[:, :], tile[:, :])
        kb.barrier()


def build_l1_gate():
    import contextlib
    kb = KB("l1gate")
    nc = kb.nc
    E = 16
    x = nc.dram_tensor("x", [TPC, D], F32, kind="ExternalInput").ap()
    gwT = nc.dram_tensor("gwT", [D, E], F32, kind="ExternalInput").ap()
    o_xln = nc.dram_tensor("o_xln", [TPC, D], BF16, kind="ExternalOutput").ap()
    o_oh = nc.dram_tensor("o_oh", [TPC, E], F32, kind="ExternalOutput").ap()
    o_loc = nc.dram_tensor("o_loc", [TPC, 1], F32, kind="ExternalOutput").ap()
    o_pmax = nc.dram_tensor("o_pmax", [TPC, 1], F32, kind="ExternalOutput").ap()
    o_eidx = nc.dram_tensor("o_eidx", [TPC, 1], F32, kind="ExternalOutput").ap()
    o_cnt = nc.dram_tensor("o_cnt", [1, E], F32, kind="ExternalOutput").ap()
    with contextlib.ExitStack() as es:
        emit_l1_gate(kb, es, x, o_xln, o_oh, o_loc, o_pmax, o_eidx, o_cnt, gwT)
    kb.finish()
    return kb.nc


def emit_l1_gate(kb, es, x, o_xln, o_oh, o_loc, o_pmax, o_eidx, o_cnt, gwT, px="g_"):
    nc = kb.nc
    E = 16
    consts = {
        "ident": np.eye(128, dtype=BF),
        "ustrict": (np.arange(128)[:, None] < np.arange(128)[None, :]).astype(BF),
        "iota16": np.broadcast_to(np.arange(16, dtype=np.float32), (128, 16)).copy(),
        "ones16": np.ones((128, 16), dtype=np.float32),
    }
    cap = {k: nc.inline_tensor(np.ascontiguousarray(v), px + "c_" + k).ap() for k, v in consts.items()}

    sb, ps = _scoped_allocs(nc, es, px)
    ids = sb("ids", [128, 128], BF16)
    ustr = sb("ustr", [128, 128], BF16)
    iot = sb("iot", [128, E], F32)
    on16 = sb("on16", [128, E], F32)
    onescol = sb("onescol", [128, 1], BF16)
    onesrf = sb("onesrf2", [1, 128], F32)
    gws = [sb(f"gws{i}", [128, E], BF16) for i in range(ND)]
    xt2 = [sb(f"xt{i}", [128, D], F32) for i in range(2)]
    sq = sb("sq", [128, D], F32)
    xlnb = sb("xlnb", [128, D], BF16)
    xlt = [sb(f"xlt{i}", [128, 128], BF16) for i in range(ND)]
    lg = sb("lg", [128, E], F32)
    ohb = sb("ohb", [128, E], BF16)
    ohf = sb("ohf", [128, E], F32)
    cum = sb("cum", [128, E], F32)
    loc = sb("loc", [128, E], F32)
    gmax = sb("gmax", [128, 1], F32)
    gsum = sb("gsum", [128, 1], F32)
    loccol = sb("loccol", [128, 1], F32)
    eidxc = sb("eidxc", [128, 1], F32)
    roff = sb("roff", [1, E], F32)
    epscol = sb("epscol", [128, 1], F32)
    st = LNStats(nc, "g", sb)

    p_tr = ps("p_tr", [128, 512], BF16)
    p_g = ps("p_g", [128, E], F32)
    p_w = ps("p_w", [128, E], F32)
    p_b = ps("p_b", [128, E], F32)
    p_r = ps("p_r", [1, E], F32)

    t, v, s = nc.tensor, nc.vector, nc.scalar
    kb.sdma(ids[:, :], cap["ident"][:, :])
    kb.sdma(ustr[:, :], cap["ustrict"][:, :])
    kb.sdma(iot[:, :], cap["iota16"][:, :])
    kb.sdma(on16[:, :], cap["ones16"][:, :])
    for i in range(ND):
        kb.gdma(gws[i][:, :], gwT[128 * i:128 * (i + 1), :])
    kb.vop(v.memset, epscol[:, :], LN_EPS)
    with nc.allow_low_precision(reason="ones"):
        kb.vop(v.memset, onescol[:, :], 1.0)
    kb.vop(v.memset, onesrf[:, :], 1.0)
    kb.vop(v.memset, roff[:, :], 0.0)
    kb.barrier()

    ln_done = [kb.mark("act"), kb.mark("act")]
    tr_evac = kb.mark("pe")
    for gt in range(TPC // 128):
        xb_ = xt2[gt % 2]
        kb.wait_mark(nc.sync, ln_done[gt % 2])
        kb.sdma(xb_[:, :], x[128 * gt:128 * (gt + 1), :])
        kb.wait(v, "sdma")
        kb.wait(s, "sdma")
        kb.wait(s, "pe")
        ln_tile(kb, xb_[:, :], xlnb[:, :], sq[:, :], st, epscol)
        ln_done[gt % 2] = kb.mark("act")
        kb.wait(nc.sync, "act")
        kb.sdma(o_xln[128 * gt:128 * (gt + 1), :], xlnb[:, :])
        kb.wait(t, "act")
        for g in range(2):
            kb.wait(t, "dve")
            for k_ in range(4):
                dt = 4 * g + k_
                ins = t.transpose(p_tr[:, 128 * k_:128 * (k_ + 1)],
                                  xlnb[:, 128 * dt:128 * (dt + 1)], ids[:, :])
            kb.tick("pe", ins)
            kb.wait(v, "pe")
            for k_ in range(4):
                dt = 4 * g + k_
                kb.vop(v.tensor_copy, xlt[dt][:, :], p_tr[:, 128 * k_:128 * (k_ + 1)])
        kb.wait(t, "dve")
        for dt in range(ND):
            ins = t.matmul(p_g[:, :], xlt[dt][:, :], gws[dt][:, :],
                           start=(dt == 0), stop=(dt == ND - 1))
        kb.tick("pe", ins)
        kb.wait(v, "pe")
        kb.vop(v.tensor_copy, lg[:, :], p_g[:, :])
        kb.vop(v.reduce_max, gmax[:, :], lg[:, :], AX.X, negate=True)
        kb.wait(s, "dve")
        kb.sop(s.activation, lg[:, :], lg[:, :], AF.Exp, bias=gmax[:, 0:1])
        kb.wait(v, "act")
        kb.vop(v.reduce_sum, gsum[:, :], lg[:, :], AX.X)
        kb.vop(v.reciprocal, gsum[:, :], gsum[:, :])
        kb.vop(v.tensor_scalar_mul, lg[:, :], lg[:, :], gsum[:, 0:1])
        kb.vop(v.reduce_max, gmax[:, :], lg[:, :], AX.X)
        kb.vop(v.tensor_scalar, ohf[:, :], lg[:, :], gmax[:, 0:1], None, ALU.is_equal)
        kb.vop(v.tensor_tensor_scan, cum[:, :], on16[:, :], ohf[:, :], 0.0,
               ALU.mult, ALU.add)
        kb.vop(v.tensor_single_scalar, cum[:, :], cum[:, :], 1.0, ALU.is_equal)
        kb.vop(v.tensor_tensor, ohf[:, :], ohf[:, :], cum[:, :], ALU.mult)
        with nc.allow_low_precision(reason="bf16 onehot"):
            kb.vop(v.tensor_copy, ohb[:, :], ohf[:, :])
        kb.vop(v.tensor_tensor, cum[:, :], ohf[:, :], iot[:, :], ALU.mult)
        kb.vop(v.reduce_sum, eidxc[:, :], cum[:, :], AX.X)
        kb.wait(t, "dve")
        t.matmul(p_w[:, :], ustr[:, :], ohb[:, :], start=True, stop=False)
        kb.tick("pe", t.matmul(p_w[:, :], onesrf[0:1, :], roff[0:1, :],
                               start=False, stop=True))
        kb.wait(v, "pe")
        kb.vop(v.tensor_tensor, loc[:, :], p_w[:, :], ohf[:, :], ALU.mult)
        kb.vop(v.reduce_sum, loccol[:, :], loc[:, :], AX.X)
        kb.wait(t, "dve")
        kb.tick("pe", t.matmul(p_r[0:1, :], onescol[:, 0:1], ohb[:, :],
                               start=True, stop=True))
        kb.wait(v, "pe")
        kb.vop(v.tensor_add, roff[0:1, :], roff[0:1, :], p_r[0:1, :])
        kb.wait(nc.sync, "dve")
        kb.sdma(o_oh[128 * gt:128 * (gt + 1), :], ohf[:, :])
        kb.sdma(o_loc[128 * gt:128 * (gt + 1), :], loccol[:, :])
        kb.sdma(o_pmax[128 * gt:128 * (gt + 1), :], gmax[:, :])
        kb.sdma(o_eidx[128 * gt:128 * (gt + 1), :], eidxc[:, :])
    kb.wait(nc.sync, "dve")
    kb.sdma(o_cnt[:, :], roff[:, :])


def build_l1_slots():
    kb = KB("l1slots")
    nc = kb.nc
    E, C = 16, 2560
    allcnt = nc.dram_tensor("allcnt", [NCORES, E], F32, kind="ExternalInput").ap()
    coremask = nc.dram_tensor("coremask", [NCORES, 1], F32, kind="ExternalInput").ap()
    tokidx = nc.dram_tensor("tokidx", [TPC, 1], F32, kind="ExternalInput").ap()
    oh = nc.dram_tensor("oh", [TPC, E], F32, kind="ExternalInput").ap()
    loc = nc.dram_tensor("loc", [TPC, 1], F32, kind="ExternalInput").ap()
    pmax = nc.dram_tensor("pmax", [TPC, 1], F32, kind="ExternalInput").ap()
    eidx = nc.dram_tensor("eidx", [TPC, 1], F32, kind="ExternalInput").ap()
    o_idx = nc.dram_tensor("o_idx", [E * C, 1], F32, kind="ExternalOutput").ap()
    o_w = nc.dram_tensor("o_w", [TPC, 1], F32, kind="ExternalOutput").ap()
    o_slot = nc.dram_tensor("o_slot", [TPC, 1], F32, kind="ExternalOutput").ap()

    sb = nc.alloc_sbuf_tensor
    acnt = sb("acnt", [NCORES, E], F32)
    cmsk = sb("cmsk", [NCORES, 1], F32)
    cobc = sb("cobc", [128, E], F32)
    ohs = sb("ohs", [128, E], F32)
    locs = sb("locs", [128, 1], F32)
    pms = sb("pms", [128, 1], F32)
    eis = sb("eis", [128, 1], F32)
    tks = sb("tks", [128, 1], F32)
    sel = sb("sel", [128, E], F32)
    gl = sb("gl", [128, 1], F32)
    kp = sb("kp", [128, 1], F32)
    wc = sb("wc", [128, 1], F32)
    sl2 = sb("sl2", [128, 1], F32)
    vals = sb("vals", [128, 1], F32)
    sidx = sb("sidx", [128, 1], I32)
    zt = sb("zt", [128, 320], F32)
    p_c = nc.alloc_psum_tensor("p_c", [128, E], F32)

    t, v, s = nc.tensor, nc.vector, nc.scalar
    kb.sdma(acnt[:, :], allcnt[:, :])
    kb.sdma(cmsk[:, :], coremask[:, :])
    kb.vop(v.memset, zt[:, :], 0.0)
    kb.barrier()
    kb.sdma(o_idx[:, :].rearrange("(p q) o -> p (q o)", p=128), zt[:, :])
    onesr = sb("onesr", [1, 128], F32)
    kb.vop(v.memset, onesr[:, :], 1.0)
    kb.tick("pe", t.matmul(p_c[0:1, :], cmsk[:, :], acnt[:, :], start=True, stop=True))
    kb.wait(v, "pe")
    kb.vop(v.tensor_copy, cobc[0:1, :], p_c[0:1, :])
    kb.wait(t, "dve")
    kb.tick("pe", t.matmul(p_c[:, :], onesr[0:1, :], cobc[0:1, :], start=True, stop=True))
    kb.wait(v, "pe")
    kb.vop(v.tensor_copy, cobc[:, :], p_c[:, :])
    kb.barrier()

    for gt in range(TPC // 128):
        r = slice(128 * gt, 128 * (gt + 1))
        kb.wait(nc.sync, "dve")
        kb.wait(nc.sync, "gdma")
        kb.sdma(ohs[:, :], oh[r, :])
        kb.sdma(locs[:, :], loc[r, :])
        kb.sdma(pms[:, :], pmax[r, :])
        kb.sdma(eis[:, :], eidx[r, :])
        kb.sdma(tks[:, :], tokidx[r, :])
        kb.wait(v, "sdma")
        kb.vop(v.tensor_tensor, sel[:, :], ohs[:, :], cobc[:, :], ALU.mult)
        kb.vop(v.reduce_sum, gl[:, :], sel[:, :], AX.X)
        kb.vop(v.tensor_add, gl[:, :], gl[:, :], locs[:, :])
        kb.vop(v.tensor_single_scalar, kp[:, :], gl[:, :], float(C), ALU.is_lt)
        kb.vop(v.tensor_tensor, wc[:, :], pms[:, :], kp[:, :], ALU.mult)
        kb.vop(v.tensor_scalar, sl2[:, :], eis[:, :], float(C), gl[:, 0:1],
               ALU.mult, ALU.add)
        kb.vop(v.tensor_scalar, kp[:, :], kp[:, :], -1e9, 1e9, ALU.mult, ALU.add)
        kb.vop(v.tensor_add, sl2[:, :], sl2[:, :], kp[:, :])
        kb.vop(v.tensor_scalar_add, vals[:, :], tks[:, :], 1.0)
        with nc.allow_low_precision(reason="int index cast"):
            kb.vop(v.tensor_copy, sidx[:, :], sl2[:, :])
        kb.wait(nc.sync, "dve")
        kb.sdma(o_w[r, :], wc[:, :])
        kb.sdma(o_slot[r, :], sl2[:, :])
        kb.wait(nc.gpsimd, "dve")
        kb.wait(nc.gpsimd, "sdma")
        nm_ = kb._dma_slot("gdma")
        ins = nc.gpsimd.indirect_dma_start(
            o_idx[:, :], bass.IndirectOffsetOnAxis(sidx[:, 0:1], 0),
            vals[:, 0:1], None, bounds_check=E * C - 1, oob_is_err=False)
        kb._dma_post(nm_, ins)
    kb.finish()
    return kb.nc


def build_l1_experts():
    kb = KB("l1exp")
    nc = kb.nc
    E, C, FF = 16, 2560, 4096
    NEC = 2
    idx2T = nc.dram_tensor("idx2T", [NEC * C, NCORES], F32, kind="ExternalInput").ap()
    xga = nc.dram_tensor("xga", [B * S, D], BF16, kind="ExternalInput").ap()
    w1 = nc.dram_tensor("w1", [NEC, D, FF], F32, kind="ExternalInput").ap()
    w2 = nc.dram_tensor("w2", [NEC, FF, D], F32, kind="ExternalInput").ap()
    o_y = nc.dram_tensor("o_y", [NEC * C, D], BF16, kind="ExternalOutput").ap()

    sb = nc.alloc_sbuf_tensor
    idc = nc.inline_tensor(np.eye(128, dtype=BF), "c_ident").ap()
    idss = sb("idss", [128, 128], BF16)
    code = sb("code", [128, NCORES], F32)
    gidx = sb("gidx", [128, 1], F32)
    gemp = sb("gemp", [128, 1], F32)
    gi32 = sb("gi32", [128, 1], I32)
    xg = sb("xg", [128, D], BF16)
    xgT = [sb(f"xgT{i}", [128, C], BF16) for i in range(ND)]
    hT = [sb(f"hT{i}", [128, 512], BF16) for i in range(32)]
    w1s = [sb(f"w1s{i}", [128, 512], BF16) for i in range(ND)]
    w2s = [sb(f"w2s{i}", [128, D], BF16) for i in range(32)]
    yt = sb("yt", [128, D], BF16)
    p_tr = nc.alloc_psum_tensor("p_tr", [128, 512], BF16)
    p_h = [nc.alloc_psum_tensor(f"p_h{i}", [128, 512], F32) for i in range(2)]
    p_y = [nc.alloc_psum_tensor(f"p_y{i}", [128, 512], F32) for i in range(4)]

    t, v, s = nc.tensor, nc.vector, nc.scalar
    kb.sdma(idss[:, :], idc[:, :])
    kb.barrier()

    for e in range(NEC):
        kb.wait(nc.gpsimd, "pe")
        for i in range(32):
            kb.gdma(w2s[i][:, :], w2[e, 128 * i:128 * (i + 1), :])
        tr_ev = kb.mark("dve")
        for tl in range(20):
            sl0 = e * C + 128 * tl
            kb.wait(nc.sync, "dve")
            kb.sdma(code[:, :], idx2T[sl0:sl0 + 128, :])
            kb.wait(v, "sdma")
            kb.vop(v.reduce_sum, gidx[:, :], code[:, :], AX.X)
            kb.vop(v.tensor_single_scalar, gemp[:, :], gidx[:, :], 0.0, ALU.is_equal)
            kb.vop(v.tensor_scalar, gemp[:, :], gemp[:, :], 1e9, -1.0, ALU.mult, ALU.add)
            kb.vop(v.tensor_add, gidx[:, :], gidx[:, :], gemp[:, :])
            with nc.allow_low_precision(reason="index cast"):
                kb.vop(v.tensor_copy, gi32[:, :], gidx[:, :])
            kb.wait(v, "gdma")
            kb.wait(v, "pe")
            kb.vop(v.memset, xg[:, :], 0.0)
            kb.wait(nc.gpsimd, "dve")
            nm_ = kb._dma_slot("gdma")
            ins = nc.gpsimd.indirect_dma_start(
                xg[:, :], None, xga[:, :],
                bass.IndirectOffsetOnAxis(gi32[:, 0:1], 0),
                bounds_check=B * S - 1, oob_is_err=False)
            kb._dma_post(nm_, ins)
            kb.wait(t, "gdma")
            for g in range(2):
                kb.wait(t, "dve")
                for k_ in range(4):
                    dt = 4 * g + k_
                    ins = t.transpose(p_tr[:, 128 * k_:128 * (k_ + 1)],
                                      xg[:, 128 * dt:128 * (dt + 1)], idss[:, :])
                    if k_ == 3:
                        kb.tick("pe", ins)
                kb.wait(v, "pe")
                for k_ in range(4):
                    dt = 4 * g + k_
                    kb.vop(v.tensor_copy, xgT[dt][:, 128 * tl:128 * (tl + 1)],
                           p_tr[:, 128 * k_:128 * (k_ + 1)])
            tr_ev = kb.mark("dve")
        kb.barrier()
        for sc in range(5):
            ssl = slice(512 * sc, 512 * (sc + 1))
            for fq in range(8):
                kb.wait(nc.gpsimd, "pe")
                for i in range(ND):
                    kb.gdma(w1s[i][:, :], w1[e, 128 * i:128 * (i + 1),
                                             512 * fq:512 * (fq + 1)])
                kb.wait(t, "gdma")
                hev = kb.mark("act")
                for ft in range(4):
                    ph = p_h[ft % 2]
                    kb.wait_mark(t, hev)
                    for dt in range(ND):
                        ins = t.matmul(ph[:, :], w1s[dt][:, 128 * ft:128 * (ft + 1)],
                                       xgT[dt][:, ssl],
                                       start=(dt == 0), stop=(dt == ND - 1))
                    kb.tick("pe", ins)
                    kb.wait(s, "pe")
                    kb.sop(s.activation, hT[4 * fq + ft][:, :], ph[:, :], AF.Relu)
                    hev = kb.mark("act")
            kb.barrier()
            yev = [kb.mark("dve"), kb.mark("dve")]
            for st_ in range(4):
                pa, pb_ = p_y[2 * (st_ % 2)], p_y[2 * (st_ % 2) + 1]
                kb.wait_mark(t, yev[st_ % 2])
                for fl in range(32):
                    t.matmul(pa[:, :], hT[fl][:, 128 * st_:128 * (st_ + 1)],
                             w2s[fl][:, 0:512], start=(fl == 0), stop=(fl == 31))
                for fl in range(32):
                    ins = t.matmul(pb_[:, :], hT[fl][:, 128 * st_:128 * (st_ + 1)],
                                   w2s[fl][:, 512:1024], start=(fl == 0), stop=(fl == 31))
                kb.tick("pe", ins)
                kb.wait(v, "pe")
                kb.wait(v, "sdma")
                with nc.allow_low_precision(reason="bf16 expert out"):
                    kb.vop(v.tensor_copy, yt[:, 0:512], pa[:, :])
                    kb.vop(v.tensor_copy, yt[:, 512:1024], pb_[:, :])
                yev[st_ % 2] = kb.mark("dve")
                kb.wait(nc.sync, "dve")
                kb.sdma(o_y[e * C + 512 * sc + 128 * st_:
                            e * C + 512 * sc + 128 * (st_ + 1), :], yt[:, :])
            kb.barrier()
    kb.finish()
    return kb.nc


def build_l1_combine():
    kb = KB("l1comb")
    nc = kb.nc
    E, C = 16, 2560
    x = nc.dram_tensor("x", [TPC, D], F32, kind="ExternalInput").ap()
    ya = nc.dram_tensor("ya", [E * C, D], BF16, kind="ExternalInput").ap()
    slot = nc.dram_tensor("slot", [TPC, 1], F32, kind="ExternalInput").ap()
    w = nc.dram_tensor("w", [TPC, 1], F32, kind="ExternalInput").ap()
    out = nc.dram_tensor("out", [TPC, D], F32, kind="ExternalOutput").ap()

    sb = nc.alloc_sbuf_tensor
    xs = sb("xs", [128, D], F32)
    yg = sb("yg", [128, D], BF16)
    sc_ = sb("sc", [128, 1], F32)
    wc = sb("wc", [128, 1], F32)
    si32 = sb("si32", [128, 1], I32)
    os_ = sb("os", [128, D], F32)
    v = nc.vector
    for gt in range(TPC // 128):
        r = slice(128 * gt, 128 * (gt + 1))
        kb.wait(nc.sync, "dve")
        kb.sdma(xs[:, :], x[r, :])
        kb.sdma(sc_[:, :], slot[r, :])
        kb.sdma(wc[:, :], w[r, :])
        kb.wait(v, "sdma")
        kb.wait(v, "gdma")
        kb.vop(v.memset, yg[:, :], 0.0)
        with nc.allow_low_precision(reason="index cast"):
            kb.vop(v.tensor_copy, si32[:, :], sc_[:, :])
        kb.wait(nc.gpsimd, "dve")
        nm_ = kb._dma_slot("gdma")
        ins = nc.gpsimd.indirect_dma_start(
            yg[:, :], None, ya[:, :], bass.IndirectOffsetOnAxis(si32[:, 0:1], 0),
            bounds_check=E * C - 1, oob_is_err=False)
        kb._dma_post(nm_, ins)
        kb.wait(v, "gdma")
        kb.vop(v.scalar_tensor_tensor, os_[:, :], yg[:, :], wc[:, 0:1], xs[:, :],
               ALU.mult, ALU.add)
        kb.wait(nc.sync, "dve")
        kb.sdma(out[r, :], os_[:, :])
    kb.finish()
    return kb.nc


def build_fused():
    import contextlib
    kb = KB("fused")
    nc = kb.nc
    x = nc.dram_tensor("x", [TPC, D], F32, kind="ExternalInput").ap()
    wqkvT = nc.dram_tensor("wqkvT", [D, 3 * D], F32, kind="ExternalInput").ap()
    woT = nc.dram_tensor("woT", [D, D], F32, kind="ExternalInput").ap()
    wr = nc.dram_tensor("wr", [D, 8], F32, kind="ExternalInput").ap()
    br = nc.dram_tensor("br", [1, 8], F32, kind="ExternalInput").ap()
    w1 = nc.dram_tensor("w1", [8, D, 4096], F32, kind="ExternalInput").ap()
    b1r = nc.dram_tensor("b1r", [8, 128, 32], F32, kind="ExternalInput").ap()
    w2 = nc.dram_tensor("w2", [8, 4096, D], F32, kind="ExternalInput").ap()
    b2 = nc.dram_tensor("b2", [8, D], F32, kind="ExternalInput").ap()
    winT = nc.dram_tensor("winT", [D, 4384], F32, kind="ExternalInput").ap()
    convw = nc.dram_tensor("convw", [2304, 4], F32, kind="ExternalInput").ap()
    convb = nc.dram_tensor("convb", [2304, 1], F32, kind="ExternalInput").ap()
    dtb = nc.dram_tensor("dtb", [32, 1], F32, kind="ExternalInput").ap()
    alog = nc.dram_tensor("alog", [32, 1], F32, kind="ExternalInput").ap()
    dpar = nc.dram_tensor("dpar", [32, 1], F32, kind="ExternalInput").ap()
    nwr = nc.dram_tensor("nwr", [128, 16], F32, kind="ExternalInput").ap()
    woutT = nc.dram_tensor("woutT", [2 * D, D], F32, kind="ExternalInput").ap()
    gwT = nc.dram_tensor("gwT", [D, 16], F32, kind="ExternalInput").ap()

    x1 = nc.dram_tensor("x1", [TPC, D], F32, kind="Internal").ap()
    x2 = nc.dram_tensor("x2", [TPC, D], F32, kind="Internal").ap()
    x3 = nc.dram_tensor("x3", [TPC, D], F32, kind="ExternalOutput").ap()
    o_xln = nc.dram_tensor("o_xln", [TPC, D], BF16, kind="ExternalOutput").ap()
    o_oh = nc.dram_tensor("o_oh", [TPC, 16], F32, kind="ExternalOutput").ap()
    o_loc = nc.dram_tensor("o_loc", [TPC, 1], F32, kind="ExternalOutput").ap()
    o_pmax = nc.dram_tensor("o_pmax", [TPC, 1], F32, kind="ExternalOutput").ap()
    o_eidx = nc.dram_tensor("o_eidx", [TPC, 1], F32, kind="ExternalOutput").ap()
    o_cnt = nc.dram_tensor("o_cnt", [1, 16], F32, kind="ExternalOutput").ap()

    with contextlib.ExitStack() as es:
        emit_l0_attn(kb, es, x, x1, wqkvT, woT)
    kb.barrier()
    with contextlib.ExitStack() as es:
        emit_l0_ffn(kb, es, x1, x2, wr, br, w1, b1r, w2, b2)
    kb.barrier()
    with contextlib.ExitStack() as es:
        emit_l1_attn(kb, es, x2, x3, winT, convw, convb, dtb, alog, dpar, nwr, woutT)
    kb.barrier()
    with contextlib.ExitStack() as es:
        emit_l1_gate(kb, es, x3, o_xln, o_oh, o_loc, o_pmax, o_eidx, o_cnt, gwT)
    kb.finish()
    return kb.nc

def _shard_rows(a):
    a = np.ascontiguousarray(a)
    return [a[i * TPC:(i + 1) * TPC] for i in range(NCORES)]


_NC_CACHE = {}


def _get_nc(name, builder):
    if name not in _NC_CACHE:
        _NC_CACHE[name] = builder()
    return _NC_CACHE[name]


def kernel(**inputs):
    x = np.ascontiguousarray(np.asarray(inputs["L0_attn_x"], dtype=np.float32).reshape(B * S, D))
    wqkvT = np.ascontiguousarray(np.asarray(inputs["L0_attn_W_qkv"], dtype=np.float32).T)
    woT = np.ascontiguousarray(np.asarray(inputs["L0_attn_W_o"], dtype=np.float32).T)
    core_ids = list(range(NCORES))

    w1 = np.ascontiguousarray(np.asarray(inputs["L0_ffn_W1"], dtype=np.float32))
    b1 = np.asarray(inputs["L0_ffn_b1"], dtype=np.float32)
    b1r = np.ascontiguousarray(b1.reshape(8, 32, 128).transpose(0, 2, 1))
    w2 = np.ascontiguousarray(np.asarray(inputs["L0_ffn_W2"], dtype=np.float32))
    b2 = np.ascontiguousarray(np.asarray(inputs["L0_ffn_b2"], dtype=np.float32))
    wr = np.ascontiguousarray(np.asarray(inputs["L0_ffn_Wr"], dtype=np.float32))
    br = np.ascontiguousarray(np.asarray(inputs["L0_ffn_br"], dtype=np.float32).reshape(1, 8))
    winT = np.ascontiguousarray(np.asarray(inputs["L1_attn_W_in"], dtype=np.float32).T)
    convw = np.ascontiguousarray(np.asarray(inputs["L1_attn_conv_w"], dtype=np.float32))
    convb = np.ascontiguousarray(np.asarray(inputs["L1_attn_conv_b"], dtype=np.float32).reshape(-1, 1))
    dtb = np.ascontiguousarray(np.asarray(inputs["L1_attn_dt_bias"], dtype=np.float32).reshape(-1, 1))
    alog = np.ascontiguousarray(np.asarray(inputs["L1_attn_A_log"], dtype=np.float32).reshape(-1, 1))
    dpar = np.ascontiguousarray(np.asarray(inputs["L1_attn_D_param"], dtype=np.float32).reshape(-1, 1))
    nwr = np.ascontiguousarray(np.asarray(inputs["L1_attn_norm_w"], dtype=np.float32).reshape(16, 128).T)
    woutT = np.ascontiguousarray(np.asarray(inputs["L1_attn_W_out"], dtype=np.float32).T)
    gwT = np.ascontiguousarray(np.asarray(inputs["L1_ffn_gate_w"], dtype=np.float32).T)
    ncA = _get_nc("fused", build_fused)
    xs = _shard_rows(x)
    res = run_bass_kernel_spmd(
        ncA, [{"x": xs[i], "wqkvT": wqkvT, "woT": woT, "wr": wr, "br": br,
               "w1": w1, "b1r": b1r, "w2": w2, "b2": b2, "winT": winT,
               "convw": convw, "convb": convb, "dtb": dtb, "alog": alog,
               "dpar": dpar, "nwr": nwr, "woutT": woutT, "gwT": gwT}
              for i in range(NCORES)], core_ids=core_ids)
    r4a = res.results
    x3 = [r4a[i]["x3"] for i in range(NCORES)]

    allcnt = np.ascontiguousarray(np.concatenate([r["o_cnt"] for r in r4a], axis=0))
    nc4b = _get_nc("l1slots", build_l1_slots)
    in_maps = []
    for i in range(NCORES):
        in_maps.append({
            "allcnt": allcnt,
            "coremask": np.ascontiguousarray((np.arange(NCORES) < i)
                                             .astype(np.float32).reshape(-1, 1)),
            "tokidx": np.ascontiguousarray(np.arange(i * TPC, (i + 1) * TPC,
                                                     dtype=np.float32).reshape(-1, 1)),
            "oh": r4a[i]["o_oh"], "loc": r4a[i]["o_loc"],
            "pmax": r4a[i]["o_pmax"], "eidx": r4a[i]["o_eidx"],
        })
    res = run_bass_kernel_spmd(nc4b, in_maps, core_ids=core_ids)
    r4b = res.results

    idxparts = np.stack([r4b[i]["o_idx"].reshape(16, 2560) for i in range(NCORES)])
    xga = np.ascontiguousarray(np.concatenate([r["o_xln"] for r in r4a], axis=0))
    w1f = np.asarray(inputs["L1_ffn_w1"], dtype=np.float32)
    w2f = np.asarray(inputs["L1_ffn_w2"], dtype=np.float32)
    nc4c = _get_nc("l1exp", build_l1_experts)
    in_maps = []
    for i in range(NCORES):
        idx2T = np.ascontiguousarray(
            idxparts[:, 2 * i:2 * i + 2, :].reshape(NCORES, -1).T)
        in_maps.append({"idx2T": idx2T, "xga": xga,
                        "w1": np.ascontiguousarray(w1f[2 * i:2 * i + 2]),
                        "w2": np.ascontiguousarray(w2f[2 * i:2 * i + 2])})
    res = run_bass_kernel_spmd(nc4c, in_maps, core_ids=core_ids)
    ya = np.ascontiguousarray(np.concatenate([res.results[i]["o_y"]
                                              for i in range(NCORES)], axis=0))

    nc4d = _get_nc("l1comb", build_l1_combine)
    res = run_bass_kernel_spmd(
        nc4d, [{"x": x3[i], "ya": ya, "slot": r4b[i]["o_slot"], "w": r4b[i]["o_w"]}
               for i in range(NCORES)], core_ids=core_ids)
    out = np.concatenate([res.results[i]["out"] for i in range(NCORES)], axis=0)
    return out.reshape(B, S, D)
